# version 06
# baseline (speedup 1.0000x reference)
import numpy as np
import ml_dtypes
import concourse.bass as bass
import concourse.mybir as mybir
from concourse.bass_utils import run_bass_kernel_spmd
from contextlib import ExitStack

BF16 = ml_dtypes.bfloat16
F32 = mybir.dt.float32
BF = mybir.dt.bfloat16
AF = mybir.ActivationFunctionType
ALU = mybir.AluOpType
AX = mybir.AxisListType
MAGIC = float(np.float32(3 * 2 ** 22))

NCORE = 8
T, D, H, DK = 16384, 2048, 8, 256
TC = T // NCORE
HALO = 8
XW = TC + HALO
PBLK = [(0, 512), (512, 512), (1024, 512), (1536, 512), (2048, 8)]
NCH, CH = 16, 128
FF = 5632
NT = 16
NB = 4
GU_SH = 1408
WD_SH = 256


class Sem:
    def __init__(self, h):
        self.h = h
        self.n = 0


class KB:
    ENGS = ("tensor", "vector", "scalar", "gpsimd", "sync")

    def __init__(self, strict=False):
        self.q = {k: [] for k in self.ENGS}
        self.lastwait = {k: {} for k in self.ENGS}
        self.strict = strict
        self.own = {}

    def op(self, eng, meth, *a, w=(), inc=(), **kw):
        if self.strict and eng in self.own:
            w = list(w) + [self.own[eng]] + ([self.ownq[eng]] if eng in
                                             getattr(self, "ownq", {}) else [])
        wl = []
        for x in w:
            if x is None:
                continue
            s, v = x if isinstance(x, tuple) else (x, x.n)
            if v <= 0:
                continue
            lw = self.lastwait[eng]
            if lw.get(id(s), -1) >= v:
                continue
            lw[id(s)] = v
            wl.append((s.h, v))
        il = []
        for x in inc:
            s, k = x if isinstance(x, tuple) else (x, 1)
            s.n += k
            il.append((s.h, k))

        def fn(e, meth=meth, a=a, kw=kw, wl=wl, il=il):
            for h, v in wl:
                e.wait_ge(h, v)
            inst = getattr(e, meth)(*a, **kw)
            for h, k in il:
                inst.then_inc(h, k)
        self.q[eng].append(fn)

    def wait(self, eng, *sems):
        wl = []
        for x in sems:
            s, v = x if isinstance(x, tuple) else (x, x.n)
            if v <= 0:
                continue
            lw = self.lastwait[eng]
            if lw.get(id(s), -1) >= v:
                continue
            lw[id(s)] = v
            wl.append((s.h, v))

        def fn(e, wl=wl):
            for h, v in wl:
                e.wait_ge(h, v)
        self.q[eng].append(fn)


class Bank:
    def __init__(self, ap):
        self.ap = ap
        self.rd = []
        self.wr = None

    def take(self):
        r, self.rd = self.rd, []
        return r


class _Stop(Exception):
    pass


def build(strict=False, stop_after=None, dump=False):
    nc = bass.Bass("TRN2", target_bir_lowering=False,
                   detect_race_conditions=strict)

    def din(name, shape, dt=F32):
        return nc.dram_tensor(name, shape, dt, kind="ExternalInput").ap()

    xT = din("xT", [D, TC])
    xhalo = din("xhalo", [D, HALO])
    wq = din("wq", [H, D, DK])
    wk = din("wk", [H, D, DK])
    wv = din("wv", [H, D, DK])
    wb = din("wb", [D, H])
    convw = din("convw", [3, D, 4])
    wo = din("wo", [NT, D, 128])
    wguT_sh = din("wguT_sh", [D, GU_SH])
    wdT_sh = din("wdT_sh", [FF, WD_SH])
    nw2_i = din("nw2", [128, 2])
    nww_i = din("nww16", [128, NT])
    c_eye_i = din("c_eye_d", [128, 128], BF)
    c_eye2_i = din("c_eye2_d", [128, 128], BF)
    c_slm_i = din("c_slm_d", [128, 128], BF)
    c_um_i = din("c_um_d", [128, 128], BF)
    c_e256_i = din("c_e256_d", [128, 512], BF)
    c_sel_i = din("c_sel_d", [128, 16])
    c_misc_i = din("c_misc_d", [128, 32])
    out_d = nc.dram_tensor("out", [D, TC], F32, kind="ExternalOutput").ap()

    sp_q = nc.dram_tensor("sp_q", [H, 128, 2 * TC], BF).ap()
    sp_kfm = nc.dram_tensor("sp_kfm", [H, 128, 2 * TC], BF).ap()
    sp_ktm = nc.dram_tensor("sp_ktm", [H, 128, NCH * 256], BF).ap()
    sp_ti = nc.dram_tensor("sp_ti", [H, 128, NCH * 128], BF).ap()
    sp_uv = nc.dram_tensor("sp_uv", [H, 128, NCH * 256], BF).ap()
    sp_kq = nc.dram_tensor("sp_kq", [H, 128, NCH * 128], BF).ap()
    pg_in = nc.dram_tensor("pg_in", [H, 128, 1024], BF).ap()
    pg_out = nc.dram_tensor("pg_out", [NCORE, H, 128, 1024], BF, addr_space="Shared").ap()
    ws_in = nc.dram_tensor("ws_in", [1, 16], F32).ap()
    ws_out = nc.dram_tensor("ws_out", [1, 16], F32, addr_space="Shared").ap()
    tgu_in = nc.dram_tensor("tgu_in", [11, NT, 128, 128], BF).ap()
    tgu_out = nc.dram_tensor("tgu_out", [NCORE, 11, NT, 128, 128], BF, addr_space="Shared").ap()
    twd_in = nc.dram_tensor("twd_in", [2, 44, 128, 128], BF).ap()
    twd_out = nc.dram_tensor("twd_out", [NCORE, 2, 44, 128, 128], BF, addr_space="Shared").ap()
    y1_d = nc.dram_tensor("y1_d", [NB, 128, NT * 512], BF).ap()

    kb = KB(strict=strict)
    with ExitStack() as ctx:
        E = ctx.enter_context
        bigA = E(nc.sbuf_tensor("bigA", [128, NT * XW], BF))
        bigB = E(nc.sbuf_tensor("bigB", [128, 32768], BF))
        bigC = E(nc.sbuf_tensor("bigC", [128, 22528], BF))
        scrF = E(nc.sbuf_tensor("scrF", [128, 1472], F32))
        c_eye = E(nc.sbuf_tensor("c_eye", [128, 128], BF))
        c_eye2 = E(nc.sbuf_tensor("c_eye2", [128, 128], BF))
        c_slm = E(nc.sbuf_tensor("c_slm", [128, 128], BF))
        c_um = E(nc.sbuf_tensor("c_um", [128, 128], BF))
        c_e256 = E(nc.sbuf_tensor("c_e256", [128, 512], BF))
        c_sel = E(nc.sbuf_tensor("c_sel", [128, 16], F32))
        c_misc = E(nc.sbuf_tensor("c_misc", [128, 32], F32))
        c_conv = E(nc.sbuf_tensor("c_conv", [128, 192], F32))
        c_nw2 = E(nc.sbuf_tensor("c_nw2", [128, 2], F32))
        c_nww = E(nc.sbuf_tensor("c_nww", [128, 16], F32))
        ones_c = E(nc.sbuf_tensor("ones_c", [128, 1], BF))
        ones_r = E(nc.sbuf_tensor("ones_r", [1, 128], BF))
        ones_rf = E(nc.sbuf_tensor("ones_rf", [1, 128], F32))
        beta_t = E(nc.sbuf_tensor("beta_t", [128, 128], F32))
        rows = E(nc.sbuf_tensor("rows", [1, 3136], F32))
        rows_b = E(nc.sbuf_tensor("rows_b", [1, 5120], BF))
        mxc = E(nc.sbuf_tensor("mxc", [128, 16], BF))
        colsF = E(nc.sbuf_tensor("colsF", [128, 2], F32))
        cfm = E(nc.sbuf_tensor("cfm", [128, 16], F32))
        cfs = E(nc.sbuf_tensor("cfs", [128, 16], F32))
        S0 = E(nc.sbuf_tensor("S0", [128, 512], BF))
        psA = E(nc.psum_tensor("psA", [128, 512], F32))
        psB = E(nc.psum_tensor("psB", [128, 512], F32))
        psC = E(nc.psum_tensor("psC", [128, 512], F32))
        psT1 = E(nc.psum_tensor("psT1", [128, 1024], BF))
        psT2 = E(nc.psum_tensor("psT2", [128, 1024], BF))
        psN1 = E(nc.psum_tensor("psN1", [128, 512], F32))
        psN2 = E(nc.psum_tensor("psN2", [128, 512], F32))
        psS = E(nc.psum_tensor("psS", [128, 512], F32))
        sLD = Sem(E(nc.semaphore("sLD")))
        sLG = Sem(E(nc.semaphore("sLG")))
        sPE = Sem(E(nc.semaphore("sPE")))
        sDV = Sem(E(nc.semaphore("sDV")))
        sAC = Sem(E(nc.semaphore("sAC")))
        sGP = Sem(E(nc.semaphore("sGP")))
        sCC = Sem(E(nc.semaphore("sCC")))
        block = E(nc.Block())

        kb.own = {"tensor": sPE, "vector": sDV, "scalar": sAC, "gpsimd": sLG,
                  "sync": sLD}
        kb.ownq = {"gpsimd": sGP}
        bA, bB_, bC_ = Bank(psA), Bank(psB), Bank(psC)
        bT1, bT2 = Bank(psT1), Bank(psT2)
        bN1, bN2, bS = Bank(psN1), Bank(psN2), Bank(psS)

        def ckpt(name):
            if stop_after == name:
                raise _Stop()

        def mm(bank, out_ap, lhsT, rhs, start, stop, w=(), inc=()):
            ww = list(w) + (bank.take() if start else [])
            ii = list(inc) + ([(sPE, 1)] if stop else [])
            kb.op("tensor", "matmul", out_ap, lhsT, rhs, w=ww, inc=ii,
                  start=start, stop=stop)
            if stop:
                bank.wr = (sPE, sPE.n)

        def transpose(bank, out_ap, in_ap, ident, w=()):
            ww = list(w) + bank.take()
            kb.op("tensor", "transpose", out_ap, in_ap, ident, w=ww, inc=[(sPE, 1)])
            bank.wr = (sPE, sPE.n)

        def rd(bank, eng, sem, meth, *a, w=(), inc=(), **kw):
            ww = list(w) + ([bank.wr] if bank.wr else [])
            kb.op(eng, meth, *a, w=ww, inc=list(inc) + [(sem, 1)], **kw)
            bank.rd.append((sem, sem.n))

        def vop(meth, *a, w=(), inc=(), **kw):
            kb.op("vector", meth, *a, w=w, inc=list(inc) + [(sDV, 1)], **kw)

        def aop(meth, *a, w=(), inc=(), **kw):
            kb.op("scalar", meth, *a, w=w, inc=list(inc) + [(sAC, 1)], **kw)

        def dma(eng, sem, dst, src, w=()):
            kb.op(eng, "dma_start", w=w, inc=[(sem, 16)], out=dst, in_=src)

        try:

            dma("sync", sLD, c_eye[:, :], c_eye_i)
            dma("sync", sLD, c_eye2[:, :], c_eye2_i)
            dma("sync", sLD, c_slm[:, :], c_slm_i)
            dma("sync", sLD, c_um[:, :], c_um_i)
            dma("sync", sLD, c_e256[:, :], c_e256_i)
            dma("sync", sLD, c_sel[:, :], c_sel_i)
            dma("sync", sLD, c_misc[:, :], c_misc_i)
            dma("sync", sLD, c_nw2[:, :], nw2_i)
            dma("sync", sLD, c_nww[:, :], nww_i)
            for t3 in range(3):
                dma("sync", sLD,
                    c_conv[:, 64 * t3:64 * t3 + 64].rearrange("p (f j) -> p f j", f=16),
                    convw[t3].rearrange("(f p) j -> p f j", p=128))
            CONSTW = (sLD, sLD.n)

            xv = bigA[:, :].rearrange("p (d c) -> p d c", d=NT)
            dma("gpsimd", sLG, xv[:, :, 0:HALO], xhalo.rearrange("(d p) c -> p d c", p=128))
            for dq in range(4):
                dma("gpsimd", sLG, xv[:, 4 * dq:4 * dq + 4, HALO:XW],
                    xT.rearrange("(d p) t -> p d t", p=128)[:, 4 * dq:4 * dq + 4, :])
            dma("gpsimd", sLG, bigB[:, 32640:32768].rearrange("p (d o) -> p d o", d=NT),
                wb.rearrange("(d p) o -> p d o", p=128))
            XLOAD = (sLG, sLG.n)
            kb.op("gpsimd", "memset", ones_c[:, :], 1.0, inc=[(sGP, 1)])
            kb.op("gpsimd", "memset", ones_r[:, :], 1.0, inc=[(sGP, 1)])
            kb.op("gpsimd", "memset", ones_rf[:, :], 1.0, inc=[(sGP, 1)])
            kb.op("gpsimd", "memset", rows[0:1, 0:16], 0.0, inc=[(sGP, 1)])
            ONES = (sGP, sGP.n)

            for tb in range(NB):
                for d in range(NT):
                    mm(bS, psS[0:8, 0:512], bigB[:, 32640 + 8 * d:32640 + 8 * d + 8],
                       bigA[:, XW * d + HALO + 512 * tb: XW * d + HALO + 512 * tb + 512],
                       start=(d == 0), stop=(d == NT - 1),
                       w=([XLOAD, CONSTW] if d == 0 else ()))
                rd(bS, "scalar", sAC, "activation", bigB[0:8, 512 * tb:512 * tb + 512],
                   psS[0:8, 0:512], AF.Sigmoid)
            BETAF = (sAC, sAC.n)
            for i in range(NCH):
                bnk = bT1 if i % 2 == 0 else bT2
                transpose(bnk, bnk.ap[0:128, 0:8], bigB[0:8, 128 * i:128 * i + 128],
                          c_eye[0:8, 0:8], w=[BETAF])
                rd(bnk, "vector", sDV, "tensor_copy", beta_t[:, 8 * i:8 * i + 8],
                   bnk.ap[0:128, 0:8])
            BETAT = (sDV, sDV.n)
            kb.op("vector", "memset", rows[:, :], 0.0, inc=[(sDV, 1)], w=[ONES])
            kb.op("vector", "memset", rows_b[:, :], 0.0, inc=[(sDV, 1)])
            ckpt('beta')

            accq = bigC
            for k in range(NT):
                dma("sync", sLD, scrF[:, 0:1408],
                    wguT_sh.rearrange("(a p) o -> a p o", p=128)[k], w=[(sDV, sDV.n)])
                vop("reduce_sum", scrF[:, 1408 + k:1409 + k], scrF[:, 0:1408], AX.X,
                    apply_absolute_value=True, w=[(sLD, sLD.n)])
            for k in range(44):
                dma("sync", sLD, scrF[:, 0:256],
                    wdT_sh.rearrange("(a p) o -> a p o", p=128)[k],
                    w=[(sDV, sDV.n)])
                vop("reduce_sum", scrF[:, 1424 + k:1425 + k], scrF[:, 0:256], AX.X,
                    apply_absolute_value=True, w=[(sLD, sLD.n)])
            vop("reduce_sum", colsF[:, 0:1], scrF[:, 1408:1424], AX.X)
            vop("reduce_sum", colsF[:, 1:2], scrF[:, 1424:1468], AX.X)
            kb.op("vector", "memset", scrF[:, 1468:1469], 1.0, inc=[(sDV, 1)])
            mm(bS, psS[0:1, 0:2], scrF[:, 1468:1469], colsF[:, 0:2], start=True, stop=True,
               w=[(sDV, sDV.n)])
            rd(bS, "vector", sDV, "tensor_copy", rows[0:1, 0:2], psS[0:1, 0:2],
               w=[ONES])
            kb.op("sync", "dma_start", out=ws_in, in_=rows[0:1, 0:16],
                  w=[(sDV, sDV.n), (sGP, sGP.n)], inc=[(sLD, 16)])
            kb.op("gpsimd", "collective_compute", "AllReduce", ALU.add,
                  [list(range(NCORE))], [ws_in], [ws_out],
                  w=[(sLD, sLD.n)], inc=[(sCC, 1)])
            dma("sync", sLD, rows[0:1, 16:32], ws_out, w=[(sCC, sCC.n)])
            NGU, NWD = float(D * 11264), float(FF * D)
            vop("tensor_scalar", rows[0:1, 1536:2048], rows[0:1, 0:512],
                1.0 / NGU, 1e-5, op0=ALU.mult, op1=ALU.max,
                w=[(sLD, sLD.n), (sDV, sDV.n)])
            vop("tensor_scalar", rows[0:1, 2048:2560], rows[0:1, 0:512],
                1.0 / NWD, 1e-5, op0=ALU.mult, op1=ALU.max)
            vop("reciprocal", rows[0:1, 2560:3072], rows[0:1, 1536:2048])
            MRDYt = (sDV, sDV.n)
            mm(bS, psS[0:128, 0:2], ones_rf[:, :], rows[0:1, 1552:1554],
               start=True, stop=True, w=[MRDYt, ONES])
            mm(bS, psS[0:128, 2:4], ones_rf[:, :], rows[0:1, 2064:2066],
               start=True, stop=True)
            mm(bS, psS[0:128, 4:6], ones_rf[:, :], rows[0:1, 2576:2578],
               start=True, stop=True)
            PE3 = (sPE, sPE.n)
            vop("reciprocal", rows[0:1, 2560:3072], rows[0:1, 2048:2560], w=[PE3])
            mm(bS, psS[0:128, 6:8], ones_rf[:, :], rows[0:1, 2576:2578],
               start=True, stop=True, w=[(sDV, sDV.n)])
            rd(bS, "vector", sDV, "tensor_copy", cfm[:, 0:2], psS[0:128, 0:2])
            rd(bS, "vector", sDV, "tensor_copy", cfm[:, 2:4], psS[0:128, 2:4])
            rd(bS, "vector", sDV, "tensor_copy", cfs[:, 0:2], psS[0:128, 4:6])
            rd(bS, "vector", sDV, "tensor_copy", cfs[:, 2:4], psS[0:128, 6:8])
            TSCALE = (sDV, sDV.n)
            ckpt('tern1')

            def CM(i):
                return c_misc[0:1, i:i + 1]

            def stats_sum(bank, tiles):
                for i, (ap_, w_) in enumerate(tiles):
                    mm(bank, bank.ap[0:1, 0:512], ones_c[:, :], ap_,
                       start=(i == 0), stop=(i == len(tiles) - 1), w=list(w_))

            def bcast(bank, row_bf, w=()):
                mm(bank, bank.ap[0:128, 0:512], ones_r[:, :], row_bf,
                   start=True, stop=True, w=w)

            O_QB, O_KB, O_KT, O_VT = 0, 4096, 8192, 12288
            O_SQ, O_QP = 16384, 20480
            O_TI, O_UV, O_KQ = 22656, 24704, 28800
            O_M, O_X, O_XT, O_R, O_RT, O_DG = 30848, 30976, 31104, 31232, 31360, 31488
            C_TERN, C_PG = 0, 2816

            BETAPE = (sPE, sPE.n)
            bufs_free = [None]
            wslot_user = [None, None]

            def proj_conv(widx, conv_off, dst_off, half, extraw=()):
                slot = widx % 2
                for j in range(4):
                    vop("tensor_scalar", bigB[:, O_DG + 128 * j:O_DG + 128 * j + 128],
                        c_eye[:, :], c_conv[:, conv_off + j:conv_off + j + 1], None,
                        op0=ALU.mult, w=[CONSTW, (sPE, sPE.n)] + list(extraw))
                DIAG = (sDV, sDV.n)
                for bi, (c0, cw) in enumerate(PBLK):
                    bank = bA if bi % 2 == 0 else bB_
                    for d in range(NT):
                        mm(bank, bank.ap[0:128, 0:cw],
                           bigC[:, 11008 + 2048 * slot + 128 * d:
                                11008 + 2048 * slot + 128 * d + 128],
                           bigA[:, XW * d + c0:XW * d + c0 + cw],
                           start=(d == 0), stop=(d == NT - 1),
                           w=([(sLG, sLG.n), XLOAD] if (bi == 0 and d == 0) else ()))
                    rd(bank, "scalar", sAC, "copy", bigB[:, O_QP + c0:O_QP + c0 + cw],
                       bank.ap[0:128, 0:cw], w=list(extraw))
                QPRE = (sAC, sAC.n)
                for tb in range(NB):
                    for j in range(4):
                        s0 = HALO - 3 + 512 * tb + j
                        mm(bC_, psC[0:128, 0:512],
                           bigB[:, O_DG + 128 * j:O_DG + 128 * j + 128],
                           bigB[:, O_QP + s0:O_QP + s0 + 512],
                           start=(j == 0), stop=(j == 3),
                           w=([QPRE, DIAG] if j == 0 else ()))
                    rd(bC_, "scalar", sAC, "activation",
                       bigB[:, dst_off + 2048 * half + 512 * tb:
                            dst_off + 2048 * half + 512 * tb + 512],
                       psC[0:128, 0:512], AF.Silu, w=list(extraw))
                return (sAC, sAC.n)

            def l2norm(dst_off, sc, bi_):
                vop("tensor_tensor", bigB[:, O_SQ:O_SQ + 4096],
                    bigB[:, dst_off:dst_off + 4096],
                    bigB[:, dst_off:dst_off + 4096], ALU.mult, w=[(sAC, sAC.n)])
                SQ = (sDV, sDV.n)
                for tb in range(NB):
                    stats_sum(bS, [
                        (bigB[:, O_SQ + 512 * tb:O_SQ + 512 * tb + 512], [SQ, ONES]),
                        (bigB[:, O_SQ + 2048 + 512 * tb:O_SQ + 2048 + 512 * tb + 512], []),
                    ])
                    rd(bS, "scalar", sAC, "activation", rows[0:1, 64:576],
                       psS[0:1, 0:512], AF.Sqrt, scale=sc, bias=bi_)
                    vop("reciprocal", rows[0:1, 64:576], rows[0:1, 64:576], w=[(sAC, sAC.n)])
                    aop("copy", rows_b[0:1, 4096:4608], rows[0:1, 64:576], w=[(sDV, sDV.n)])
                    bcast(bN2, rows_b[0:1, 4096:4608], w=[(sAC, sAC.n)])
                    for hf in range(2):
                        sl = bigB[:, dst_off + 2048 * hf + 512 * tb:
                                  dst_off + 2048 * hf + 512 * tb + 512]
                        rd(bN2, "vector", sDV, "tensor_tensor", sl, sl,
                           psN2[0:128, 0:512], ALU.mult)

            TERN_DONE = None
            CC_W = None
            for h in range(H):
                extraw = ([BETAPE] if h == 0 else []) + [bufs_free[0]]
                wsrc = [(wq, 0), (wq, 1), (wk, 0), (wk, 1), (wv, 0), (wv, 1)]

                def wload(widx):
                    wt, hf = wsrc[widx]
                    slot = widx % 2
                    dma("gpsimd", sLG, bigC[:, 11008 + 2048 * slot:11008 + 2048 * slot + 2048]
                        .rearrange("p (d o) -> p d o", d=NT),
                        wt[h][:, 128 * hf:128 * hf + 128]
                        .rearrange("(d p) o -> p d o", p=128),
                        w=[wslot_user[slot], (bufs_free[0] if h > 0 else None)])

                wload(0)
                wload(1)
                dsts = [O_QB, O_QB, O_KB, O_KB, O_KT, O_KT]
                coffs = [8 * h, 8 * h + 4, 64 + 8 * h, 64 + 8 * h + 4,
                         128 + 8 * h, 128 + 8 * h + 4]
                for widx in range(6):
                    proj_conv(widx, coffs[widx], dsts[widx], widx % 2, extraw=extraw)
                    wslot_user[widx % 2] = (sPE, sPE.n)
                    if widx + 2 < 6:
                        wload(widx + 2)
                VDONE = (sAC, sAC.n)
                ckpt('proj0')
                l2norm(O_QB, CM(3), CM(0))
                l2norm(O_KB, CM(6), CM(1))
                KNORM = (sDV, sDV.n)
                ckpt('l2')

                def do_transposes(src_off, dst_off, wfirst):
                    for g in range(8):
                        bnk = bT1 if g % 2 == 0 else bT2
                        for s in range(4):
                            idx = 4 * g + s
                            ci, m = idx // 2, idx % 2
                            transpose(bnk, bnk.ap[0:128, 128 * s:128 * s + 128],
                                      bigB[:, src_off + 2048 * m + 128 * ci:
                                           src_off + 2048 * m + 128 * ci + 128],
                                      c_eye[:, :],
                                      w=([wfirst] if (g == 0 and s == 0) else ()))
                        for s in range(4):
                            idx = 4 * g + s
                            ci, m = idx // 2, idx % 2
                            rd(bnk, "vector", sDV, "tensor_copy",
                               bigB[:, dst_off + 256 * ci + 128 * m:
                                    dst_off + 256 * ci + 128 * m + 128],
                               bnk.ap[0:128, 128 * s:128 * s + 128])
                do_transposes(O_KT, O_VT, VDONE)
                do_transposes(O_KB, O_KT, KNORM)
                TRD = (sDV, sDV.n)
                ckpt('tr')

                pgb = C_PG + 1024 * h
                for pt in range(2):
                    vop("tensor_copy", bigC[:, pgb + 512 * pt:pgb + 512 * pt + 256],
                        c_e256[:, 256 * pt:256 * pt + 256],
                        w=[CONSTW] + ([TERN_DONE] if h == 1 else []))
                    kb.op("vector", "memset",
                          bigC[:, pgb + 512 * pt + 256:pgb + 512 * pt + 512], 0.0,
                          inc=[(sDV, 1)])
                for ci in range(NCH):
                    kc = 128 * ci
                    bcol = beta_t[:, 8 * ci + h:8 * ci + h + 1]
                    mm(bN1, psN1[0:128, 0:128], bigB[:, O_KB + kc:O_KB + kc + 128],
                       bigB[:, O_KB + kc:O_KB + kc + 128], start=True, stop=False,
                       w=[KNORM, BETAT])
                    mm(bN1, psN1[0:128, 0:128],
                       bigB[:, O_KB + 2048 + kc:O_KB + 2048 + kc + 128],
                       bigB[:, O_KB + 2048 + kc:O_KB + 2048 + kc + 128],
                       start=False, stop=True)
                    rd(bN1, "vector", sDV, "scalar_tensor_tensor",
                       bigB[:, O_M:O_M + 128], psN1[0:128, 0:128], bcol, c_slm[:, :],
                       op0=ALU.mult, op1=ALU.mult)
                    vop("tensor_tensor", bigB[:, O_M:O_M + 128], bigB[:, O_M:O_M + 128],
                        c_eye[:, :], ALU.add)
                    MRDY = (sDV, sDV.n)
                    if ci == 0:
                        ckpt('m0')
                    transpose(bT1, psT1[0:128, 0:128], bigB[:, O_M:O_M + 128],
                              c_eye[:, :], w=[MRDY])
                    rd(bT1, "vector", sDV, "tensor_tensor", bigB[:, O_X:O_X + 128],
                       c_eye2[:, :], psT1[0:128, 0:128], ALU.subtract)
                    vop("tensor_tensor", bigB[:, O_XT:O_XT + 128], c_eye2[:, :],
                        bigB[:, O_M:O_M + 128], ALU.subtract)
                    for it in range(6):
                        last = (it == 5)
                        XRDY = (sDV, sDV.n)
                        mm(bN1, psN1[0:128, 0:128], bigB[:, O_M:O_M + 128],
                           bigB[:, O_X:O_X + 128], start=True, stop=True, w=[XRDY])
                        if not last:
                            mm(bN2, psN2[0:128, 0:128], bigB[:, O_X:O_X + 128],
                               bigB[:, O_M:O_M + 128], start=True, stop=True, w=[XRDY])
                        rd(bN1, "vector", sDV, "tensor_tensor", bigB[:, O_R:O_R + 128],
                           c_eye2[:, :], psN1[0:128, 0:128], ALU.subtract)
                        if not last:
                            rd(bN2, "vector", sDV, "tensor_tensor",
                               bigB[:, O_RT:O_RT + 128], c_eye2[:, :],
                               psN2[0:128, 0:128], ALU.subtract)
                        RRDY = (sDV, sDV.n)
                        mm(bN1, psN1[0:128, 128:256], bigB[:, O_XT:O_XT + 128],
                           bigB[:, O_R:O_R + 128], start=True, stop=True, w=[RRDY])
                        if not last:
                            mm(bN2, psN2[0:128, 128:256], bigB[:, O_X:O_X + 128],
                               bigB[:, O_RT:O_RT + 128], start=True, stop=True, w=[RRDY])
                            rd(bN1, "vector", sDV, "tensor_copy", bigB[:, O_X:O_X + 128],
                               psN1[0:128, 128:256], w=[(sPE, sPE.n)])
                            rd(bN2, "vector", sDV, "tensor_copy", bigB[:, O_XT:O_XT + 128],
                               psN2[0:128, 128:256])
                        else:
                            rd(bN1, "vector", sDV, "tensor_scalar",
                               bigB[:, O_TI + kc:O_TI + kc + 128], psN1[0:128, 128:256],
                               bcol, None, op0=ALU.mult)
                    TIR = (sDV, sDV.n)
                    if ci == 0:
                        ckpt('newt0')
                    mm(bA, psA[0:128, 0:256], bigB[:, O_TI + kc:O_TI + kc + 128],
                       bigB[:, O_VT + 256 * ci:O_VT + 256 * ci + 256],
                       start=True, stop=True, w=[TIR, TRD])
                    rd(bA, "scalar", sAC, "copy",
                       bigB[:, O_UV + 256 * ci:O_UV + 256 * ci + 256], psA[0:128, 0:256])
                    mm(bN2, psN2[0:128, 0:128], bigB[:, O_KB + kc:O_KB + kc + 128],
                       bigB[:, O_QB + kc:O_QB + kc + 128], start=True, stop=False)
                    mm(bN2, psN2[0:128, 0:128],
                       bigB[:, O_KB + 2048 + kc:O_KB + 2048 + kc + 128],
                       bigB[:, O_QB + 2048 + kc:O_QB + 2048 + kc + 128],
                       start=False, stop=True)
                    rd(bN2, "vector", sDV, "tensor_tensor",
                       bigB[:, O_KQ + kc:O_KQ + kc + 128], psN2[0:128, 0:128],
                       c_um[:, :], ALU.mult)
                    PGR = (sDV, sDV.n)
                    if ci == 0:
                        ckpt('kqt0')
                    for m in range(2):
                        mm(bB_, psB[0:128, 0:512],
                           bigB[:, O_KB + 2048 * m + kc:O_KB + 2048 * m + kc + 128],
                           bigC[:, pgb + 512 * m:pgb + 512 * m + 512],
                           start=(m == 0), stop=(m == 1), w=([PGR] if m == 0 else ()))
                    rd(bB_, "vector", sDV, "tensor_copy", bigB[:, O_SQ:O_SQ + 256],
                       psB[0:128, 0:256])
                    rd(bB_, "vector", sDV, "tensor_tensor", bigB[:, O_SQ + 256:O_SQ + 512],
                       psB[0:128, 256:512],
                       bigB[:, O_VT + 256 * ci:O_VT + 256 * ci + 256], ALU.subtract)
                    RSB = [(sDV, sDV.n), (sAC, sAC.n)]
                    mm(bB_, psB[0:128, 0:512], bigB[:, O_TI + kc:O_TI + kc + 128],
                       bigB[:, O_SQ:O_SQ + 512], start=True, stop=True, w=RSB)
                    rd(bB_, "scalar", sAC, "copy", bigB[:, O_SQ:O_SQ + 512],
                       psB[0:128, 0:512])
                    YSB = (sAC, sAC.n)
                    for pt in range(2):
                        mm(bC_, psC[0:128, 0:512],
                           bigB[:, O_KT + 256 * ci + 128 * pt:
                                O_KT + 256 * ci + 128 * pt + 128],
                           bigB[:, O_SQ:O_SQ + 512], start=True, stop=True, w=[YSB])
                        rd(bC_, "vector", sDV, "tensor_tensor",
                           bigC[:, pgb + 512 * pt:pgb + 512 * pt + 512],
                           bigC[:, pgb + 512 * pt:pgb + 512 * pt + 512],
                           psC[0:128, 0:512], ALU.subtract)
                    if ci == 0:
                        ckpt('pg0')

                ckpt('chunks')
                SPW = [(sDV, sDV.n), (sAC, sAC.n)]
                dma("sync", sLD, sp_q[h], bigB[:, O_QB:O_QB + 4096], w=SPW)
                dma("sync", sLD, sp_kfm[h], bigB[:, O_KB:O_KB + 4096], w=SPW)
                dma("sync", sLD, sp_ktm[h], bigB[:, O_KT:O_KT + 4096], w=SPW)
                dma("sync", sLD, sp_ti[h], bigB[:, O_TI:O_TI + 2048], w=SPW)
                dma("sync", sLD, sp_uv[h], bigB[:, O_UV:O_UV + 4096], w=SPW)
                dma("sync", sLD, sp_kq[h], bigB[:, O_KQ:O_KQ + 2048], w=SPW)
                bufs_free[0] = (sLD, sLD.n)
                ckpt('h%d' % h)

                if h == 0:
                    for k in range(NT):
                        dma("sync", sLD, scrF[:, 0:1408],
                            wguT_sh.rearrange("(a p) o -> a p o", p=128)[k],
                            w=[(sDV, sDV.n)])
                        vop("tensor_scalar", scrF[:, 0:1408], scrF[:, 0:1408],
                            cfs[:, 0:1], 1.25, op0=ALU.mult, op1=ALU.min,
                            w=[(sLD, sLD.n), TSCALE])
                        vop("tensor_scalar", scrF[:, 0:1408], scrF[:, 0:1408],
                            -1.25, MAGIC, op0=ALU.max, op1=ALU.add)
                        vop("tensor_scalar", bigC[:, C_TERN:C_TERN + 1408],
                            scrF[:, 0:1408], -MAGIC, None, op0=ALU.add)
                        kb.op("sync", "dma_start",
                              out=tgu_in[:, k].rearrange("l p c -> p l c"),
                              in_=bigC[:, C_TERN:C_TERN + 1408]
                              .rearrange("p (l c) -> p l c", l=11),
                              w=[(sDV, sDV.n)], inc=[(sLD, 16)])
                    for li in range(2):
                        for k in range(11):
                            dma("sync", sLD,
                                scrF[:, 0:512].rearrange("p (a c) -> p a c", a=4),
                                wdT_sh.rearrange("(a p) o -> p a o", p=128)
                                [:, 4 * k:4 * k + 4, 128 * li:128 * li + 128],
                                w=[(sDV, sDV.n)])
                            vop("tensor_scalar", scrF[:, 0:512], scrF[:, 0:512],
                                cfs[:, 3:4], 1.25, op0=ALU.mult, op1=ALU.min,
                                w=[(sLD, sLD.n), TSCALE])
                            vop("tensor_scalar", scrF[:, 0:512], scrF[:, 0:512],
                                -1.25, MAGIC, op0=ALU.max, op1=ALU.add)
                            vop("tensor_scalar", bigC[:, C_TERN:C_TERN + 512],
                                scrF[:, 0:512], -MAGIC, None, op0=ALU.add)
                            kb.op("sync", "dma_start",
                                  out=twd_in[li, 4 * k:4 * k + 4]
                                  .rearrange("a p c -> p a c"),
                                  in_=bigC[:, C_TERN:C_TERN + 512]
                                  .rearrange("p (a c) -> p a c", a=4),
                                  w=[(sDV, sDV.n)], inc=[(sLD, 16)])
                    if dump:
                        dbg_cf = nc.dram_tensor("dbg_cf", [128, 8], F32,
                                                kind="ExternalOutput").ap()
                        dbg_scr = nc.dram_tensor("dbg_scr", [128, 512], F32,
                                                 kind="ExternalOutput").ap()
                        dbg_bc = nc.dram_tensor("dbg_bc", [128, 512], BF,
                                                kind="ExternalOutput").ap()
                        dma("sync", sLD, dbg_cf[:, 0:4], cfs[:, 0:4],
                            w=[(sDV, sDV.n)])
                        dma("sync", sLD, dbg_cf[:, 4:8], cfm[:, 0:4],
                            w=[(sDV, sDV.n)])
                        dma("sync", sLD, dbg_scr, scrF[:, 0:512], w=[(sDV, sDV.n)])
                        dma("sync", sLD, dbg_bc, bigC[:, C_TERN:C_TERN + 512],
                            w=[(sDV, sDV.n)])
                    TERN_DONE = (sLD, sLD.n)
                if h == 1:
                    kb.op("gpsimd", "collective_compute", "AllGather", ALU.bypass,
                          [list(range(NCORE))], [tgu_in], [tgu_out],
                          w=[TERN_DONE], inc=[(sCC, 1)])
                    kb.op("gpsimd", "collective_compute", "AllGather", ALU.bypass,
                          [list(range(NCORE))], [twd_in], [twd_out],
                          w=[TERN_DONE], inc=[(sCC, 1)])
                    CC_W = (sCC, sCC.n)
                ckpt2 = 'h%d' % h
                if stop_after == ckpt2:
                    raise _Stop()

            for h in range(H):
                pgb = C_PG + 1024 * h
                sl2 = 1024 * (h % 2)
                W0 = [bufs_free[0], (sDV, sDV.n), (sLD, sLD.n)]
                for a in range(2):
                    bnk = bT1 if a == 0 else bT2
                    for b in range(2):
                        transpose(bnk, bnk.ap[0:128, 128 * b:128 * b + 128],
                                  bigC[:, pgb + 512 * b + 128 * a:
                                       pgb + 512 * b + 128 * a + 128],
                                  c_eye[:, :], w=W0)
                    for b in range(2):
                        rd(bnk, "vector", sDV, "tensor_copy",
                           bigB[:, sl2 + 512 * a + 128 * b:sl2 + 512 * a + 128 * b + 128],
                           bnk.ap[0:128, 128 * b:128 * b + 128])
                    vop("tensor_copy", bigB[:, sl2 + 512 * a + 256:sl2 + 512 * a + 512],
                        bigC[:, pgb + 512 * a + 256:pgb + 512 * a + 512])
                dma("sync", sLD, pg_in[h], bigB[:, sl2:sl2 + 1024], w=[(sDV, sDV.n)])
            kb.op("gpsimd", "collective_compute", "AllGather", ALU.bypass,
                  [list(range(NCORE))], [pg_in], [pg_out],
                  w=[(sLD, sLD.n)], inc=[(sCC, 1)])
            PG_AG = (sCC, sCC.n)
            ckpt('p2')

            D_KF, D_KT, D_QF, D_TI, D_UV, D_KQ = 0, 4096, 8192, 12288, 14336, 18432
            D_RS, D_US, D_OSQ, D_AG = 20480, 20736, 20992, 25088
            D_MSK = 21504
            for h in range(H):
                RELW = [(sPE, sPE.n), (sDV, sDV.n), (sAC, sAC.n), (sLD, sLD.n)]
                dma("sync", sLD, bigB[:, D_KF:D_KF + 4096], sp_kfm[h], w=RELW)
                dma("sync", sLD, bigB[:, D_KT:D_KT + 4096], sp_ktm[h], w=RELW)
                dma("sync", sLD, bigB[:, D_QF:D_QF + 4096], sp_q[h], w=RELW)
                dma("sync", sLD, bigB[:, D_TI:D_TI + 2048], sp_ti[h], w=RELW)
                dma("sync", sLD, bigB[:, D_UV:D_UV + 4096], sp_uv[h], w=RELW)
                dma("sync", sLD, bigB[:, D_KQ:D_KQ + 2048], sp_kq[h], w=RELW)
                for j in range(7):
                    dma("sync", sLD, bigB[:, D_AG + 1024 * j:D_AG + 1024 * j + 1024],
                        pg_out[j, h], w=RELW + [PG_AG])
                RELD = (sLD, sLD.n)
                kb.op("vector", "memset", S0[:, :], 0.0, inc=[(sDV, 1)],
                      w=[(sPE, sPE.n)])
                for j in range(7):
                    base = D_AG + 1024 * j
                    for pt in range(2):
                        vop("tensor_scalar", bigB[:, D_MSK + 512 * pt:D_MSK + 512 * pt + 256],
                            bigB[:, base + 512 * pt:base + 512 * pt + 256],
                            c_sel[:, j:j + 1], None, op0=ALU.mult,
                            w=[RELD] if (j == 0 and pt == 0) else ())
                        vop("scalar_tensor_tensor",
                            bigB[:, D_MSK + 512 * pt:D_MSK + 512 * pt + 256],
                            c_e256[:, 256 * pt:256 * pt + 256], c_sel[:, 8 + j:9 + j],
                            bigB[:, D_MSK + 512 * pt:D_MSK + 512 * pt + 256],
                            op0=ALU.mult, op1=ALU.add)
                        vop("tensor_scalar",
                            bigB[:, D_MSK + 512 * pt + 256:D_MSK + 512 * pt + 512],
                            bigB[:, base + 512 * pt + 256:base + 512 * pt + 512],
                            c_sel[:, j:j + 1], None, op0=ALU.mult)
                    MRD = (sDV, sDV.n)
                    for a in range(2):
                        for k2 in range(2):
                            mm(bA, psA[0:128, 256 * a:256 * a + 256],
                               bigB[:, D_MSK + 512 * k2 + 128 * a:
                                    D_MSK + 512 * k2 + 128 * a + 128],
                               S0[:, 256 * k2:256 * k2 + 256],
                               start=(k2 == 0), stop=(k2 == 1),
                               w=([MRD] if k2 == 0 else ()))
                    for a in range(2):
                        rd(bA, "vector", sDV, "tensor_tensor",
                           S0[:, 256 * a:256 * a + 256], psA[0:128, 256 * a:256 * a + 256],
                           bigB[:, D_MSK + 512 * a + 256:D_MSK + 512 * a + 512], ALU.add)
                for ci in range(NCH):
                    kc = 128 * ci
                    for m in range(2):
                        mm(bC_, psC[0:128, 0:256],
                           bigB[:, D_KF + 2048 * m + kc:D_KF + 2048 * m + kc + 128],
                           S0[:, 256 * m:256 * m + 256], start=(m == 0), stop=(m == 1),
                           w=([RELD, (sDV, sDV.n)] if m == 0 else ()))
                    rd(bC_, "scalar", sAC, "copy", bigB[:, D_RS:D_RS + 256],
                       psC[0:128, 0:256])
                    mm(bC_, psC[0:128, 256:512], bigB[:, D_TI + kc:D_TI + kc + 128],
                       bigB[:, D_RS:D_RS + 256], start=True, stop=True,
                       w=[(sAC, sAC.n)])
                    rd(bC_, "vector", sDV, "tensor_tensor", bigB[:, D_US:D_US + 256],
                       bigB[:, D_UV + 256 * ci:D_UV + 256 * ci + 256],
                       psC[0:128, 256:512], ALU.subtract)
                    URD = (sDV, sDV.n)
                    for a in range(2):
                        for k2 in range(2):
                            mm(bA, psA[0:128, 128 * a:128 * a + 128],
                               S0[:, 256 * k2 + 128 * a:256 * k2 + 128 * a + 128],
                               bigB[:, D_QF + 2048 * k2 + kc:D_QF + 2048 * k2 + kc + 128],
                               start=(k2 == 0), stop=False)
                        mm(bA, psA[0:128, 128 * a:128 * a + 128],
                           bigB[:, D_US + 128 * a:D_US + 128 * a + 128],
                           bigB[:, D_KQ + kc:D_KQ + kc + 128], start=False, stop=True,
                           w=[URD])
                    for m in range(2):
                        mm(bB_, psB[0:128, 256 * m:256 * m + 256],
                           bigB[:, D_KT + 256 * ci + 128 * m:
                                D_KT + 256 * ci + 128 * m + 128],
                           bigB[:, D_US:D_US + 256], start=(m == 0), stop=(m == 1),
                           w=([URD] if m == 0 else ()))
                    rd(bB_, "vector", sDV, "tensor_tensor", S0[:, :], S0[:, :],
                       psB[0:128, 0:512], ALU.add)
                    for a in range(2):
                        rd(bA, "scalar", sAC, "copy",
                           bigA[:, XW * (2 * h + a) + kc:XW * (2 * h + a) + kc + 128],
                           psA[0:128, 128 * a:128 * a + 128])
                ODONE = (sAC, sAC.n)
                vop("tensor_tensor", bigB[:, D_OSQ:D_OSQ + 2048],
                    bigA[:, XW * 2 * h:XW * 2 * h + 2048],
                    bigA[:, XW * 2 * h:XW * 2 * h + 2048], ALU.mult, w=[ODONE])
                vop("tensor_tensor", bigB[:, D_OSQ + 2048:D_OSQ + 4096],
                    bigA[:, XW * (2 * h + 1):XW * (2 * h + 1) + 2048],
                    bigA[:, XW * (2 * h + 1):XW * (2 * h + 1) + 2048], ALU.mult)
                OSQ = (sDV, sDV.n)
                for tb in range(NB):
                    stats_sum(bS, [
                        (bigB[:, D_OSQ + 512 * tb:D_OSQ + 512 * tb + 512], [OSQ]),
                        (bigB[:, D_OSQ + 2048 + 512 * tb:D_OSQ + 2048 + 512 * tb + 512], []),
                    ])
                    rd(bS, "scalar", sAC, "activation", rows[0:1, 64:576],
                       psS[0:1, 0:512], AF.Sqrt, scale=CM(4), bias=CM(2))
                    vop("reciprocal", rows[0:1, 64:576], rows[0:1, 64:576],
                        w=[(sAC, sAC.n)])
                    aop("copy", rows_b[0:1, 4096:4608], rows[0:1, 64:576],
                        w=[(sDV, sDV.n)])
                    bcast(bN2, rows_b[0:1, 4096:4608], w=[(sAC, sAC.n)])
                    for a in range(2):
                        sl = bigA[:, XW * (2 * h + a) + 512 * tb:
                                  XW * (2 * h + a) + 512 * tb + 512]
                        rd(bN2, "vector", sDV, "tensor_tensor", sl, sl,
                           psN2[0:128, 0:512], ALU.mult)
                        vop("tensor_scalar", sl, sl, c_nw2[:, a:a + 1], None,
                            op0=ALU.mult)
            OHAT = (sDV, sDV.n)
            ckpt('p4')

            wo_user = [None, None]
            x_user = [None, None]
            for ot in range(NT):
                slot = ot % 2
                dma("gpsimd", sLG, bigC[:, 2048 * slot:2048 * slot + 2048]
                    .rearrange("p (d o) -> p d o", d=NT),
                    wo[ot].rearrange("(d p) o -> p d o", p=128),
                    w=[wo_user[slot]] +
                    ([(sDV, sDV.n), (sPE, sPE.n), (sLD, sLD.n)] if ot < 2 else []))
                WOL = (sLG, sLG.n)
                for tb in range(NB):
                    xslot = (ot * NB + tb) % 2
                    dma("sync", sLD, scrF[:, 512 * xslot:512 * xslot + 512],
                        xT.rearrange("(d p) t -> p d t", p=128)
                        [:, ot, 512 * tb:512 * tb + 512],
                        w=[x_user[xslot]] +
                        ([(sDV, sDV.n)] if (ot == 0 and tb < 2) else []))
                    XR = (sLD, sLD.n)
                    bank = bA if (ot * NB + tb) % 2 == 0 else bB_
                    for d in range(NT):
                        mm(bank, bank.ap[0:128, 0:512],
                           bigC[:, 2048 * slot + 128 * d:2048 * slot + 128 * d + 128],
                           bigA[:, XW * d + 512 * tb:XW * d + 512 * tb + 512],
                           start=(d == 0), stop=(d == NT - 1),
                           w=([WOL, OHAT] if d == 0 else ()))
                    rd(bank, "vector", sDV, "tensor_tensor",
                       bigB[:, 2048 * ot + 512 * tb:2048 * ot + 512 * tb + 512],
                       bank.ap[0:128, 0:512], scrF[:, 512 * xslot:512 * xslot + 512],
                       ALU.add, w=[XR])
                    x_user[xslot] = (sDV, sDV.n)
                wo_user[slot] = (sPE, sPE.n)
            RDONE = (sDV, sDV.n)
            for tb in range(NB):
                stats_sum(bS, [(bigB[:, 2048 * d + 512 * tb:2048 * d + 512 * tb + 512],
                                [RDONE] if d == 0 else []) for d in range(NT)])
                rd(bS, "vector", sDV, "tensor_scalar", rows[0:1, 64:576],
                   psS[0:1, 0:512], 1.0 / float(D), None, op0=ALU.mult,
                   w=[(sAC, sAC.n)])
                marks = []
                for d in range(NT):
                    sqo = 4096 if d % 2 == 0 else 5120
                    aop("activation", bigC[:, sqo:sqo + 512],
                        bigB[:, 2048 * d + 512 * tb:2048 * d + 512 * tb + 512],
                        AF.Square, w=[RDONE] + ([marks[d - 2]] if d >= 2
                                                else [(sDV, sDV.n)]))
                    mm(bN1, psN1[0:1, 0:512], ones_c[:, :], bigC[:, sqo:sqo + 512],
                       start=(d == 0), stop=(d == NT - 1), w=[(sAC, sAC.n)],
                       inc=([] if d == NT - 1 else [(sPE, 1)]))
                    marks.append((sPE, sPE.n))
                rd(bN1, "vector", sDV, "tensor_scalar", rows[0:1, 576:1088],
                   psN1[0:1, 0:512], 1.0 / float(D), None, op0=ALU.mult)
                vop("tensor_tensor", rows[0:1, 1088:1600], rows[0:1, 64:576],
                    rows[0:1, 64:576], ALU.mult)
                vop("tensor_tensor", rows[0:1, 576:1088], rows[0:1, 576:1088],
                    rows[0:1, 1088:1600], ALU.subtract)
                aop("activation", rows[0:1, 576:1088], rows[0:1, 576:1088], AF.Sqrt,
                    bias=CM(2), scale=CM(6), w=[(sDV, sDV.n)])
                vop("reciprocal", rows[0:1, 576:1088], rows[0:1, 576:1088],
                    w=[(sAC, sAC.n)])
                aop("copy", rows_b[0:1, 4096:4608], rows[0:1, 64:576], w=[(sDV, sDV.n)])
                aop("copy", rows_b[0:1, 4608:5120], rows[0:1, 576:1088])
                LNR = (sAC, sAC.n)
                bcast(bN1, rows_b[0:1, 4096:4608], w=[LNR])
                rd(bN1, "vector", sDV, "tensor_copy", bigC[:, 4096:4608],
                   psN1[0:128, 0:512])
                bcast(bN2, rows_b[0:1, 4608:5120], w=[LNR])
                rd(bN2, "vector", sDV, "tensor_copy", bigC[:, 4608:5120],
                   psN2[0:128, 0:512])
                MB = (sDV, sDV.n)
                for d in range(NT):
                    sl = bigB[:, 2048 * d + 512 * tb:2048 * d + 512 * tb + 512]
                    vop("tensor_tensor", sl, sl, bigC[:, 4096:4608], ALU.subtract,
                        w=[MB])
                    vop("tensor_tensor", sl, sl, bigC[:, 4608:5120], ALU.mult)
                dma("sync", sLD, y1_d[tb].rearrange("p (d t) -> p d t", d=NT),
                    bigB[:, :].rearrange("p (d t) -> p d t", d=NT)
                    [:, :, 512 * tb:512 * tb + 512], w=[(sDV, sDV.n)])
            Y1S = (sLD, sLD.n)
            ckpt('p5')

            for tb in range(NB):
                marks = []
                for d in range(NT):
                    sqo = 512 if d % 2 == 0 else 5248
                    aop("activation", bigC[:, sqo:sqo + 512],
                        bigB[:, 2048 * d + 512 * tb:2048 * d + 512 * tb + 512],
                        AF.Square, w=([marks[d - 2]] if d >= 2
                                      else [(sPE, sPE.n), (sDV, sDV.n)]))
                    mm(bN1, psN1[0:1, 0:512], ones_c[:, :], bigC[:, sqo:sqo + 512],
                       start=(d == 0), stop=(d == NT - 1), w=[(sAC, sAC.n)],
                       inc=([] if d == NT - 1 else [(sPE, 1)]))
                    marks.append((sPE, sPE.n))
                rd(bN1, "scalar", sAC, "activation", rows[0:1, 64:576],
                   psN1[0:1, 0:512], AF.Sqrt, scale=CM(5), bias=CM(2))
                vop("reciprocal", rows[0:1, 64:576], rows[0:1, 64:576], w=[(sAC, sAC.n)])
                aop("copy", rows_b[0:1, 4096:4608], rows[0:1, 64:576], w=[(sDV, sDV.n)])
                bcast(bN2, rows_b[0:1, 4096:4608], w=[(sAC, sAC.n)])
                rd(bN2, "vector", sDV, "tensor_copy", bigC[:, 0:512], psN2[0:128, 0:512])
                IVB = (sDV, sDV.n)
                for d in range(NT):
                    vop("tensor_tensor", bigC[:, 512:1024],
                        bigB[:, 2048 * d + 512 * tb:2048 * d + 512 * tb + 512],
                        bigC[:, 0:512], ALU.mult, w=[IVB])
                    aop("activation", bigA[:, XW * d + 512 * tb:XW * d + 512 * tb + 512],
                        bigC[:, 512:1024], AF.Identity, scale=c_nww[:, d:d + 1],
                        w=[(sDV, sDV.n)])
                    op_init = ALU.max if d == 0 else None
                    xsl = bigA[:, XW * d + 512 * tb:XW * d + 512 * tb + 512]
                    if d == 0:
                        vop("tensor_tensor", bigC[:, 1024 + 512 * tb:1536 + 512 * tb],
                            xsl, xsl, ALU.max, w=[(sAC, sAC.n)])
                        vop("tensor_tensor", bigC[:, 3072 + 512 * tb:3584 + 512 * tb],
                            xsl, xsl, ALU.min)
                    else:
                        vop("tensor_tensor", bigC[:, 1024 + 512 * tb:1536 + 512 * tb],
                            bigC[:, 1024 + 512 * tb:1536 + 512 * tb], xsl, ALU.max,
                            w=[(sAC, sAC.n)])
                        vop("tensor_tensor", bigC[:, 3072 + 512 * tb:3584 + 512 * tb],
                            bigC[:, 3072 + 512 * tb:3584 + 512 * tb], xsl, ALU.min)
            for tb in range(NB):
                vop("tensor_scalar", bigC[:, 3072 + 512 * tb:3584 + 512 * tb],
                    bigC[:, 3072 + 512 * tb:3584 + 512 * tb], -1.0, None, op0=ALU.mult)
                vop("tensor_tensor", bigC[:, 1024 + 512 * tb:1536 + 512 * tb],
                    bigC[:, 1024 + 512 * tb:1536 + 512 * tb],
                    bigC[:, 3072 + 512 * tb:3584 + 512 * tb], ALU.max)
            XNF = (sDV, sDV.n)
            for i in range(NT):
                bnk = bT1 if i % 2 == 0 else bT2
                transpose(bnk, bnk.ap[0:128, 128 * (i % 4 // 2):128 * (i % 4 // 2) + 128],
                          bigC[:, 1024 + 128 * i:1024 + 128 * i + 128], c_eye[:, :],
                          w=[XNF])
                rd(bnk, "vector", sDV, "reduce_max", mxc[:, i:i + 1],
                   bnk.ap[0:128, 128 * (i % 4 // 2):128 * (i % 4 // 2) + 128], AX.X)
            MXC = (sDV, sDV.n)
            transpose(bT1, psT1[0:16, 0:128], mxc[:, 0:16], c_eye[:, :], w=[MXC])
            rd(bT1, "vector", sDV, "tensor_copy", bigC[0:16, 5120:5248],
               psT1[0:16, 0:128])
            kb.op("sync", "dma_start", out=rows_b[0:1, 4096:6144]
                  if False else rows_b[0:1, 0:2048],
                  in_=bigC[0:16, 5120:5248], w=[(sDV, sDV.n)], inc=[(sLD, 16)])
            MAXR = (sLD, sLD.n)
            aop("copy", rows[0:1, 1088:3136], rows_b[0:1, 0:2048], w=[MAXR])
            vop("tensor_scalar", rows[0:1, 1088:3136], rows[0:1, 1088:3136], 1e-5,
                None, op0=ALU.max, w=[(sAC, sAC.n)])
            vop("reciprocal", rows[0:1, 64:2112] if False else rows[0:1, 1088:3136],
                rows[0:1, 1088:3136])
            vop("tensor_scalar", rows[0:1, 1088:3136], rows[0:1, 1088:3136], 127.0,
                None, op0=ALU.mult)
            aop("copy", rows_b[0:1, 0:2048], rows[0:1, 1088:3136], w=[(sDV, sDV.n)])
            S1R = (sAC, sAC.n)
            vop("reciprocal", rows[0:1, 1088:3136], rows[0:1, 1088:3136], w=[S1R])
            vop("tensor_scalar", rows[0:1, 1088:3136], rows[0:1, 1088:3136],
                cfm[0:1, 0:1], None, op0=ALU.mult)
            vop("tensor_tensor", rows_b[0:1, 2048:4096] if False else rows[0:1, 64:1088],
                rows[0:1, 1088:2112], rows[0:1, 1088:2112], ALU.mult)
            vop("tensor_tensor", rows[0:1, 64:1088], rows[0:1, 64:1088],
                rows[0:1, 1088:2112], ALU.mult)
            aop("copy", rows_b[0:1, 2048:3072], rows[0:1, 64:1088], w=[(sDV, sDV.n)])
            vop("tensor_tensor", rows[0:1, 64:1088], rows[0:1, 2112:3136],
                rows[0:1, 2112:3136], ALU.mult, w=[(sAC, sAC.n)])
            vop("tensor_tensor", rows[0:1, 64:1088], rows[0:1, 64:1088],
                rows[0:1, 2112:3136], ALU.mult)
            aop("copy", rows_b[0:1, 3072:4096], rows[0:1, 64:1088], w=[(sDV, sDV.n)])
            C13 = (sAC, sAC.n)
            for tb in range(NB):
                bcast(bN1, rows_b[0:1, 512 * tb:512 * tb + 512], w=[S1R])
                rd(bN1, "vector", sDV, "tensor_copy", bigC[:, 0:512], psN1[0:128, 0:512])
                SB1 = (sDV, sDV.n)
                for d in range(NT):
                    sl = bigA[:, XW * d + 512 * tb:XW * d + 512 * tb + 512]
                    vop("tensor_tensor", bigC[:, 512:1024], sl, bigC[:, 0:512],
                        ALU.mult, w=[SB1])
                    vop("tensor_scalar", bigC[:, 512:1024], bigC[:, 512:1024],
                        MAGIC, -MAGIC, op0=ALU.add, op1=ALU.add)
                    vop("tensor_copy", sl, bigC[:, 512:1024])
            XINT = (sDV, sDV.n)

            F_WP0, F_WP1 = 8192, 12288
            F_G, F_U, F_G2, F_HF, F_F2B, F_G2B, F_HN = 16384, 16896, 17408, 17920, 18432, 18944, 19456
            F_WD0, F_WD1 = 0, 5632
            F_R2S, F_MB, F_IB, F_SQ2 = 11264, 11776, 12288, 12800
            F_Y1 = 20480
            wp_user = [None, None]
            wd_user = [None, None]
            y1r_user = [None]
            of_user = [None, None]
            for tb in range(NB):
                for i in range(44):
                    slot = i % 2
                    wpo = F_WP0 if slot == 0 else F_WP1
                    for gu in range(2):
                        oti = i + 44 * gu
                        sh, loc = oti // 11, oti % 11
                        dma("sync", sLD, bigB[:, wpo + 2048 * gu:wpo + 2048 * gu + 2048]
                            .rearrange("p (d o) -> p d o", d=NT),
                            tgu_out[sh, loc].rearrange("d p o -> p d o"),
                            w=[wp_user[slot], Y1S] +
                            ([CC_W] if (tb == 0 and i < 2) else []))
                    WPL = (sLD, sLD.n)
                    bkg = bA if i % 2 == 0 else bB_
                    bku = bB_ if i % 2 == 0 else bA
                    for d in range(NT):
                        mm(bkg, bkg.ap[0:128, 0:512],
                           bigB[:, wpo + 128 * d:wpo + 128 * d + 128],
                           bigA[:, XW * d + 512 * tb:XW * d + 512 * tb + 512],
                           start=(d == 0), stop=(d == NT - 1),
                           w=([WPL, XINT] if d == 0 else ()))
                    for d in range(NT):
                        mm(bku, bku.ap[0:128, 0:512],
                           bigB[:, wpo + 2048 + 128 * d:wpo + 2048 + 128 * d + 128],
                           bigA[:, XW * d + 512 * tb:XW * d + 512 * tb + 512],
                           start=(d == 0), stop=(d == NT - 1))
                    rd(bkg, "scalar", sAC, "activation", bigB[:, F_G:F_G + 512],
                       bkg.ap[0:128, 0:512], AF.Relu, w=[(sDV, sDV.n)])
                    rd(bku, "scalar", sAC, "copy", bigB[:, F_U:F_U + 512],
                       bku.ap[0:128, 0:512])
                    GUE = (sAC, sAC.n)
                    vop("tensor_tensor", bigB[:, F_G2:F_G2 + 512],
                        bigB[:, F_G:F_G + 512], bigB[:, F_G:F_G + 512], ALU.mult,
                        w=[GUE])
                    vop("tensor_tensor", bigC[:, 512 * i:512 * i + 512],
                        bigB[:, F_G2:F_G2 + 512], bigB[:, F_U:F_U + 512], ALU.mult)
                    hsl = bigC[:, 512 * i:512 * i + 512]
                    if i == 0:
                        vop("tensor_tensor", bigB[:, F_HF:F_HF + 512], hsl, hsl, ALU.max)
                        vop("tensor_tensor", bigB[:, F_HN:F_HN + 512], hsl, hsl, ALU.min)
                    else:
                        vop("tensor_tensor", bigB[:, F_HF:F_HF + 512],
                            bigB[:, F_HF:F_HF + 512], hsl, ALU.max)
                        vop("tensor_tensor", bigB[:, F_HN:F_HN + 512],
                            bigB[:, F_HN:F_HN + 512], hsl, ALU.min)
                    wp_user[slot] = (sDV, sDV.n)
                vop("tensor_scalar", bigB[:, F_HN:F_HN + 512], bigB[:, F_HN:F_HN + 512],
                    -1.0, None, op0=ALU.mult)
                vop("tensor_tensor", bigB[:, F_HF:F_HF + 512], bigB[:, F_HF:F_HF + 512],
                    bigB[:, F_HN:F_HN + 512], ALU.max)
                HFD = (sDV, sDV.n)
                for s in range(4):
                    bnk = bT1 if s % 2 == 0 else bT2
                    transpose(bnk, bnk.ap[0:128, 0:128],
                              bigB[:, F_HF + 128 * s:F_HF + 128 * s + 128], c_eye[:, :],
                              w=[HFD])
                    rd(bnk, "vector", sDV, "reduce_max", mxc[:, s:s + 1],
                       bnk.ap[0:128, 0:128], AX.X)
                transpose(bT1, psT1[0:4, 0:128], mxc[:, 0:4], c_eye[:, :],
                          w=[(sDV, sDV.n)])
                rd(bT1, "vector", sDV, "tensor_copy", bigB[0:4, F_R2S:F_R2S + 128],
                   psT1[0:4, 0:128])
                kb.op("sync", "dma_start", out=rows_b[0:1, 4096:4608],
                      in_=bigB[0:4, F_R2S:F_R2S + 128], w=[(sDV, sDV.n)], inc=[(sLD, 16)])
                aop("copy", rows[0:1, 64:576], rows_b[0:1, 4096:4608],
                    w=[(sLD, sLD.n), (sDV, sDV.n)])
                vop("tensor_tensor", rows[0:1, 64:576],
                    rows_b[0:1, 2048 + 512 * tb:2560 + 512 * tb], rows[0:1, 64:576],
                    ALU.mult, w=[(sAC, sAC.n), C13])
                vop("tensor_scalar", rows[0:1, 64:576], rows[0:1, 64:576], 1e-5, None,
                    op0=ALU.max)
                vop("reciprocal", rows[0:1, 576:1088], rows[0:1, 64:576])
                vop("tensor_tensor", rows[0:1, 576:1088], rows[0:1, 576:1088],
                    rows_b[0:1, 2048 + 512 * tb:2560 + 512 * tb], ALU.mult)
                vop("tensor_scalar", rows[0:1, 576:1088], rows[0:1, 576:1088], 127.0,
                    None, op0=ALU.mult)
                vop("tensor_scalar", rows[0:1, 64:576], rows[0:1, 64:576],
                    cfm[0:1, 3:4], 1.0 / 127.0, op0=ALU.mult, op1=ALU.mult)
                aop("copy", rows_b[0:1, 4096:4608], rows[0:1, 576:1088],
                    w=[(sDV, sDV.n)])
                aop("copy", rows_b[0:1, 4608:5120], rows[0:1, 64:576])
                F2R = (sAC, sAC.n)
                bcast(bN1, rows_b[0:1, 4096:4608], w=[F2R])
                rd(bN1, "vector", sDV, "tensor_copy", bigB[:, F_F2B:F_F2B + 512],
                   psN1[0:128, 0:512])
                F2B = (sDV, sDV.n)
                for i in range(44):
                    sl = bigC[:, 512 * i:512 * i + 512]
                    vop("tensor_tensor", bigB[:, F_G2:F_G2 + 512], sl,
                        bigB[:, F_F2B:F_F2B + 512], ALU.mult, w=[F2B])
                    vop("tensor_scalar", bigB[:, F_G2:F_G2 + 512],
                        bigB[:, F_G2:F_G2 + 512], MAGIC, -MAGIC,
                        op0=ALU.add, op1=ALU.add)
                    vop("tensor_copy", sl, bigB[:, F_G2:F_G2 + 512])
                X2Q = (sDV, sDV.n)
                bcast(bN2, rows_b[0:1, 4608:5120], w=[F2R])
                rd(bN2, "vector", sDV, "tensor_copy", bigB[:, F_G2B:F_G2B + 512],
                   psN2[0:128, 0:512])
                G2B = (sDV, sDV.n)
                dma("sync", sLD, bigB[:, F_Y1:F_Y1 + NT * 512], y1_d[tb],
                    w=[y1r_user[0], Y1S])
                Y1R = (sLD, sLD.n)
                for o2 in range(NT):
                    slot = o2 % 2
                    wdo = F_WD0 if slot == 0 else F_WD1
                    sh, loc = o2 // 2, o2 % 2
                    dma("sync", sLD, bigB[:, wdo:wdo + 5632]
                        .rearrange("p (f o) -> p f o", f=44),
                        twd_out[sh, loc].rearrange("f p o -> p f o"),
                        w=[wd_user[slot], XINT] +
                        ([CC_W] if (tb == 0 and o2 < 2) else []))
                    WDL = (sLD, sLD.n)
                    bank = bA if o2 % 2 == 0 else bB_
                    for f in range(44):
                        mm(bank, bank.ap[0:128, 0:512],
                           bigB[:, wdo + 128 * f:wdo + 128 * f + 128],
                           bigC[:, 512 * f:512 * f + 512],
                           start=(f == 0), stop=(f == 43),
                           w=([WDL, X2Q] if f == 0 else ()))
                    rd(bank, "vector", sDV, "tensor_tensor",
                       bigB[:, F_R2S:F_R2S + 512], bank.ap[0:128, 0:512],
                       bigB[:, F_G2B:F_G2B + 512], ALU.mult, w=[G2B, Y1R])
                    vop("tensor_tensor", bigB[:, F_Y1 + 512 * o2:F_Y1 + 512 * o2 + 512],
                        bigB[:, F_Y1 + 512 * o2:F_Y1 + 512 * o2 + 512],
                        bigB[:, F_R2S:F_R2S + 512], ALU.add)
                    wd_user[slot] = (sPE, sPE.n)
                R2D = (sDV, sDV.n)
                stats_sum(bS, [(bigB[:, F_Y1 + 512 * d:F_Y1 + 512 * d + 512],
                                [R2D] if d == 0 else []) for d in range(NT)])
                rd(bS, "vector", sDV, "tensor_scalar", rows[0:1, 64:576],
                   psS[0:1, 0:512], 1.0 / float(D), None, op0=ALU.mult,
                   w=[(sAC, sAC.n)])
                marks = []
                for d in range(NT):
                    sqo = F_SQ2 + 512 * (d % 2)
                    aop("activation", bigB[:, sqo:sqo + 512],
                        bigB[:, F_Y1 + 512 * d:F_Y1 + 512 * d + 512], AF.Square,
                        w=[R2D] + ([marks[d - 2]] if d >= 2 else [(sDV, sDV.n)]))
                    mm(bN1, psN1[0:1, 0:512], ones_c[:, :], bigB[:, sqo:sqo + 512],
                       start=(d == 0), stop=(d == NT - 1), w=[(sAC, sAC.n)],
                       inc=([] if d == NT - 1 else [(sPE, 1)]))
                    marks.append((sPE, sPE.n))
                rd(bN1, "vector", sDV, "tensor_scalar", rows[0:1, 576:1088],
                   psN1[0:1, 0:512], 1.0 / float(D), None, op0=ALU.mult)
                vop("tensor_tensor", rows[0:1, 1088:1600], rows[0:1, 64:576],
                    rows[0:1, 64:576], ALU.mult)
                vop("tensor_tensor", rows[0:1, 576:1088], rows[0:1, 576:1088],
                    rows[0:1, 1088:1600], ALU.subtract)
                aop("activation", rows[0:1, 576:1088], rows[0:1, 576:1088], AF.Sqrt,
                    bias=CM(2), scale=CM(6), w=[(sDV, sDV.n)])
                vop("reciprocal", rows[0:1, 576:1088], rows[0:1, 576:1088],
                    w=[(sAC, sAC.n)])
                aop("copy", rows_b[0:1, 4096:4608], rows[0:1, 64:576], w=[(sDV, sDV.n)])
                aop("copy", rows_b[0:1, 4608:5120], rows[0:1, 576:1088])
                LN2R = (sAC, sAC.n)
                bcast(bN1, rows_b[0:1, 4096:4608], w=[LN2R])
                rd(bN1, "vector", sDV, "tensor_copy", bigB[:, F_MB:F_MB + 512],
                   psN1[0:128, 0:512])
                bcast(bN2, rows_b[0:1, 4608:5120], w=[LN2R])
                rd(bN2, "vector", sDV, "tensor_copy", bigB[:, F_IB:F_IB + 512],
                   psN2[0:128, 0:512])
                MB2 = (sDV, sDV.n)
                for d in range(NT):
                    osl = d % 2
                    ooff = 512 * osl
                    vop("tensor_tensor", scrF[:, ooff:ooff + 512],
                        bigB[:, F_Y1 + 512 * d:F_Y1 + 512 * d + 512],
                        bigB[:, F_MB:F_MB + 512], ALU.subtract,
                        w=[MB2, of_user[osl]])
                    vop("tensor_tensor", scrF[:, ooff:ooff + 512],
                        scrF[:, ooff:ooff + 512], bigB[:, F_IB:F_IB + 512], ALU.mult)
                    kb.op("sync", "dma_start",
                          out=out_d.rearrange("(d p) t -> p d t", p=128)
                          [:, d, 512 * tb:512 * tb + 512],
                          in_=scrF[:, ooff:ooff + 512],
                          w=[(sDV, sDV.n)], inc=[(sLD, 16)])
                    of_user[osl] = (sLD, sLD.n)
                y1r_user[0] = (sDV, sDV.n)

        except _Stop:
            pass
        if dump:
            dbg_y1 = nc.dram_tensor("dbg_y1", [NB, 128, NT * 512], BF,
                                    kind="ExternalOutput").ap()
            dbg_tgu = nc.dram_tensor("dbg_tgu", [11, NT, 128, 128], BF,
                                     kind="ExternalOutput").ap()
            dbg_twd = nc.dram_tensor("dbg_twd", [2, 44, 128, 128], BF,
                                     kind="ExternalOutput").ap()
            dbg_pg = nc.dram_tensor("dbg_pg", [NCORE, 128, 1024], BF,
                                    kind="ExternalOutput").ap()
            dbg_uv = nc.dram_tensor("dbg_uv", [128, NCH * 256], BF,
                                    kind="ExternalOutput").ap()
            dbg_ti = nc.dram_tensor("dbg_ti", [128, NCH * 128], BF,
                                    kind="ExternalOutput").ap()
            dbg_kf = nc.dram_tensor("dbg_kf", [128, 2 * TC], BF,
                                    kind="ExternalOutput").ap()
            WALL = [(sLD, sLD.n), (sLG, sLG.n), (sDV, sDV.n), (sAC, sAC.n),
                    (sPE, sPE.n), (sCC, sCC.n)]
            dma("sync", sLD, dbg_y1, y1_d, w=WALL)
            dma("sync", sLD, dbg_tgu, tgu_out[0], w=WALL)
            dma("sync", sLD, dbg_twd, twd_out[0], w=WALL)
            dma("sync", sLD, dbg_pg, pg_out[:, 0], w=WALL)
            dma("sync", sLD, dbg_uv, sp_uv[0], w=WALL)
            dma("sync", sLD, dbg_ti, sp_ti[0], w=WALL)
            dma("sync", sLD, dbg_kf, sp_kfm[0], w=WALL)
            dbg_ws = nc.dram_tensor("dbg_ws", [1, 16], F32, kind="ExternalOutput").ap()
            dbg_tgin = nc.dram_tensor("dbg_tgin", [11, NT, 128, 128], BF,
                                      kind="ExternalOutput").ap()
            dbg_twin = nc.dram_tensor("dbg_twin", [2, 44, 128, 128], BF,
                                      kind="ExternalOutput").ap()
            dma("sync", sLD, dbg_ws, ws_out, w=WALL)
            dma("sync", sLD, dbg_tgin, tgu_in, w=WALL)
            dma("sync", sLD, dbg_twin, twd_in, w=WALL)
            dbg_rw = nc.dram_tensor("dbg_rw", [1, 56], F32, kind="ExternalOutput").ap()
            dbg_rb = nc.dram_tensor("dbg_rb", [1, 4096], BF, kind="ExternalOutput").ap()
            dma("sync", sLD, dbg_rw, rows[0:1, 0:56], w=WALL)
            dbg_rw2 = nc.dram_tensor("dbg_rw2", [1, 32], F32, kind="ExternalOutput").ap()
            dma("sync", sLD, dbg_rw2, rows[0:1, 2552:2584], w=WALL)
            dma("sync", sLD, dbg_rb, rows_b[0:1, 0:4096], w=WALL)
            dbg_xi = nc.dram_tensor("dbg_xi", [128, 4112], BF, kind="ExternalOutput").ap()
            dbg_x2 = nc.dram_tensor("dbg_x2", [128, 2048], BF, kind="ExternalOutput").ap()
            dma("sync", sLD, dbg_xi, bigA[:, 0:4112], w=WALL)
            dma("sync", sLD, dbg_x2, bigC[:, 0:2048], w=WALL)
        kb.wait("sync", sLD, sLG, sDV, sAC, sPE, sGP, sCC)

        @block.tensor
        def _(e):
            for fn in kb.q["tensor"]:
                fn(e)

        @block.vector
        def _(e):
            for fn in kb.q["vector"]:
                fn(e)

        @block.scalar
        def _(e):
            for fn in kb.q["scalar"]:
                fn(e)

        @block.gpsimd
        def _(e):
            for fn in kb.q["gpsimd"]:
                fn(e)

        @block.sync
        def _(e):
            for fn in kb.q["sync"]:
                fn(e)

    return nc


_NC_CACHE = [None]
_LAST_RES = [None]


def kernel(**inputs):
    x = np.asarray(inputs["L0_attn_hidden_states"], np.float32)[0]
    Wq = np.asarray(inputs["L0_attn_Wq"], np.float32)
    Wk = np.asarray(inputs["L0_attn_Wk"], np.float32)
    Wv = np.asarray(inputs["L0_attn_Wv"], np.float32)
    Wb = np.asarray(inputs["L0_attn_Wb"], np.float32)
    cq = np.asarray(inputs["L0_attn_conv_q"], np.float32)
    ck = np.asarray(inputs["L0_attn_conv_k"], np.float32)
    cv = np.asarray(inputs["L0_attn_conv_v"], np.float32)
    nw = np.asarray(inputs["L0_attn_norm_w"], np.float32)
    Wo = np.asarray(inputs["L0_attn_Wo"], np.float32)
    Wgu = np.asarray(inputs["L0_ffn_w_gate_up"], np.float32)
    Wd = np.asarray(inputs["L0_ffn_w_down"], np.float32)
    nww = np.asarray(inputs["L0_ffn_norm_weight"], np.float32)

    wq3 = np.ascontiguousarray(Wq.reshape(D, H, DK).transpose(1, 0, 2))
    wk3 = np.ascontiguousarray(Wk.reshape(D, H, DK).transpose(1, 0, 2))
    wv3 = np.ascontiguousarray(Wv.reshape(D, H, DK).transpose(1, 0, 2))
    wo16 = np.ascontiguousarray(Wo.reshape(D, NT, 128).transpose(1, 0, 2))
    convw = np.ascontiguousarray(np.stack([cq, ck, cv], 0))
    wguT = np.ascontiguousarray(Wgu.T)
    wdT = np.ascontiguousarray(Wd.T)
    nw2 = np.ascontiguousarray(nw.reshape(2, 128).T)
    nww16 = np.ascontiguousarray(nww.reshape(NT, 128).T)

    eye = np.eye(128, dtype=np.float32)
    c_eye = eye.astype(BF16)
    c_eye2 = (2.0 * eye).astype(BF16)
    c_slm = np.tril(np.ones((128, 128), np.float32), -1).astype(BF16)
    c_um = np.triu(np.ones((128, 128), np.float32)).astype(BF16)
    e256 = np.zeros((128, 512), np.float32)
    e256[:, 0:128] = eye
    e256[:, 256 + 128:512] = eye
    c_e256 = e256.astype(BF16)

    c_misc_h = np.zeros((128, 32), np.float32)
    c_misc_h[:, 8] = 1.0 / float(D * 11264)
    c_misc_h[:, 9] = 1.0 / float(FF * D)
    c_misc_h[:, 10:16] = 1.0
    c_misc_h[:, 16] = 1.0 / float(D * 11264)
    c_misc_h[:, 17] = 1.0 / float(FF * D)
    c_misc_h[:, 18:32] = 1.0
    c_misc_h[:, 0] = 256e-6
    c_misc_h[:, 1] = 1e-6
    c_misc_h[:, 2] = 1e-5
    c_misc_h[:, 3] = 256.0
    c_misc_h[:, 4] = 1.0 / 256.0
    c_misc_h[:, 5] = 1.0 / 2048.0
    c_misc_h[:, 6] = 1.0
    in_maps = []
    for c in range(NCORE):
        lo = c * TC
        sel = np.zeros((128, 16), np.float32)
        sel[:, :c] = 1.0
        sel[:, 8:16] = 1.0 - sel[:, 0:8]
        xh = (np.zeros((D, HALO), np.float32) if c == 0
              else np.ascontiguousarray(x[lo - HALO:lo].T))
        in_maps.append({
            "xT": np.ascontiguousarray(x[lo:lo + TC].T),
            "xhalo": xh,
            "wq": wq3, "wk": wk3, "wv": wv3, "wb": Wb,
            "convw": convw, "wo": wo16,
            "wguT_sh": np.ascontiguousarray(wguT[:, GU_SH * c:GU_SH * (c + 1)]),
            "wdT_sh": np.ascontiguousarray(wdT[:, WD_SH * c:WD_SH * (c + 1)]),
            "nw2": nw2, "nww16": nww16,
            "c_eye_d": c_eye, "c_eye2_d": c_eye2, "c_slm_d": c_slm, "c_um_d": c_um,
            "c_e256_d": c_e256, "c_sel_d": sel, "c_misc_d": c_misc_h,
        })
    if _NC_CACHE[0] is None:
        _NC_CACHE[0] = build()
    nc = _NC_CACHE[0]
    res = run_bass_kernel_spmd(nc, in_maps, core_ids=list(range(NCORE)))
    _LAST_RES[0] = res
    out = np.concatenate(
        [np.asarray(res.results[c]["out"], np.float32).T for c in range(NCORE)], 0)
    return out[None]




# version 10
# speedup vs baseline: 1.0545x; 1.0545x over previous
import numpy as np
import ml_dtypes
import concourse.bass as bass
import concourse.mybir as mybir
from concourse.bass_utils import run_bass_kernel_spmd
from contextlib import ExitStack

BF16 = ml_dtypes.bfloat16
F32 = mybir.dt.float32
BF = mybir.dt.bfloat16
AF = mybir.ActivationFunctionType
ALU = mybir.AluOpType
AX = mybir.AxisListType
MAGIC = float(np.float32(3 * 2 ** 22))

NCORE = 8
T, D, H, DK = 16384, 2048, 8, 256
TC = T // NCORE
HALO = 8
XW = TC + HALO
PBLK = [(0, 512), (512, 512), (1024, 512), (1536, 512), (2048, 8)]
NCH, CH = 16, 128
FF = 5632
NT = 16
NB = 4
GU_SH = 1408
WD_SH = 256


class Sem:
    def __init__(self, h):
        self.h = h
        self.n = 0


class KB:
    ENGS = ("tensor", "vector", "scalar", "gpsimd", "sync")

    def __init__(self, strict=False):
        self.q = {k: [] for k in self.ENGS}
        self.lastwait = {k: {} for k in self.ENGS}
        self.strict = strict
        self.own = {}

    def op(self, eng, meth, *a, w=(), inc=(), **kw):
        if self.strict and eng in self.own:
            w = list(w) + [self.own[eng]] + ([self.ownq[eng]] if eng in
                                             getattr(self, "ownq", {}) else [])
        wl = []
        for x in w:
            if x is None:
                continue
            s, v = x if isinstance(x, tuple) else (x, x.n)
            if v <= 0:
                continue
            lw = self.lastwait[eng]
            if lw.get(id(s), -1) >= v:
                continue
            lw[id(s)] = v
            wl.append((s.h, v))
        il = []
        for x in inc:
            s, k = x if isinstance(x, tuple) else (x, 1)
            s.n += k
            il.append((s.h, k))

        def fn(e, meth=meth, a=a, kw=kw, wl=wl, il=il):
            for h, v in wl:
                e.wait_ge(h, v)
            inst = getattr(e, meth)(*a, **kw)
            for h, k in il:
                inst.then_inc(h, k)
        self.q[eng].append(fn)

    def wait(self, eng, *sems):
        wl = []
        for x in sems:
            s, v = x if isinstance(x, tuple) else (x, x.n)
            if v <= 0:
                continue
            lw = self.lastwait[eng]
            if lw.get(id(s), -1) >= v:
                continue
            lw[id(s)] = v
            wl.append((s.h, v))

        def fn(e, wl=wl):
            for h, v in wl:
                e.wait_ge(h, v)
        self.q[eng].append(fn)


class Bank:
    def __init__(self, ap):
        self.ap = ap
        self.rd = []
        self.wr = None

    def take(self):
        r, self.rd = self.rd, []
        return r


class _Stop(Exception):
    pass


def build(strict=False, stop_after=None, dump=False):
    nc = bass.Bass("TRN2", target_bir_lowering=False,
                   detect_race_conditions=strict)

    def din(name, shape, dt=F32):
        return nc.dram_tensor(name, shape, dt, kind="ExternalInput").ap()

    xT = din("xT", [D, TC])
    xhalo = din("xhalo", [D, HALO])
    wq = din("wq", [H, D, DK])
    wk = din("wk", [H, D, DK])
    wv = din("wv", [H, D, DK])
    wb = din("wb", [D, H])
    convw = din("convw", [3, D, 4])
    wo = din("wo", [NT, D, 128])
    wguT_sh = din("wguT_sh", [D, GU_SH])
    wdT_sh = din("wdT_sh", [FF, WD_SH])
    nw2_i = din("nw2", [128, 2])
    nww_i = din("nww16", [128, NT])
    c_eye_i = din("c_eye_d", [128, 128], BF)
    c_eye2_i = din("c_eye2_d", [128, 128], BF)
    c_slm_i = din("c_slm_d", [128, 128], BF)
    c_um_i = din("c_um_d", [128, 128], BF)
    c_e256_i = din("c_e256_d", [128, 512], BF)
    c_sel_i = din("c_sel_d", [128, 16])
    c_misc_i = din("c_misc_d", [128, 32])
    out_d = nc.dram_tensor("out", [D, TC], F32, kind="ExternalOutput").ap()

    sp_q = nc.dram_tensor("sp_q", [H, 128, 2 * TC], BF).ap()
    sp_kfm = nc.dram_tensor("sp_kfm", [H, 128, 2 * TC], BF).ap()
    sp_ktm = nc.dram_tensor("sp_ktm", [H, 128, NCH * 256], BF).ap()
    sp_ti = nc.dram_tensor("sp_ti", [H, 128, NCH * 128], BF).ap()
    sp_uv = nc.dram_tensor("sp_uv", [H, 128, NCH * 256], BF).ap()
    sp_kq = nc.dram_tensor("sp_kq", [H, 128, NCH * 128], BF).ap()
    pg_in = nc.dram_tensor("pg_in", [H, 128, 1024], BF).ap()
    pg_out = nc.dram_tensor("pg_out", [NCORE, H, 128, 1024], BF, addr_space="Shared").ap()
    ws_in = nc.dram_tensor("ws_in", [1, 16], F32).ap()
    ws_out = nc.dram_tensor("ws_out", [1, 16], F32, addr_space="Shared").ap()
    tgu_in = nc.dram_tensor("tgu_in", [11, NT, 128, 128], BF).ap()
    tgu_out = nc.dram_tensor("tgu_out", [NCORE, 11, NT, 128, 128], BF, addr_space="Shared").ap()
    twd_in = nc.dram_tensor("twd_in", [2, 44, 128, 128], BF).ap()
    twd_out = nc.dram_tensor("twd_out", [NCORE, 2, 44, 128, 128], BF, addr_space="Shared").ap()
    y1_d = nc.dram_tensor("y1_d", [NB, 128, NT * 512], BF).ap()

    kb = KB(strict=strict)
    with ExitStack() as ctx:
        E = ctx.enter_context
        bigA = E(nc.sbuf_tensor("bigA", [128, NT * XW], BF))
        bigB = E(nc.sbuf_tensor("bigB", [128, 32768], BF))
        bigC = E(nc.sbuf_tensor("bigC", [128, 22528], BF))
        scrF = E(nc.sbuf_tensor("scrF", [128, 1472], F32))
        c_eye = E(nc.sbuf_tensor("c_eye", [128, 128], BF))
        c_eye2 = E(nc.sbuf_tensor("c_eye2", [128, 128], BF))
        c_slm = E(nc.sbuf_tensor("c_slm", [128, 128], BF))
        c_um = E(nc.sbuf_tensor("c_um", [128, 128], BF))
        c_e256 = E(nc.sbuf_tensor("c_e256", [128, 512], BF))
        c_sel = E(nc.sbuf_tensor("c_sel", [128, 16], F32))
        c_misc = E(nc.sbuf_tensor("c_misc", [128, 32], F32))
        c_conv = E(nc.sbuf_tensor("c_conv", [128, 192], F32))
        c_nw2 = E(nc.sbuf_tensor("c_nw2", [128, 2], F32))
        c_nww = E(nc.sbuf_tensor("c_nww", [128, 16], F32))
        ones_c = E(nc.sbuf_tensor("ones_c", [128, 1], BF))
        ones_r = E(nc.sbuf_tensor("ones_r", [1, 128], BF))
        ones_rf = E(nc.sbuf_tensor("ones_rf", [1, 128], F32))
        beta_t = E(nc.sbuf_tensor("beta_t", [128, 128], F32))
        rows = E(nc.sbuf_tensor("rows", [1, 3136], F32))
        rows_b = E(nc.sbuf_tensor("rows_b", [1, 5120], BF))
        mxc = E(nc.sbuf_tensor("mxc", [128, 16], BF))
        colsF = E(nc.sbuf_tensor("colsF", [128, 2], F32))
        cfm = E(nc.sbuf_tensor("cfm", [128, 16], F32))
        cfs = E(nc.sbuf_tensor("cfs", [128, 16], F32))
        S0 = E(nc.sbuf_tensor("S0", [128, 1024], BF))
        psA = E(nc.psum_tensor("psA", [128, 512], F32))
        psB = E(nc.psum_tensor("psB", [128, 512], F32))
        psC = E(nc.psum_tensor("psC", [128, 512], F32))
        psT1 = E(nc.psum_tensor("psT1", [128, 1024], BF))
        psT2 = E(nc.psum_tensor("psT2", [128, 1024], BF))
        psN1 = E(nc.psum_tensor("psN1", [128, 512], F32))
        psN2 = E(nc.psum_tensor("psN2", [128, 512], F32))
        psS = E(nc.psum_tensor("psS", [128, 512], F32))
        sLD = Sem(E(nc.semaphore("sLD")))
        sLG = Sem(E(nc.semaphore("sLG")))
        sPE = Sem(E(nc.semaphore("sPE")))
        sDV = Sem(E(nc.semaphore("sDV")))
        sAC = Sem(E(nc.semaphore("sAC")))
        sGP = Sem(E(nc.semaphore("sGP")))
        sCC = Sem(E(nc.semaphore("sCC")))
        block = E(nc.Block())

        kb.own = {"tensor": sPE, "vector": sDV, "scalar": sAC, "gpsimd": sLG,
                  "sync": sLD}
        kb.ownq = {"gpsimd": sGP}
        bA, bB_, bC_ = Bank(psA), Bank(psB), Bank(psC)
        bT1, bT2 = Bank(psT1), Bank(psT2)
        bN1, bN2, bS = Bank(psN1), Bank(psN2), Bank(psS)

        def ckpt(name):
            if stop_after == name:
                raise _Stop()

        def mm(bank, out_ap, lhsT, rhs, start, stop, w=(), inc=()):
            ww = list(w) + (bank.take() if start else [])
            ii = list(inc) + ([(sPE, 1)] if stop else [])
            kb.op("tensor", "matmul", out_ap, lhsT, rhs, w=ww, inc=ii,
                  start=start, stop=stop)
            if stop:
                bank.wr = (sPE, sPE.n)

        def transpose(bank, out_ap, in_ap, ident, w=()):
            ww = list(w) + bank.take()
            kb.op("tensor", "transpose", out_ap, in_ap, ident, w=ww, inc=[(sPE, 1)])
            bank.wr = (sPE, sPE.n)

        def rd(bank, eng, sem, meth, *a, w=(), inc=(), **kw):
            ww = list(w) + ([bank.wr] if bank.wr else [])
            kb.op(eng, meth, *a, w=ww, inc=list(inc) + [(sem, 1)], **kw)
            bank.rd.append((sem, sem.n))

        def vop(meth, *a, w=(), inc=(), **kw):
            kb.op("vector", meth, *a, w=w, inc=list(inc) + [(sDV, 1)], **kw)

        def aop(meth, *a, w=(), inc=(), **kw):
            kb.op("scalar", meth, *a, w=w, inc=list(inc) + [(sAC, 1)], **kw)

        def dma(eng, sem, dst, src, w=()):
            kb.op(eng, "dma_start", w=w, inc=[(sem, 16)], out=dst, in_=src)

        try:

            dma("sync", sLD, c_eye[:, :], c_eye_i)
            dma("sync", sLD, c_eye2[:, :], c_eye2_i)
            dma("sync", sLD, c_slm[:, :], c_slm_i)
            dma("sync", sLD, c_um[:, :], c_um_i)
            dma("sync", sLD, c_e256[:, :], c_e256_i)
            dma("sync", sLD, c_sel[:, :], c_sel_i)
            dma("sync", sLD, c_misc[:, :], c_misc_i)
            dma("sync", sLD, c_nw2[:, :], nw2_i)
            dma("sync", sLD, c_nww[:, :], nww_i)
            for t3 in range(3):
                dma("sync", sLD,
                    c_conv[:, 64 * t3:64 * t3 + 64].rearrange("p (f j) -> p f j", f=16),
                    convw[t3].rearrange("(f p) j -> p f j", p=128))
            CONSTW = (sLD, sLD.n)

            xv = bigA[:, :].rearrange("p (d c) -> p d c", d=NT)
            dma("gpsimd", sLG, xv[:, :, 0:HALO], xhalo.rearrange("(d p) c -> p d c", p=128))
            for dq in range(4):
                dma("gpsimd", sLG, xv[:, 4 * dq:4 * dq + 4, HALO:XW],
                    xT.rearrange("(d p) t -> p d t", p=128)[:, 4 * dq:4 * dq + 4, :])
            dma("gpsimd", sLG, bigB[:, 32640:32768].rearrange("p (d o) -> p d o", d=NT),
                wb.rearrange("(d p) o -> p d o", p=128))
            XLOAD = (sLG, sLG.n)
            kb.op("gpsimd", "memset", ones_c[:, :], 1.0, inc=[(sGP, 1)])
            kb.op("gpsimd", "memset", ones_r[:, :], 1.0, inc=[(sGP, 1)])
            kb.op("gpsimd", "memset", ones_rf[:, :], 1.0, inc=[(sGP, 1)])
            kb.op("gpsimd", "memset", rows[0:1, 0:16], 0.0, inc=[(sGP, 1)])
            ONES = (sGP, sGP.n)

            for tb in range(NB):
                for d in range(NT):
                    mm(bS, psS[0:8, 0:512], bigB[:, 32640 + 8 * d:32640 + 8 * d + 8],
                       bigA[:, XW * d + HALO + 512 * tb: XW * d + HALO + 512 * tb + 512],
                       start=(d == 0), stop=(d == NT - 1),
                       w=([XLOAD, CONSTW] if d == 0 else ()))
                rd(bS, "scalar", sAC, "activation", bigB[0:8, 512 * tb:512 * tb + 512],
                   psS[0:8, 0:512], AF.Sigmoid)
            BETAF = (sAC, sAC.n)
            for i in range(NCH):
                bnk = bT1 if i % 2 == 0 else bT2
                transpose(bnk, bnk.ap[0:128, 0:8], bigB[0:8, 128 * i:128 * i + 128],
                          c_eye[0:8, 0:8], w=[BETAF])
                rd(bnk, "vector", sDV, "tensor_copy", beta_t[:, 8 * i:8 * i + 8],
                   bnk.ap[0:128, 0:8])
            BETAT = (sDV, sDV.n)
            kb.op("vector", "memset", rows[:, :], 0.0, inc=[(sDV, 1)], w=[ONES])
            kb.op("vector", "memset", rows_b[:, :], 0.0, inc=[(sDV, 1)])
            ckpt('beta')

            accq = bigC
            for k in range(NT):
                dma("sync", sLD, scrF[:, 0:1408],
                    wguT_sh.rearrange("(a p) o -> a p o", p=128)[k], w=[(sDV, sDV.n)])
                vop("reduce_sum", scrF[:, 1408 + k:1409 + k], scrF[:, 0:1408], AX.X,
                    apply_absolute_value=True, w=[(sLD, sLD.n)])
            for k in range(44):
                dma("sync", sLD, scrF[:, 0:256],
                    wdT_sh.rearrange("(a p) o -> a p o", p=128)[k],
                    w=[(sDV, sDV.n)])
                vop("reduce_sum", scrF[:, 1424 + k:1425 + k], scrF[:, 0:256], AX.X,
                    apply_absolute_value=True, w=[(sLD, sLD.n)])
            vop("reduce_sum", colsF[:, 0:1], scrF[:, 1408:1424], AX.X)
            vop("reduce_sum", colsF[:, 1:2], scrF[:, 1424:1468], AX.X)
            kb.op("vector", "memset", scrF[:, 1468:1469], 1.0, inc=[(sDV, 1)])
            mm(bS, psS[0:1, 0:2], scrF[:, 1468:1469], colsF[:, 0:2], start=True, stop=True,
               w=[(sDV, sDV.n)])
            rd(bS, "vector", sDV, "tensor_copy", rows[0:1, 0:2], psS[0:1, 0:2],
               w=[ONES])
            kb.op("sync", "dma_start", out=ws_in, in_=rows[0:1, 0:16],
                  w=[(sDV, sDV.n), (sGP, sGP.n)], inc=[(sLD, 16)])
            kb.op("gpsimd", "collective_compute", "AllReduce", ALU.add,
                  [list(range(NCORE))], [ws_in], [ws_out],
                  w=[(sLD, sLD.n)], inc=[(sCC, 1)])
            dma("sync", sLD, rows[0:1, 16:32], ws_out, w=[(sCC, sCC.n)])
            NGU, NWD = float(D * 11264), float(FF * D)
            vop("tensor_scalar", rows[0:1, 1536:2048], rows[0:1, 0:512],
                1.0 / NGU, 1e-5, op0=ALU.mult, op1=ALU.max,
                w=[(sLD, sLD.n), (sDV, sDV.n)])
            vop("tensor_scalar", rows[0:1, 2048:2560], rows[0:1, 0:512],
                1.0 / NWD, 1e-5, op0=ALU.mult, op1=ALU.max)
            vop("reciprocal", rows[0:1, 2560:3072], rows[0:1, 1536:2048])
            MRDYt = (sDV, sDV.n)
            mm(bS, psS[0:128, 0:2], ones_rf[:, :], rows[0:1, 1552:1554],
               start=True, stop=True, w=[MRDYt, ONES])
            mm(bS, psS[0:128, 2:4], ones_rf[:, :], rows[0:1, 2064:2066],
               start=True, stop=True)
            mm(bS, psS[0:128, 4:6], ones_rf[:, :], rows[0:1, 2576:2578],
               start=True, stop=True)
            PE3 = (sPE, sPE.n)
            vop("reciprocal", rows[0:1, 2560:3072], rows[0:1, 2048:2560], w=[PE3])
            mm(bS, psS[0:128, 6:8], ones_rf[:, :], rows[0:1, 2576:2578],
               start=True, stop=True, w=[(sDV, sDV.n)])
            rd(bS, "vector", sDV, "tensor_copy", cfm[:, 0:2], psS[0:128, 0:2])
            rd(bS, "vector", sDV, "tensor_copy", cfm[:, 2:4], psS[0:128, 2:4])
            rd(bS, "vector", sDV, "tensor_copy", cfs[:, 0:2], psS[0:128, 4:6])
            rd(bS, "vector", sDV, "tensor_copy", cfs[:, 2:4], psS[0:128, 6:8])
            TSCALE = (sDV, sDV.n)
            ckpt('tern1')

            def CM(i):
                return c_misc[0:1, i:i + 1]

            def stats_sum(bank, tiles):
                for i, (ap_, w_) in enumerate(tiles):
                    mm(bank, bank.ap[0:1, 0:512], ones_c[:, :], ap_,
                       start=(i == 0), stop=(i == len(tiles) - 1), w=list(w_))

            def bcast(bank, row_bf, w=()):
                mm(bank, bank.ap[0:128, 0:512], ones_r[:, :], row_bf,
                   start=True, stop=True, w=w)

            O_QB, O_KB, O_KT, O_VT = 0, 4096, 8192, 12288
            O_SQ, O_QP = 16384, 20480
            O_TI, O_UV, O_KQ = 22656, 24704, 28800
            O_M, O_X, O_XT, O_R, O_RT, O_DG = 30848, 30976, 31104, 31232, 31360, 31488
            C_TERN, C_PG = 0, 2816

            BETAPE = (sPE, sPE.n)
            bufs_free = [None]
            wslot_user = [None, None]

            def proj_conv(widx, conv_off, dst_off, half, extraw=()):
                slot = widx % 2
                for j in range(4):
                    vop("tensor_scalar", bigB[:, O_DG + 128 * j:O_DG + 128 * j + 128],
                        c_eye[:, :], c_conv[:, conv_off + j:conv_off + j + 1], None,
                        op0=ALU.mult, w=[CONSTW, (sPE, sPE.n)] + list(extraw))
                DIAG = (sDV, sDV.n)
                for bi, (c0, cw) in enumerate(PBLK):
                    bank = bA if bi % 2 == 0 else bB_
                    for d in range(NT):
                        mm(bank, bank.ap[0:128, 0:cw],
                           bigC[:, 11008 + 2048 * slot + 128 * d:
                                11008 + 2048 * slot + 128 * d + 128],
                           bigA[:, XW * d + c0:XW * d + c0 + cw],
                           start=(d == 0), stop=(d == NT - 1),
                           w=([(sLG, sLG.n), XLOAD] if (bi == 0 and d == 0) else ()))
                    rd(bank, "scalar", sAC, "copy", bigB[:, O_QP + c0:O_QP + c0 + cw],
                       bank.ap[0:128, 0:cw], w=list(extraw))
                QPRE = (sAC, sAC.n)
                for tb in range(NB):
                    for j in range(4):
                        s0 = HALO - 3 + 512 * tb + j
                        mm(bC_, psC[0:128, 0:512],
                           bigB[:, O_DG + 128 * j:O_DG + 128 * j + 128],
                           bigB[:, O_QP + s0:O_QP + s0 + 512],
                           start=(j == 0), stop=(j == 3),
                           w=([QPRE, DIAG] if j == 0 else ()))
                    rd(bC_, "scalar", sAC, "activation",
                       bigB[:, dst_off + 2048 * half + 512 * tb:
                            dst_off + 2048 * half + 512 * tb + 512],
                       psC[0:128, 0:512], AF.Silu, w=list(extraw))
                return (sAC, sAC.n)

            def l2norm(dst_off, sc, bi_):
                vop("tensor_tensor", bigB[:, O_SQ:O_SQ + 4096],
                    bigB[:, dst_off:dst_off + 4096],
                    bigB[:, dst_off:dst_off + 4096], ALU.mult, w=[(sAC, sAC.n)])
                SQ = (sDV, sDV.n)
                for tb in range(NB):
                    stats_sum(bS, [
                        (bigB[:, O_SQ + 512 * tb:O_SQ + 512 * tb + 512], [SQ, ONES]),
                        (bigB[:, O_SQ + 2048 + 512 * tb:O_SQ + 2048 + 512 * tb + 512], []),
                    ])
                    rd(bS, "scalar", sAC, "activation", rows[0:1, 64:576],
                       psS[0:1, 0:512], AF.Sqrt, scale=sc, bias=bi_)
                    vop("reciprocal", rows[0:1, 64:576], rows[0:1, 64:576], w=[(sAC, sAC.n)])
                    aop("copy", rows_b[0:1, 4096:4608], rows[0:1, 64:576], w=[(sDV, sDV.n)])
                    bcast(bN2, rows_b[0:1, 4096:4608], w=[(sAC, sAC.n)])
                    for hf in range(2):
                        sl = bigB[:, dst_off + 2048 * hf + 512 * tb:
                                  dst_off + 2048 * hf + 512 * tb + 512]
                        rd(bN2, "vector", sDV, "tensor_tensor", sl, sl,
                           psN2[0:128, 0:512], ALU.mult)

            TERN_DONE = None
            CC_W = None
            for h in range(H):
                extraw = ([BETAPE] if h == 0 else []) + [bufs_free[0]]
                wsrc = [(wq, 0), (wq, 1), (wk, 0), (wk, 1), (wv, 0), (wv, 1)]

                def wload(widx):
                    wt, hf = wsrc[widx]
                    slot = widx % 2
                    dma("gpsimd", sLG, bigC[:, 11008 + 2048 * slot:11008 + 2048 * slot + 2048]
                        .rearrange("p (d o) -> p d o", d=NT),
                        wt[h][:, 128 * hf:128 * hf + 128]
                        .rearrange("(d p) o -> p d o", p=128),
                        w=[wslot_user[slot], (bufs_free[0] if h > 0 else None)])

                wload(0)
                wload(1)
                dsts = [O_QB, O_QB, O_KB, O_KB, O_KT, O_KT]
                coffs = [8 * h, 8 * h + 4, 64 + 8 * h, 64 + 8 * h + 4,
                         128 + 8 * h, 128 + 8 * h + 4]
                for widx in range(6):
                    proj_conv(widx, coffs[widx], dsts[widx], widx % 2, extraw=extraw)
                    wslot_user[widx % 2] = (sPE, sPE.n)
                    if widx + 2 < 6:
                        wload(widx + 2)
                VDONE = (sAC, sAC.n)
                ckpt('proj0')
                l2norm(O_QB, CM(3), CM(0))
                l2norm(O_KB, CM(6), CM(1))
                KNORM = (sDV, sDV.n)
                ckpt('l2')

                def do_transposes(src_off, dst_off, wfirst):
                    for g in range(8):
                        bnk = bT1 if g % 2 == 0 else bT2
                        for s in range(4):
                            idx = 4 * g + s
                            ci, m = idx // 2, idx % 2
                            transpose(bnk, bnk.ap[0:128, 128 * s:128 * s + 128],
                                      bigB[:, src_off + 2048 * m + 128 * ci:
                                           src_off + 2048 * m + 128 * ci + 128],
                                      c_eye[:, :],
                                      w=([wfirst] if (g == 0 and s == 0) else ()))
                        for s in range(4):
                            idx = 4 * g + s
                            ci, m = idx // 2, idx % 2
                            rd(bnk, "vector", sDV, "tensor_copy",
                               bigB[:, dst_off + 256 * ci + 128 * m:
                                    dst_off + 256 * ci + 128 * m + 128],
                               bnk.ap[0:128, 128 * s:128 * s + 128])
                do_transposes(O_KT, O_VT, VDONE)
                do_transposes(O_KB, O_KT, KNORM)
                TRD = (sDV, sDV.n)
                ckpt('tr')

                pgb = C_PG + 1024 * h
                for pt in range(2):
                    vop("tensor_copy", bigC[:, pgb + 512 * pt:pgb + 512 * pt + 256],
                        c_e256[:, 256 * pt:256 * pt + 256],
                        w=[CONSTW] + ([TERN_DONE] if h == 1 else []))
                    kb.op("vector", "memset",
                          bigC[:, pgb + 512 * pt + 256:pgb + 512 * pt + 512], 0.0,
                          inc=[(sDV, 1)])
                SCR = [(O_M, O_X, O_XT, O_R, O_RT), (32000, 32128, 32256, 32384, 32512)]
                BKS = [(bN1, psN1, bN2, psN2, bT1, psT1),
                       (bS, psS, bA, psA, bT2, psT2)]

                bmk = [None, None]

                def b_stageA(ci):
                    par = ci % 2
                    oM, oX, oXT, oR, oRT = SCR[par]
                    b1, p1, b2, p2, bt, pt_ = BKS[par]
                    kc = 128 * ci
                    bcol = beta_t[:, 8 * ci + h:8 * ci + h + 1]
                    mm(b1, p1[0:128, 0:128], bigB[:, O_KB + kc:O_KB + kc + 128],
                       bigB[:, O_KB + kc:O_KB + kc + 128], start=True, stop=False,
                       w=[KNORM, BETAT])
                    mm(b1, p1[0:128, 0:128],
                       bigB[:, O_KB + 2048 + kc:O_KB + 2048 + kc + 128],
                       bigB[:, O_KB + 2048 + kc:O_KB + 2048 + kc + 128],
                       start=False, stop=True)
                    rd(b1, "vector", sDV, "scalar_tensor_tensor",
                       bigB[:, oM:oM + 128], p1[0:128, 0:128], bcol, c_slm[:, :],
                       op0=ALU.mult, op1=ALU.mult)
                    vop("tensor_tensor", bigB[:, oM:oM + 128], bigB[:, oM:oM + 128],
                        c_eye[:, :], ALU.add)
                    MRDY = (sDV, sDV.n)
                    transpose(bt, pt_[0:128, 0:128], bigB[:, oM:oM + 128],
                              c_eye[:, :], w=[MRDY])
                    rd(bt, "vector", sDV, "tensor_tensor", bigB[:, oX:oX + 128],
                       c_eye2[:, :], pt_[0:128, 0:128], ALU.subtract)
                    vop("tensor_tensor", bigB[:, oXT:oXT + 128], c_eye2[:, :],
                        bigB[:, oM:oM + 128], ALU.subtract)
                    bmk[par] = (sDV, sDV.n)

                def b_step(ci, it):
                    par = ci % 2
                    oM, oX, oXT, oR, oRT = SCR[par]
                    b1, p1, b2, p2, bt, pt_ = BKS[par]
                    kc = 128 * ci
                    bcol = beta_t[:, 8 * ci + h:8 * ci + h + 1]
                    last = (it == 5)
                    XRDY = bmk[par]
                    mm(b1, p1[0:128, 0:128], bigB[:, oM:oM + 128],
                       bigB[:, oX:oX + 128], start=True, stop=True, w=[XRDY])
                    if not last:
                        mm(b2, p2[0:128, 0:128], bigB[:, oX:oX + 128],
                           bigB[:, oM:oM + 128], start=True, stop=True, w=[XRDY])
                    rd(b1, "vector", sDV, "tensor_tensor", bigB[:, oR:oR + 128],
                       c_eye2[:, :], p1[0:128, 0:128], ALU.subtract)
                    if not last:
                        rd(b2, "vector", sDV, "tensor_tensor",
                           bigB[:, oRT:oRT + 128], c_eye2[:, :],
                           p2[0:128, 0:128], ALU.subtract)
                    RRDY = (sDV, sDV.n)
                    mm(b1, p1[0:128, 128:256], bigB[:, oXT:oXT + 128],
                       bigB[:, oR:oR + 128], start=True, stop=True, w=[RRDY])
                    if not last:
                        mm(b2, p2[0:128, 128:256], bigB[:, oX:oX + 128],
                           bigB[:, oRT:oRT + 128], start=True, stop=True, w=[RRDY])
                        rd(b1, "vector", sDV, "tensor_copy", bigB[:, oX:oX + 128],
                           p1[0:128, 128:256], w=[(sPE, sPE.n)])
                        rd(b2, "vector", sDV, "tensor_copy", bigB[:, oXT:oXT + 128],
                           p2[0:128, 128:256])
                        bmk[par] = (sDV, sDV.n)
                    else:
                        rd(b1, "vector", sDV, "tensor_scalar",
                           bigB[:, O_TI + kc:O_TI + kc + 128], p1[0:128, 128:256],
                           bcol, None, op0=ALU.mult)
                        bmk[par] = (sDV, sDV.n)

                def b_stageC(ci):
                    par = ci % 2
                    b1, p1, b2, p2, bt, pt_ = BKS[par]
                    uvb, uvp = (bA, psA) if par == 0 else (bN1, psN1)
                    kqb, kqp = (bN2, psN2) if par == 0 else (bS, psS)
                    kc = 128 * ci
                    TIR = (sDV, sDV.n)
                    mm(uvb, uvp[0:128, 0:256], bigB[:, O_TI + kc:O_TI + kc + 128],
                       bigB[:, O_VT + 256 * ci:O_VT + 256 * ci + 256],
                       start=True, stop=True, w=[TIR, TRD])
                    rd(uvb, "scalar", sAC, "copy",
                       bigB[:, O_UV + 256 * ci:O_UV + 256 * ci + 256],
                       uvp[0:128, 0:256])
                    mm(kqb, kqp[0:128, 0:128], bigB[:, O_KB + kc:O_KB + kc + 128],
                       bigB[:, O_QB + kc:O_QB + kc + 128], start=True, stop=False)
                    mm(kqb, kqp[0:128, 0:128],
                       bigB[:, O_KB + 2048 + kc:O_KB + 2048 + kc + 128],
                       bigB[:, O_QB + 2048 + kc:O_QB + 2048 + kc + 128],
                       start=False, stop=True)
                    rd(kqb, "vector", sDV, "tensor_tensor",
                       bigB[:, O_KQ + kc:O_KQ + kc + 128], kqp[0:128, 0:128],
                       c_um[:, :], ALU.mult)

                def b_stagePG(ci):
                    kc = 128 * ci
                    PGR = (sDV, sDV.n)
                    for m in range(2):
                        mm(bB_, psB[0:128, 0:512],
                           bigB[:, O_KB + 2048 * m + kc:O_KB + 2048 * m + kc + 128],
                           bigC[:, pgb + 512 * m:pgb + 512 * m + 512],
                           start=(m == 0), stop=(m == 1), w=([PGR] if m == 0 else ()))
                    rd(bB_, "vector", sDV, "tensor_copy", bigB[:, O_SQ:O_SQ + 256],
                       psB[0:128, 0:256])
                    rd(bB_, "vector", sDV, "tensor_tensor",
                       bigB[:, O_SQ + 256:O_SQ + 512], psB[0:128, 256:512],
                       bigB[:, O_VT + 256 * ci:O_VT + 256 * ci + 256], ALU.subtract)
                    RSB = [(sDV, sDV.n)]
                    mm(bB_, psB[0:128, 0:512], bigB[:, O_TI + kc:O_TI + kc + 128],
                       bigB[:, O_SQ:O_SQ + 512], start=True, stop=True, w=RSB)
                    rd(bB_, "scalar", sAC, "copy", bigB[:, O_SQ:O_SQ + 512],
                       psB[0:128, 0:512])
                    YSB = (sAC, sAC.n)
                    for pt in range(2):
                        mm(bC_, psC[0:128, 0:512],
                           bigB[:, O_KT + 256 * ci + 128 * pt:
                                O_KT + 256 * ci + 128 * pt + 128],
                           bigB[:, O_SQ:O_SQ + 512], start=True, stop=True, w=[YSB])
                        rd(bC_, "vector", sDV, "tensor_tensor",
                           bigC[:, pgb + 512 * pt:pgb + 512 * pt + 512],
                           bigC[:, pgb + 512 * pt:pgb + 512 * pt + 512],
                           psC[0:128, 0:512], ALU.subtract)

                for cp in range(NCH // 2):
                    c0, c1 = 2 * cp, 2 * cp + 1
                    b_stageA(c0)
                    b_stageA(c1)
                    for it in range(6):
                        b_step(c0, it)
                        b_step(c1, it)
                    b_stageC(c0)
                    b_stageC(c1)
                    b_stagePG(c0)
                    b_stagePG(c1)
                ckpt('chunks')
                SPW = [(sDV, sDV.n), (sAC, sAC.n)]
                dma("sync", sLD, sp_q[h], bigB[:, O_QB:O_QB + 4096], w=SPW)
                dma("sync", sLD, sp_kfm[h], bigB[:, O_KB:O_KB + 4096], w=SPW)
                dma("sync", sLD, sp_ktm[h], bigB[:, O_KT:O_KT + 4096], w=SPW)
                dma("sync", sLD, sp_ti[h], bigB[:, O_TI:O_TI + 2048], w=SPW)
                dma("sync", sLD, sp_uv[h], bigB[:, O_UV:O_UV + 4096], w=SPW)
                dma("sync", sLD, sp_kq[h], bigB[:, O_KQ:O_KQ + 2048], w=SPW)
                bufs_free[0] = (sLD, sLD.n)
                ckpt('h%d' % h)

                if h == 0:
                    for k in range(NT):
                        dma("sync", sLD, scrF[:, 0:1408],
                            wguT_sh.rearrange("(a p) o -> a p o", p=128)[k],
                            w=[(sDV, sDV.n)])
                        vop("tensor_scalar", scrF[:, 0:1408], scrF[:, 0:1408],
                            cfs[:, 0:1], 1.25, op0=ALU.mult, op1=ALU.min,
                            w=[(sLD, sLD.n), TSCALE])
                        vop("tensor_scalar", scrF[:, 0:1408], scrF[:, 0:1408],
                            -1.25, MAGIC, op0=ALU.max, op1=ALU.add)
                        vop("tensor_scalar", bigC[:, C_TERN:C_TERN + 1408],
                            scrF[:, 0:1408], -MAGIC, None, op0=ALU.add)
                        kb.op("sync", "dma_start",
                              out=tgu_in[:, k].rearrange("l p c -> p l c"),
                              in_=bigC[:, C_TERN:C_TERN + 1408]
                              .rearrange("p (l c) -> p l c", l=11),
                              w=[(sDV, sDV.n)], inc=[(sLD, 16)])
                    for li in range(2):
                        for k in range(11):
                            dma("sync", sLD,
                                scrF[:, 0:512].rearrange("p (a c) -> p a c", a=4),
                                wdT_sh.rearrange("(a p) o -> p a o", p=128)
                                [:, 4 * k:4 * k + 4, 128 * li:128 * li + 128],
                                w=[(sDV, sDV.n)])
                            vop("tensor_scalar", scrF[:, 0:512], scrF[:, 0:512],
                                cfs[:, 3:4], 1.25, op0=ALU.mult, op1=ALU.min,
                                w=[(sLD, sLD.n), TSCALE])
                            vop("tensor_scalar", scrF[:, 0:512], scrF[:, 0:512],
                                -1.25, MAGIC, op0=ALU.max, op1=ALU.add)
                            vop("tensor_scalar", bigC[:, C_TERN:C_TERN + 512],
                                scrF[:, 0:512], -MAGIC, None, op0=ALU.add)
                            kb.op("sync", "dma_start",
                                  out=twd_in[li, 4 * k:4 * k + 4]
                                  .rearrange("a p c -> p a c"),
                                  in_=bigC[:, C_TERN:C_TERN + 512]
                                  .rearrange("p (a c) -> p a c", a=4),
                                  w=[(sDV, sDV.n)], inc=[(sLD, 16)])
                    if dump:
                        dbg_cf = nc.dram_tensor("dbg_cf", [128, 8], F32,
                                                kind="ExternalOutput").ap()
                        dbg_scr = nc.dram_tensor("dbg_scr", [128, 512], F32,
                                                 kind="ExternalOutput").ap()
                        dbg_bc = nc.dram_tensor("dbg_bc", [128, 512], BF,
                                                kind="ExternalOutput").ap()
                        dma("sync", sLD, dbg_cf[:, 0:4], cfs[:, 0:4],
                            w=[(sDV, sDV.n)])
                        dma("sync", sLD, dbg_cf[:, 4:8], cfm[:, 0:4],
                            w=[(sDV, sDV.n)])
                        dma("sync", sLD, dbg_scr, scrF[:, 0:512], w=[(sDV, sDV.n)])
                        dma("sync", sLD, dbg_bc, bigC[:, C_TERN:C_TERN + 512],
                            w=[(sDV, sDV.n)])
                    TERN_DONE = (sLD, sLD.n)
                if h == 1:
                    kb.op("gpsimd", "collective_compute", "AllGather", ALU.bypass,
                          [list(range(NCORE))], [tgu_in], [tgu_out],
                          w=[TERN_DONE], inc=[(sCC, 1)])
                    kb.op("gpsimd", "collective_compute", "AllGather", ALU.bypass,
                          [list(range(NCORE))], [twd_in], [twd_out],
                          w=[TERN_DONE], inc=[(sCC, 1)])
                    CC_W = (sCC, sCC.n)
                ckpt2 = 'h%d' % h
                if stop_after == ckpt2:
                    raise _Stop()

            for h in range(H):
                pgb = C_PG + 1024 * h
                sl2 = 1024 * (h % 2)
                W0 = [bufs_free[0], (sDV, sDV.n), (sLD, sLD.n)]
                for a in range(2):
                    bnk = bT1 if a == 0 else bT2
                    for b in range(2):
                        transpose(bnk, bnk.ap[0:128, 128 * b:128 * b + 128],
                                  bigC[:, pgb + 512 * b + 128 * a:
                                       pgb + 512 * b + 128 * a + 128],
                                  c_eye[:, :], w=W0)
                    for b in range(2):
                        rd(bnk, "vector", sDV, "tensor_copy",
                           bigB[:, sl2 + 512 * a + 128 * b:sl2 + 512 * a + 128 * b + 128],
                           bnk.ap[0:128, 128 * b:128 * b + 128])
                    vop("tensor_copy", bigB[:, sl2 + 512 * a + 256:sl2 + 512 * a + 512],
                        bigC[:, pgb + 512 * a + 256:pgb + 512 * a + 512])
                dma("sync", sLD, pg_in[h], bigB[:, sl2:sl2 + 1024], w=[(sDV, sDV.n)])
            kb.op("gpsimd", "collective_compute", "AllGather", ALU.bypass,
                  [list(range(NCORE))], [pg_in], [pg_out],
                  w=[(sLD, sLD.n)], inc=[(sCC, 1)])
            PG_AG = (sCC, sCC.n)
            ckpt('p2')

            D_KF, D_KT, D_QF, D_TI, D_UV, D_KQ = 0, 4096, 8192, 12288, 14336, 18432
            D_RS, D_US, D_OSQ, D_AG = 20480, 20736, 20992, 25088
            D_MSK = 21504
            def d_reload(h):
                RELW = [(sPE, sPE.n), (sDV, sDV.n), (sAC, sAC.n), (sLD, sLD.n)]
                dma("sync", sLD, bigB[:, D_KF:D_KF + 4096], sp_kfm[h], w=RELW)
                dma("sync", sLD, bigB[:, D_KT:D_KT + 4096], sp_ktm[h], w=RELW)
                dma("sync", sLD, bigB[:, D_QF:D_QF + 4096], sp_q[h], w=RELW)
                dma("sync", sLD, bigB[:, D_TI:D_TI + 2048], sp_ti[h], w=RELW)
                dma("sync", sLD, bigB[:, D_UV:D_UV + 4096], sp_uv[h], w=RELW)
                dma("sync", sLD, bigB[:, D_KQ:D_KQ + 2048], sp_kq[h], w=RELW)
                for j in range(7):
                    dma("sync", sLD, bigB[:, D_AG + 1024 * j:D_AG + 1024 * j + 1024],
                        pg_out[j, h], w=RELW + [PG_AG])
                return (sLD, sLD.n)
            MSKC = 16384

            def d_compose(h, RELDh):
                hoff = 512 * (h % 2)
                kb.op("vector", "memset", S0[:, hoff:hoff + 512], 0.0,
                      inc=[(sDV, 1)], w=[(sPE, sPE.n)])
                for j in range(7):
                    base = D_AG + 1024 * j
                    for pt in range(2):
                        vop("tensor_scalar", bigC[:, MSKC + 512 * pt:MSKC + 512 * pt + 256],
                            bigB[:, base + 512 * pt:base + 512 * pt + 256],
                            c_sel[:, j:j + 1], None, op0=ALU.mult,
                            w=[RELDh, (sPE, sPE.n)] if (j == 0 and pt == 0) else ())
                        vop("scalar_tensor_tensor",
                            bigC[:, MSKC + 512 * pt:MSKC + 512 * pt + 256],
                            c_e256[:, 256 * pt:256 * pt + 256], c_sel[:, 8 + j:9 + j],
                            bigC[:, MSKC + 512 * pt:MSKC + 512 * pt + 256],
                            op0=ALU.mult, op1=ALU.add)
                        vop("tensor_scalar",
                            bigC[:, MSKC + 512 * pt + 256:MSKC + 512 * pt + 512],
                            bigB[:, base + 512 * pt + 256:base + 512 * pt + 512],
                            c_sel[:, j:j + 1], None, op0=ALU.mult)
                    MRD = (sDV, sDV.n)
                    for a in range(2):
                        for k2 in range(2):
                            mm(bA, psA[0:128, 256 * a:256 * a + 256],
                               bigC[:, MSKC + 512 * k2 + 128 * a:
                                    MSKC + 512 * k2 + 128 * a + 128],
                               S0[:, hoff + 256 * k2:hoff + 256 * k2 + 256],
                               start=(k2 == 0), stop=(k2 == 1),
                               w=([MRD] if k2 == 0 else ()))
                    for a in range(2):
                        rd(bA, "vector", sDV, "tensor_tensor",
                           S0[:, hoff + 256 * a:hoff + 256 * a + 256],
                           psA[0:128, 256 * a:256 * a + 256],
                           bigC[:, MSKC + 512 * a + 256:MSKC + 512 * a + 512],
                           ALU.add)
                return (sDV, sDV.n)
            RELD = d_reload(0)
            CMARK = d_compose(0, RELD)
            for h in range(H):
                hoff = 512 * (h % 2)

                smark = CMARK
                for ci in range(NCH):
                    kc = 128 * ci
                    for m in range(2):
                        mm(bC_, psC[0:128, 0:256],
                           bigB[:, D_KF + 2048 * m + kc:D_KF + 2048 * m + kc + 128],
                           S0[:, hoff + 256 * m:hoff + 256 * m + 256],
                           start=(m == 0), stop=(m == 1),
                           w=([RELD, CMARK, smark] if m == 0 else ()))
                    rd(bC_, "scalar", sAC, "copy", bigB[:, D_RS:D_RS + 256],
                       psC[0:128, 0:256])
                    mm(bC_, psC[0:128, 256:512], bigB[:, D_TI + kc:D_TI + kc + 128],
                       bigB[:, D_RS:D_RS + 256], start=True, stop=True,
                       w=[(sAC, sAC.n)])
                    rd(bC_, "vector", sDV, "tensor_tensor", bigB[:, D_US:D_US + 256],
                       bigB[:, D_UV + 256 * ci:D_UV + 256 * ci + 256],
                       psC[0:128, 256:512], ALU.subtract)
                    URD = (sDV, sDV.n)
                    for a in range(2):
                        for k2 in range(2):
                            mm(bA, psA[0:128, 128 * a:128 * a + 128],
                               S0[:, hoff + 256 * k2 + 128 * a:
                                  hoff + 256 * k2 + 128 * a + 128],
                               bigB[:, D_QF + 2048 * k2 + kc:D_QF + 2048 * k2 + kc + 128],
                               start=(k2 == 0), stop=False)
                        mm(bA, psA[0:128, 128 * a:128 * a + 128],
                           bigB[:, D_US + 128 * a:D_US + 128 * a + 128],
                           bigB[:, D_KQ + kc:D_KQ + kc + 128], start=False, stop=True,
                           w=[URD])
                    for m in range(2):
                        mm(bB_, psB[0:128, 256 * m:256 * m + 256],
                           bigB[:, D_KT + 256 * ci + 128 * m:
                                D_KT + 256 * ci + 128 * m + 128],
                           bigB[:, D_US:D_US + 256], start=(m == 0), stop=(m == 1),
                           w=([URD] if m == 0 else ()))
                    rd(bB_, "vector", sDV, "tensor_tensor",
                       S0[:, hoff:hoff + 512], S0[:, hoff:hoff + 512],
                       psB[0:128, 0:512], ALU.add)
                    smark = (sDV, sDV.n)
                    for a in range(2):
                        rd(bA, "scalar", sAC, "copy",
                           bigA[:, XW * (2 * h + a) + kc:XW * (2 * h + a) + kc + 128],
                           psA[0:128, 128 * a:128 * a + 128])
                ODONE = (sAC, sAC.n)
                if h + 1 < H:
                    RELD = d_reload(h + 1)
                    CMARK = d_compose(h + 1, RELD)
                vop("tensor_tensor", bigB[:, D_OSQ:D_OSQ + 2048],
                    bigA[:, XW * 2 * h:XW * 2 * h + 2048],
                    bigA[:, XW * 2 * h:XW * 2 * h + 2048], ALU.mult, w=[ODONE])
                vop("tensor_tensor", bigB[:, D_OSQ + 2048:D_OSQ + 4096],
                    bigA[:, XW * (2 * h + 1):XW * (2 * h + 1) + 2048],
                    bigA[:, XW * (2 * h + 1):XW * (2 * h + 1) + 2048], ALU.mult)
                OSQ = (sDV, sDV.n)
                for tb in range(NB):
                    stats_sum(bS, [
                        (bigB[:, D_OSQ + 512 * tb:D_OSQ + 512 * tb + 512], [OSQ]),
                        (bigB[:, D_OSQ + 2048 + 512 * tb:D_OSQ + 2048 + 512 * tb + 512], []),
                    ])
                    rd(bS, "scalar", sAC, "activation", rows[0:1, 64:576],
                       psS[0:1, 0:512], AF.Sqrt, scale=CM(4), bias=CM(2))
                    vop("reciprocal", rows[0:1, 64:576], rows[0:1, 64:576],
                        w=[(sAC, sAC.n)])
                    aop("copy", rows_b[0:1, 4096:4608], rows[0:1, 64:576],
                        w=[(sDV, sDV.n)])
                    bcast(bN2, rows_b[0:1, 4096:4608], w=[(sAC, sAC.n)])
                    for a in range(2):
                        sl = bigA[:, XW * (2 * h + a) + 512 * tb:
                                  XW * (2 * h + a) + 512 * tb + 512]
                        rd(bN2, "vector", sDV, "tensor_tensor", sl, sl,
                           psN2[0:128, 0:512], ALU.mult)
                        vop("tensor_scalar", sl, sl, c_nw2[:, a:a + 1], None,
                            op0=ALU.mult)
            OHAT = (sDV, sDV.n)
            ckpt('p4')

            wo_user = [None, None]
            x_user = [None, None]
            for ot in range(NT):
                slot = ot % 2
                dma("gpsimd", sLG, bigC[:, 2048 * slot:2048 * slot + 2048]
                    .rearrange("p (d o) -> p d o", d=NT),
                    wo[ot].rearrange("(d p) o -> p d o", p=128),
                    w=[wo_user[slot]] +
                    ([(sDV, sDV.n), (sPE, sPE.n), (sLD, sLD.n)] if ot < 2 else []))
                WOL = (sLG, sLG.n)
                for tb in range(NB):
                    xslot = (ot * NB + tb) % 2
                    dma("sync", sLD, scrF[:, 512 * xslot:512 * xslot + 512],
                        xT.rearrange("(d p) t -> p d t", p=128)
                        [:, ot, 512 * tb:512 * tb + 512],
                        w=[x_user[xslot]] +
                        ([(sDV, sDV.n)] if (ot == 0 and tb < 2) else []))
                    XR = (sLD, sLD.n)
                    bank = bA if (ot * NB + tb) % 2 == 0 else bB_
                    for d in range(NT):
                        mm(bank, bank.ap[0:128, 0:512],
                           bigC[:, 2048 * slot + 128 * d:2048 * slot + 128 * d + 128],
                           bigA[:, XW * d + 512 * tb:XW * d + 512 * tb + 512],
                           start=(d == 0), stop=(d == NT - 1),
                           w=([WOL, OHAT] if d == 0 else ()))
                    rd(bank, "vector", sDV, "tensor_tensor",
                       bigB[:, 2048 * ot + 512 * tb:2048 * ot + 512 * tb + 512],
                       bank.ap[0:128, 0:512], scrF[:, 512 * xslot:512 * xslot + 512],
                       ALU.add, w=[XR])
                    x_user[xslot] = (sDV, sDV.n)
                wo_user[slot] = (sPE, sPE.n)
            RDONE = (sDV, sDV.n)
            for tb in range(NB):
                stats_sum(bS, [(bigB[:, 2048 * d + 512 * tb:2048 * d + 512 * tb + 512],
                                [RDONE] if d == 0 else []) for d in range(NT)])
                rd(bS, "vector", sDV, "tensor_scalar", rows[0:1, 64:576],
                   psS[0:1, 0:512], 1.0 / float(D), None, op0=ALU.mult,
                   w=[(sAC, sAC.n)])
                marks = []
                for d in range(NT):
                    sqo = 4096 if d % 2 == 0 else 5120
                    aop("activation", bigC[:, sqo:sqo + 512],
                        bigB[:, 2048 * d + 512 * tb:2048 * d + 512 * tb + 512],
                        AF.Square, w=[RDONE] + ([marks[d - 2]] if d >= 2
                                                else [(sDV, sDV.n)]))
                    mm(bN1, psN1[0:1, 0:512], ones_c[:, :], bigC[:, sqo:sqo + 512],
                       start=(d == 0), stop=(d == NT - 1), w=[(sAC, sAC.n)],
                       inc=([] if d == NT - 1 else [(sPE, 1)]))
                    marks.append((sPE, sPE.n))
                rd(bN1, "vector", sDV, "tensor_scalar", rows[0:1, 576:1088],
                   psN1[0:1, 0:512], 1.0 / float(D), None, op0=ALU.mult)
                vop("tensor_tensor", rows[0:1, 1088:1600], rows[0:1, 64:576],
                    rows[0:1, 64:576], ALU.mult)
                vop("tensor_tensor", rows[0:1, 576:1088], rows[0:1, 576:1088],
                    rows[0:1, 1088:1600], ALU.subtract)
                aop("activation", rows[0:1, 576:1088], rows[0:1, 576:1088], AF.Sqrt,
                    bias=CM(2), scale=CM(6), w=[(sDV, sDV.n)])
                vop("reciprocal", rows[0:1, 576:1088], rows[0:1, 576:1088],
                    w=[(sAC, sAC.n)])
                aop("copy", rows_b[0:1, 4096:4608], rows[0:1, 64:576], w=[(sDV, sDV.n)])
                aop("copy", rows_b[0:1, 4608:5120], rows[0:1, 576:1088])
                LNR = (sAC, sAC.n)
                bcast(bN1, rows_b[0:1, 4096:4608], w=[LNR])
                rd(bN1, "vector", sDV, "tensor_copy", bigC[:, 4096:4608],
                   psN1[0:128, 0:512])
                bcast(bN2, rows_b[0:1, 4608:5120], w=[LNR])
                rd(bN2, "vector", sDV, "tensor_copy", bigC[:, 4608:5120],
                   psN2[0:128, 0:512])
                MB = (sDV, sDV.n)
                for d in range(NT):
                    sl = bigB[:, 2048 * d + 512 * tb:2048 * d + 512 * tb + 512]
                    vop("tensor_tensor", sl, sl, bigC[:, 4096:4608], ALU.subtract,
                        w=[MB])
                    vop("tensor_tensor", sl, sl, bigC[:, 4608:5120], ALU.mult)
                dma("sync", sLD, y1_d[tb].rearrange("p (d t) -> p d t", d=NT),
                    bigB[:, :].rearrange("p (d t) -> p d t", d=NT)
                    [:, :, 512 * tb:512 * tb + 512], w=[(sDV, sDV.n)])
            Y1S = (sLD, sLD.n)
            ckpt('p5')

            for tb in range(NB):
                marks = []
                for d in range(NT):
                    sqo = 512 if d % 2 == 0 else 5248
                    aop("activation", bigC[:, sqo:sqo + 512],
                        bigB[:, 2048 * d + 512 * tb:2048 * d + 512 * tb + 512],
                        AF.Square, w=([marks[d - 2]] if d >= 2
                                      else [(sPE, sPE.n), (sDV, sDV.n)]))
                    mm(bN1, psN1[0:1, 0:512], ones_c[:, :], bigC[:, sqo:sqo + 512],
                       start=(d == 0), stop=(d == NT - 1), w=[(sAC, sAC.n)],
                       inc=([] if d == NT - 1 else [(sPE, 1)]))
                    marks.append((sPE, sPE.n))
                rd(bN1, "scalar", sAC, "activation", rows[0:1, 64:576],
                   psN1[0:1, 0:512], AF.Sqrt, scale=CM(5), bias=CM(2))
                vop("reciprocal", rows[0:1, 64:576], rows[0:1, 64:576], w=[(sAC, sAC.n)])
                aop("copy", rows_b[0:1, 4096:4608], rows[0:1, 64:576], w=[(sDV, sDV.n)])
                bcast(bN2, rows_b[0:1, 4096:4608], w=[(sAC, sAC.n)])
                rd(bN2, "vector", sDV, "tensor_copy", bigC[:, 0:512], psN2[0:128, 0:512])
                IVB = (sDV, sDV.n)
                for d in range(NT):
                    vop("tensor_tensor", bigC[:, 512:1024],
                        bigB[:, 2048 * d + 512 * tb:2048 * d + 512 * tb + 512],
                        bigC[:, 0:512], ALU.mult, w=[IVB])
                    aop("activation", bigA[:, XW * d + 512 * tb:XW * d + 512 * tb + 512],
                        bigC[:, 512:1024], AF.Identity, scale=c_nww[:, d:d + 1],
                        w=[(sDV, sDV.n)])
                    op_init = ALU.max if d == 0 else None
                    xsl = bigA[:, XW * d + 512 * tb:XW * d + 512 * tb + 512]
                    if d == 0:
                        vop("tensor_tensor", bigC[:, 1024 + 512 * tb:1536 + 512 * tb],
                            xsl, xsl, ALU.max, w=[(sAC, sAC.n)])
                        vop("tensor_tensor", bigC[:, 3072 + 512 * tb:3584 + 512 * tb],
                            xsl, xsl, ALU.min)
                    else:
                        vop("tensor_tensor", bigC[:, 1024 + 512 * tb:1536 + 512 * tb],
                            bigC[:, 1024 + 512 * tb:1536 + 512 * tb], xsl, ALU.max,
                            w=[(sAC, sAC.n)])
                        vop("tensor_tensor", bigC[:, 3072 + 512 * tb:3584 + 512 * tb],
                            bigC[:, 3072 + 512 * tb:3584 + 512 * tb], xsl, ALU.min)
            for tb in range(NB):
                vop("tensor_scalar", bigC[:, 3072 + 512 * tb:3584 + 512 * tb],
                    bigC[:, 3072 + 512 * tb:3584 + 512 * tb], -1.0, None, op0=ALU.mult)
                vop("tensor_tensor", bigC[:, 1024 + 512 * tb:1536 + 512 * tb],
                    bigC[:, 1024 + 512 * tb:1536 + 512 * tb],
                    bigC[:, 3072 + 512 * tb:3584 + 512 * tb], ALU.max)
            XNF = (sDV, sDV.n)
            for i in range(NT):
                bnk = bT1 if i % 2 == 0 else bT2
                transpose(bnk, bnk.ap[0:128, 128 * (i % 4 // 2):128 * (i % 4 // 2) + 128],
                          bigC[:, 1024 + 128 * i:1024 + 128 * i + 128], c_eye[:, :],
                          w=[XNF])
                rd(bnk, "vector", sDV, "reduce_max", mxc[:, i:i + 1],
                   bnk.ap[0:128, 128 * (i % 4 // 2):128 * (i % 4 // 2) + 128], AX.X)
            MXC = (sDV, sDV.n)
            transpose(bT1, psT1[0:16, 0:128], mxc[:, 0:16], c_eye[:, :], w=[MXC])
            rd(bT1, "vector", sDV, "tensor_copy", bigC[0:16, 5120:5248],
               psT1[0:16, 0:128])
            kb.op("sync", "dma_start", out=rows_b[0:1, 4096:6144]
                  if False else rows_b[0:1, 0:2048],
                  in_=bigC[0:16, 5120:5248], w=[(sDV, sDV.n)], inc=[(sLD, 16)])
            MAXR = (sLD, sLD.n)
            aop("copy", rows[0:1, 1088:3136], rows_b[0:1, 0:2048], w=[MAXR])
            vop("tensor_scalar", rows[0:1, 1088:3136], rows[0:1, 1088:3136], 1e-5,
                None, op0=ALU.max, w=[(sAC, sAC.n)])
            vop("reciprocal", rows[0:1, 64:2112] if False else rows[0:1, 1088:3136],
                rows[0:1, 1088:3136])
            vop("tensor_scalar", rows[0:1, 1088:3136], rows[0:1, 1088:3136], 127.0,
                None, op0=ALU.mult)
            aop("copy", rows_b[0:1, 0:2048], rows[0:1, 1088:3136], w=[(sDV, sDV.n)])
            S1R = (sAC, sAC.n)
            vop("reciprocal", rows[0:1, 1088:3136], rows[0:1, 1088:3136], w=[S1R])
            vop("tensor_scalar", rows[0:1, 1088:3136], rows[0:1, 1088:3136],
                cfm[0:1, 0:1], None, op0=ALU.mult)
            vop("tensor_tensor", rows_b[0:1, 2048:4096] if False else rows[0:1, 64:1088],
                rows[0:1, 1088:2112], rows[0:1, 1088:2112], ALU.mult)
            vop("tensor_tensor", rows[0:1, 64:1088], rows[0:1, 64:1088],
                rows[0:1, 1088:2112], ALU.mult)
            aop("copy", rows_b[0:1, 2048:3072], rows[0:1, 64:1088], w=[(sDV, sDV.n)])
            vop("tensor_tensor", rows[0:1, 64:1088], rows[0:1, 2112:3136],
                rows[0:1, 2112:3136], ALU.mult, w=[(sAC, sAC.n)])
            vop("tensor_tensor", rows[0:1, 64:1088], rows[0:1, 64:1088],
                rows[0:1, 2112:3136], ALU.mult)
            aop("copy", rows_b[0:1, 3072:4096], rows[0:1, 64:1088], w=[(sDV, sDV.n)])
            C13 = (sAC, sAC.n)
            for tb in range(NB):
                bcast(bN1, rows_b[0:1, 512 * tb:512 * tb + 512], w=[S1R])
                rd(bN1, "vector", sDV, "tensor_copy", bigC[:, 0:512], psN1[0:128, 0:512])
                SB1 = (sDV, sDV.n)
                for d in range(NT):
                    sl = bigA[:, XW * d + 512 * tb:XW * d + 512 * tb + 512]
                    vop("tensor_tensor", bigC[:, 512:1024], sl, bigC[:, 0:512],
                        ALU.mult, w=[SB1])
                    vop("tensor_scalar", bigC[:, 512:1024], bigC[:, 512:1024],
                        MAGIC, -MAGIC, op0=ALU.add, op1=ALU.add)
                    vop("tensor_copy", sl, bigC[:, 512:1024])
            XINT = (sDV, sDV.n)

            F_WP0, F_WP1 = 8192, 12288
            F_G, F_U, F_G2, F_HF, F_F2B, F_G2B, F_HN = 16384, 16896, 17408, 17920, 18432, 18944, 19456
            F_WD0, F_WD1 = 0, 5632
            F_R2S, F_MB, F_IB, F_SQ2 = 11264, 11776, 12288, 12800
            F_Y1 = 20480
            wp_user = [None, None]
            wd_user = [None, None]
            y1r_user = [None]
            of_user = [None, None]
            for tb in range(NB):
                for i in range(44):
                    slot = i % 2
                    wpo = F_WP0 if slot == 0 else F_WP1
                    for gu in range(2):
                        oti = i + 44 * gu
                        sh, loc = oti // 11, oti % 11
                        dma("sync", sLD, bigB[:, wpo + 2048 * gu:wpo + 2048 * gu + 2048]
                            .rearrange("p (d o) -> p d o", d=NT),
                            tgu_out[sh, loc].rearrange("d p o -> p d o"),
                            w=[wp_user[slot], Y1S] +
                            ([CC_W] if (tb == 0 and i < 2) else []))
                    WPL = (sLD, sLD.n)
                    bkg = bA if i % 2 == 0 else bB_
                    bku = bB_ if i % 2 == 0 else bA
                    for d in range(NT):
                        mm(bkg, bkg.ap[0:128, 0:512],
                           bigB[:, wpo + 128 * d:wpo + 128 * d + 128],
                           bigA[:, XW * d + 512 * tb:XW * d + 512 * tb + 512],
                           start=(d == 0), stop=(d == NT - 1),
                           w=([WPL, XINT] if d == 0 else ()))
                    for d in range(NT):
                        mm(bku, bku.ap[0:128, 0:512],
                           bigB[:, wpo + 2048 + 128 * d:wpo + 2048 + 128 * d + 128],
                           bigA[:, XW * d + 512 * tb:XW * d + 512 * tb + 512],
                           start=(d == 0), stop=(d == NT - 1))
                    rd(bkg, "scalar", sAC, "activation", bigB[:, F_G:F_G + 512],
                       bkg.ap[0:128, 0:512], AF.Relu, w=[(sDV, sDV.n)])
                    rd(bku, "scalar", sAC, "copy", bigB[:, F_U:F_U + 512],
                       bku.ap[0:128, 0:512])
                    GUE = (sAC, sAC.n)
                    vop("tensor_tensor", bigB[:, F_G2:F_G2 + 512],
                        bigB[:, F_G:F_G + 512], bigB[:, F_G:F_G + 512], ALU.mult,
                        w=[GUE])
                    vop("tensor_tensor", bigC[:, 512 * i:512 * i + 512],
                        bigB[:, F_G2:F_G2 + 512], bigB[:, F_U:F_U + 512], ALU.mult)
                    hsl = bigC[:, 512 * i:512 * i + 512]
                    if i == 0:
                        vop("tensor_tensor", bigB[:, F_HF:F_HF + 512], hsl, hsl, ALU.max)
                        vop("tensor_tensor", bigB[:, F_HN:F_HN + 512], hsl, hsl, ALU.min)
                    else:
                        vop("tensor_tensor", bigB[:, F_HF:F_HF + 512],
                            bigB[:, F_HF:F_HF + 512], hsl, ALU.max)
                        vop("tensor_tensor", bigB[:, F_HN:F_HN + 512],
                            bigB[:, F_HN:F_HN + 512], hsl, ALU.min)
                    wp_user[slot] = (sDV, sDV.n)
                vop("tensor_scalar", bigB[:, F_HN:F_HN + 512], bigB[:, F_HN:F_HN + 512],
                    -1.0, None, op0=ALU.mult)
                vop("tensor_tensor", bigB[:, F_HF:F_HF + 512], bigB[:, F_HF:F_HF + 512],
                    bigB[:, F_HN:F_HN + 512], ALU.max)
                HFD = (sDV, sDV.n)
                for s in range(4):
                    bnk = bT1 if s % 2 == 0 else bT2
                    transpose(bnk, bnk.ap[0:128, 0:128],
                              bigB[:, F_HF + 128 * s:F_HF + 128 * s + 128], c_eye[:, :],
                              w=[HFD])
                    rd(bnk, "vector", sDV, "reduce_max", mxc[:, s:s + 1],
                       bnk.ap[0:128, 0:128], AX.X)
                transpose(bT1, psT1[0:4, 0:128], mxc[:, 0:4], c_eye[:, :],
                          w=[(sDV, sDV.n)])
                rd(bT1, "vector", sDV, "tensor_copy", bigB[0:4, F_R2S:F_R2S + 128],
                   psT1[0:4, 0:128])
                kb.op("sync", "dma_start", out=rows_b[0:1, 4096:4608],
                      in_=bigB[0:4, F_R2S:F_R2S + 128], w=[(sDV, sDV.n)], inc=[(sLD, 16)])
                aop("copy", rows[0:1, 64:576], rows_b[0:1, 4096:4608],
                    w=[(sLD, sLD.n), (sDV, sDV.n)])
                vop("tensor_tensor", rows[0:1, 64:576],
                    rows_b[0:1, 2048 + 512 * tb:2560 + 512 * tb], rows[0:1, 64:576],
                    ALU.mult, w=[(sAC, sAC.n), C13])
                vop("tensor_scalar", rows[0:1, 64:576], rows[0:1, 64:576], 1e-5, None,
                    op0=ALU.max)
                vop("reciprocal", rows[0:1, 576:1088], rows[0:1, 64:576])
                vop("tensor_tensor", rows[0:1, 576:1088], rows[0:1, 576:1088],
                    rows_b[0:1, 2048 + 512 * tb:2560 + 512 * tb], ALU.mult)
                vop("tensor_scalar", rows[0:1, 576:1088], rows[0:1, 576:1088], 127.0,
                    None, op0=ALU.mult)
                vop("tensor_scalar", rows[0:1, 64:576], rows[0:1, 64:576],
                    cfm[0:1, 3:4], 1.0 / 127.0, op0=ALU.mult, op1=ALU.mult)
                aop("copy", rows_b[0:1, 4096:4608], rows[0:1, 576:1088],
                    w=[(sDV, sDV.n)])
                aop("copy", rows_b[0:1, 4608:5120], rows[0:1, 64:576])
                F2R = (sAC, sAC.n)
                bcast(bN1, rows_b[0:1, 4096:4608], w=[F2R])
                rd(bN1, "vector", sDV, "tensor_copy", bigB[:, F_F2B:F_F2B + 512],
                   psN1[0:128, 0:512])
                F2B = (sDV, sDV.n)
                x2marks = []
                for i in range(44):
                    sl = bigC[:, 512 * i:512 * i + 512]
                    vop("tensor_tensor", bigB[:, F_G2:F_G2 + 512], sl,
                        bigB[:, F_F2B:F_F2B + 512], ALU.mult, w=[F2B])
                    vop("tensor_scalar", sl, bigB[:, F_G2:F_G2 + 512],
                        MAGIC, -MAGIC, op0=ALU.add, op1=ALU.add)
                    x2marks.append((sDV, sDV.n))
                X2Q = (sDV, sDV.n)
                bcast(bN2, rows_b[0:1, 4608:5120], w=[F2R])
                rd(bN2, "vector", sDV, "tensor_copy", bigB[:, F_G2B:F_G2B + 512],
                   psN2[0:128, 0:512])
                G2B = (sDV, sDV.n)
                dma("sync", sLD, bigB[:, F_Y1:F_Y1 + NT * 512], y1_d[tb],
                    w=[y1r_user[0], Y1S])
                Y1R = (sLD, sLD.n)
                for o2 in range(NT):
                    slot = o2 % 2
                    wdo = F_WD0 if slot == 0 else F_WD1
                    sh, loc = o2 // 2, o2 % 2
                    dma("sync", sLD, bigB[:, wdo:wdo + 5632]
                        .rearrange("p (f o) -> p f o", f=44),
                        twd_out[sh, loc].rearrange("f p o -> p f o"),
                        w=[wd_user[slot], XINT] +
                        ([CC_W] if (tb == 0 and o2 < 2) else []))
                    WDL = (sLD, sLD.n)
                    bank = bA if o2 % 2 == 0 else bB_
                    for f in range(44):
                        mm(bank, bank.ap[0:128, 0:512],
                           bigB[:, wdo + 128 * f:wdo + 128 * f + 128],
                           bigC[:, 512 * f:512 * f + 512],
                           start=(f == 0), stop=(f == 43),
                           w=([WDL, x2marks[f]] if o2 < 2 else
                              ([WDL] if f == 0 else ())))
                    rd(bank, "vector", sDV, "tensor_tensor",
                       bigB[:, F_R2S:F_R2S + 512], bank.ap[0:128, 0:512],
                       bigB[:, F_G2B:F_G2B + 512], ALU.mult, w=[G2B, Y1R])
                    vop("tensor_tensor", bigB[:, F_Y1 + 512 * o2:F_Y1 + 512 * o2 + 512],
                        bigB[:, F_Y1 + 512 * o2:F_Y1 + 512 * o2 + 512],
                        bigB[:, F_R2S:F_R2S + 512], ALU.add)
                    wd_user[slot] = (sPE, sPE.n)
                R2D = (sDV, sDV.n)
                stats_sum(bS, [(bigB[:, F_Y1 + 512 * d:F_Y1 + 512 * d + 512],
                                [R2D] if d == 0 else []) for d in range(NT)])
                rd(bS, "vector", sDV, "tensor_scalar", rows[0:1, 64:576],
                   psS[0:1, 0:512], 1.0 / float(D), None, op0=ALU.mult,
                   w=[(sAC, sAC.n)])
                marks = []
                for d in range(NT):
                    sqo = F_SQ2 + 512 * (d % 2)
                    aop("activation", bigB[:, sqo:sqo + 512],
                        bigB[:, F_Y1 + 512 * d:F_Y1 + 512 * d + 512], AF.Square,
                        w=[R2D] + ([marks[d - 2]] if d >= 2 else [(sDV, sDV.n)]))
                    mm(bN1, psN1[0:1, 0:512], ones_c[:, :], bigB[:, sqo:sqo + 512],
                       start=(d == 0), stop=(d == NT - 1), w=[(sAC, sAC.n)],
                       inc=([] if d == NT - 1 else [(sPE, 1)]))
                    marks.append((sPE, sPE.n))
                rd(bN1, "vector", sDV, "tensor_scalar", rows[0:1, 576:1088],
                   psN1[0:1, 0:512], 1.0 / float(D), None, op0=ALU.mult)
                vop("tensor_tensor", rows[0:1, 1088:1600], rows[0:1, 64:576],
                    rows[0:1, 64:576], ALU.mult)
                vop("tensor_tensor", rows[0:1, 576:1088], rows[0:1, 576:1088],
                    rows[0:1, 1088:1600], ALU.subtract)
                aop("activation", rows[0:1, 576:1088], rows[0:1, 576:1088], AF.Sqrt,
                    bias=CM(2), scale=CM(6), w=[(sDV, sDV.n)])
                vop("reciprocal", rows[0:1, 576:1088], rows[0:1, 576:1088],
                    w=[(sAC, sAC.n)])
                aop("copy", rows_b[0:1, 4096:4608], rows[0:1, 64:576], w=[(sDV, sDV.n)])
                aop("copy", rows_b[0:1, 4608:5120], rows[0:1, 576:1088])
                LN2R = (sAC, sAC.n)
                bcast(bN1, rows_b[0:1, 4096:4608], w=[LN2R])
                rd(bN1, "vector", sDV, "tensor_copy", bigB[:, F_MB:F_MB + 512],
                   psN1[0:128, 0:512])
                bcast(bN2, rows_b[0:1, 4608:5120], w=[LN2R])
                rd(bN2, "vector", sDV, "tensor_copy", bigB[:, F_IB:F_IB + 512],
                   psN2[0:128, 0:512])
                MB2 = (sDV, sDV.n)
                for d in range(NT):
                    osl = d % 2
                    ooff = 512 * osl
                    vop("tensor_tensor", scrF[:, ooff:ooff + 512],
                        bigB[:, F_Y1 + 512 * d:F_Y1 + 512 * d + 512],
                        bigB[:, F_MB:F_MB + 512], ALU.subtract,
                        w=[MB2, of_user[osl]])
                    vop("tensor_tensor", scrF[:, ooff:ooff + 512],
                        scrF[:, ooff:ooff + 512], bigB[:, F_IB:F_IB + 512], ALU.mult)
                    kb.op("sync", "dma_start",
                          out=out_d.rearrange("(d p) t -> p d t", p=128)
                          [:, d, 512 * tb:512 * tb + 512],
                          in_=scrF[:, ooff:ooff + 512],
                          w=[(sDV, sDV.n)], inc=[(sLD, 16)])
                    of_user[osl] = (sLD, sLD.n)
                y1r_user[0] = (sDV, sDV.n)

        except _Stop:
            pass
        if dump:
            dbg_y1 = nc.dram_tensor("dbg_y1", [NB, 128, NT * 512], BF,
                                    kind="ExternalOutput").ap()
            dbg_tgu = nc.dram_tensor("dbg_tgu", [11, NT, 128, 128], BF,
                                     kind="ExternalOutput").ap()
            dbg_twd = nc.dram_tensor("dbg_twd", [2, 44, 128, 128], BF,
                                     kind="ExternalOutput").ap()
            dbg_pg = nc.dram_tensor("dbg_pg", [NCORE, 128, 1024], BF,
                                    kind="ExternalOutput").ap()
            dbg_uv = nc.dram_tensor("dbg_uv", [128, NCH * 256], BF,
                                    kind="ExternalOutput").ap()
            dbg_ti = nc.dram_tensor("dbg_ti", [128, NCH * 128], BF,
                                    kind="ExternalOutput").ap()
            dbg_kf = nc.dram_tensor("dbg_kf", [128, 2 * TC], BF,
                                    kind="ExternalOutput").ap()
            WALL = [(sLD, sLD.n), (sLG, sLG.n), (sDV, sDV.n), (sAC, sAC.n),
                    (sPE, sPE.n), (sCC, sCC.n)]
            dma("sync", sLD, dbg_y1, y1_d, w=WALL)
            dma("sync", sLD, dbg_tgu, tgu_out[0], w=WALL)
            dma("sync", sLD, dbg_twd, twd_out[0], w=WALL)
            dma("sync", sLD, dbg_pg, pg_out[:, 0], w=WALL)
            dma("sync", sLD, dbg_uv, sp_uv[0], w=WALL)
            dma("sync", sLD, dbg_ti, sp_ti[0], w=WALL)
            dma("sync", sLD, dbg_kf, sp_kfm[0], w=WALL)
            dbg_ws = nc.dram_tensor("dbg_ws", [1, 16], F32, kind="ExternalOutput").ap()
            dbg_tgin = nc.dram_tensor("dbg_tgin", [11, NT, 128, 128], BF,
                                      kind="ExternalOutput").ap()
            dbg_twin = nc.dram_tensor("dbg_twin", [2, 44, 128, 128], BF,
                                      kind="ExternalOutput").ap()
            dma("sync", sLD, dbg_ws, ws_out, w=WALL)
            dma("sync", sLD, dbg_tgin, tgu_in, w=WALL)
            dma("sync", sLD, dbg_twin, twd_in, w=WALL)
            dbg_rw = nc.dram_tensor("dbg_rw", [1, 56], F32, kind="ExternalOutput").ap()
            dbg_rb = nc.dram_tensor("dbg_rb", [1, 4096], BF, kind="ExternalOutput").ap()
            dma("sync", sLD, dbg_rw, rows[0:1, 0:56], w=WALL)
            dbg_rw2 = nc.dram_tensor("dbg_rw2", [1, 32], F32, kind="ExternalOutput").ap()
            dma("sync", sLD, dbg_rw2, rows[0:1, 2552:2584], w=WALL)
            dma("sync", sLD, dbg_rb, rows_b[0:1, 0:4096], w=WALL)
            dbg_xi = nc.dram_tensor("dbg_xi", [128, 4112], BF, kind="ExternalOutput").ap()
            dbg_x2 = nc.dram_tensor("dbg_x2", [128, 2048], BF, kind="ExternalOutput").ap()
            dma("sync", sLD, dbg_xi, bigA[:, 0:4112], w=WALL)
            dma("sync", sLD, dbg_x2, bigC[:, 0:2048], w=WALL)
        kb.wait("sync", sLD, sLG, sDV, sAC, sPE, sGP, sCC)

        @block.tensor
        def _(e):
            for fn in kb.q["tensor"]:
                fn(e)

        @block.vector
        def _(e):
            for fn in kb.q["vector"]:
                fn(e)

        @block.scalar
        def _(e):
            for fn in kb.q["scalar"]:
                fn(e)

        @block.gpsimd
        def _(e):
            for fn in kb.q["gpsimd"]:
                fn(e)

        @block.sync
        def _(e):
            for fn in kb.q["sync"]:
                fn(e)

    return nc


_NC_CACHE = [None]
_LAST_RES = [None]


def kernel(**inputs):
    x = np.asarray(inputs["L0_attn_hidden_states"], np.float32)[0]
    Wq = np.asarray(inputs["L0_attn_Wq"], np.float32)
    Wk = np.asarray(inputs["L0_attn_Wk"], np.float32)
    Wv = np.asarray(inputs["L0_attn_Wv"], np.float32)
    Wb = np.asarray(inputs["L0_attn_Wb"], np.float32)
    cq = np.asarray(inputs["L0_attn_conv_q"], np.float32)
    ck = np.asarray(inputs["L0_attn_conv_k"], np.float32)
    cv = np.asarray(inputs["L0_attn_conv_v"], np.float32)
    nw = np.asarray(inputs["L0_attn_norm_w"], np.float32)
    Wo = np.asarray(inputs["L0_attn_Wo"], np.float32)
    Wgu = np.asarray(inputs["L0_ffn_w_gate_up"], np.float32)
    Wd = np.asarray(inputs["L0_ffn_w_down"], np.float32)
    nww = np.asarray(inputs["L0_ffn_norm_weight"], np.float32)

    wq3 = np.ascontiguousarray(Wq.reshape(D, H, DK).transpose(1, 0, 2))
    wk3 = np.ascontiguousarray(Wk.reshape(D, H, DK).transpose(1, 0, 2))
    wv3 = np.ascontiguousarray(Wv.reshape(D, H, DK).transpose(1, 0, 2))
    wo16 = np.ascontiguousarray(Wo.reshape(D, NT, 128).transpose(1, 0, 2))
    convw = np.ascontiguousarray(np.stack([cq, ck, cv], 0))
    wguT = np.ascontiguousarray(Wgu.T)
    wdT = np.ascontiguousarray(Wd.T)
    nw2 = np.ascontiguousarray(nw.reshape(2, 128).T)
    nww16 = np.ascontiguousarray(nww.reshape(NT, 128).T)

    eye = np.eye(128, dtype=np.float32)
    c_eye = eye.astype(BF16)
    c_eye2 = (2.0 * eye).astype(BF16)
    c_slm = np.tril(np.ones((128, 128), np.float32), -1).astype(BF16)
    c_um = np.triu(np.ones((128, 128), np.float32)).astype(BF16)
    e256 = np.zeros((128, 512), np.float32)
    e256[:, 0:128] = eye
    e256[:, 256 + 128:512] = eye
    c_e256 = e256.astype(BF16)

    c_misc_h = np.zeros((128, 32), np.float32)
    c_misc_h[:, 8] = 1.0 / float(D * 11264)
    c_misc_h[:, 9] = 1.0 / float(FF * D)
    c_misc_h[:, 10:16] = 1.0
    c_misc_h[:, 16] = 1.0 / float(D * 11264)
    c_misc_h[:, 17] = 1.0 / float(FF * D)
    c_misc_h[:, 18:32] = 1.0
    c_misc_h[:, 0] = 256e-6
    c_misc_h[:, 1] = 1e-6
    c_misc_h[:, 2] = 1e-5
    c_misc_h[:, 3] = 256.0
    c_misc_h[:, 4] = 1.0 / 256.0
    c_misc_h[:, 5] = 1.0 / 2048.0
    c_misc_h[:, 6] = 1.0
    in_maps = []
    for c in range(NCORE):
        lo = c * TC
        sel = np.zeros((128, 16), np.float32)
        sel[:, :c] = 1.0
        sel[:, 8:16] = 1.0 - sel[:, 0:8]
        xh = (np.zeros((D, HALO), np.float32) if c == 0
              else np.ascontiguousarray(x[lo - HALO:lo].T))
        in_maps.append({
            "xT": np.ascontiguousarray(x[lo:lo + TC].T),
            "xhalo": xh,
            "wq": wq3, "wk": wk3, "wv": wv3, "wb": Wb,
            "convw": convw, "wo": wo16,
            "wguT_sh": np.ascontiguousarray(wguT[:, GU_SH * c:GU_SH * (c + 1)]),
            "wdT_sh": np.ascontiguousarray(wdT[:, WD_SH * c:WD_SH * (c + 1)]),
            "nw2": nw2, "nww16": nww16,
            "c_eye_d": c_eye, "c_eye2_d": c_eye2, "c_slm_d": c_slm, "c_um_d": c_um,
            "c_e256_d": c_e256, "c_sel_d": sel, "c_misc_d": c_misc_h,
        })
    if _NC_CACHE[0] is None:
        _NC_CACHE[0] = build()
    nc = _NC_CACHE[0]
    res = run_bass_kernel_spmd(nc, in_maps, core_ids=list(range(NCORE)))
    _LAST_RES[0] = res
    out = np.concatenate(
        [np.asarray(res.results[c]["out"], np.float32).T for c in range(NCORE)], 0)
    return out[None]




# version 11 (final)
# speedup vs baseline: 1.0589x; 1.0042x over previous
import numpy as np
import ml_dtypes
import concourse.bass as bass
import concourse.mybir as mybir
from concourse.bass_utils import run_bass_kernel_spmd
from contextlib import ExitStack

BF16 = ml_dtypes.bfloat16
F32 = mybir.dt.float32
BF = mybir.dt.bfloat16
AF = mybir.ActivationFunctionType
ALU = mybir.AluOpType
AX = mybir.AxisListType
MAGIC = float(np.float32(3 * 2 ** 22))

NCORE = 8
T, D, H, DK = 16384, 2048, 8, 256
TC = T // NCORE
HALO = 8
XW = TC + HALO
PBLK = [(0, 512), (512, 512), (1024, 512), (1536, 512), (2048, 8)]
NCH, CH = 16, 128
FF = 5632
NT = 16
NB = 4
GU_SH = 1408
WD_SH = 256


class Sem:
    def __init__(self, h):
        self.h = h
        self.n = 0


class KB:
    ENGS = ("tensor", "vector", "scalar", "gpsimd", "sync")

    def __init__(self, strict=False):
        self.q = {k: [] for k in self.ENGS}
        self.lastwait = {k: {} for k in self.ENGS}
        self.strict = strict
        self.own = {}

    def op(self, eng, meth, *a, w=(), inc=(), **kw):
        if self.strict and eng in self.own:
            w = list(w) + [self.own[eng]] + ([self.ownq[eng]] if eng in
                                             getattr(self, "ownq", {}) else [])
        wl = []
        for x in w:
            if x is None:
                continue
            s, v = x if isinstance(x, tuple) else (x, x.n)
            if v <= 0:
                continue
            lw = self.lastwait[eng]
            if lw.get(id(s), -1) >= v:
                continue
            lw[id(s)] = v
            wl.append((s.h, v))
        il = []
        for x in inc:
            s, k = x if isinstance(x, tuple) else (x, 1)
            s.n += k
            il.append((s.h, k))

        def fn(e, meth=meth, a=a, kw=kw, wl=wl, il=il):
            for h, v in wl:
                e.wait_ge(h, v)
            inst = getattr(e, meth)(*a, **kw)
            for h, k in il:
                inst.then_inc(h, k)
        self.q[eng].append(fn)

    def wait(self, eng, *sems):
        wl = []
        for x in sems:
            s, v = x if isinstance(x, tuple) else (x, x.n)
            if v <= 0:
                continue
            lw = self.lastwait[eng]
            if lw.get(id(s), -1) >= v:
                continue
            lw[id(s)] = v
            wl.append((s.h, v))

        def fn(e, wl=wl):
            for h, v in wl:
                e.wait_ge(h, v)
        self.q[eng].append(fn)


class Bank:
    def __init__(self, ap):
        self.ap = ap
        self.rd = []
        self.wr = None

    def take(self):
        r, self.rd = self.rd, []
        return r


class _Stop(Exception):
    pass


def build(strict=False, stop_after=None, dump=False):
    nc = bass.Bass("TRN2", target_bir_lowering=False,
                   detect_race_conditions=strict)

    def din(name, shape, dt=F32):
        return nc.dram_tensor(name, shape, dt, kind="ExternalInput").ap()

    xT = din("xT", [D, TC])
    xhalo = din("xhalo", [D, HALO])
    wq = din("wq", [H, D, DK])
    wk = din("wk", [H, D, DK])
    wv = din("wv", [H, D, DK])
    wb = din("wb", [D, H])
    convw = din("convw", [3, D, 4])
    wo = din("wo", [NT, D, 128])
    wguT_sh = din("wguT_sh", [D, GU_SH])
    wdT_sh = din("wdT_sh", [FF, WD_SH])
    nw2_i = din("nw2", [128, 2])
    nww_i = din("nww16", [128, NT])
    c_eye_i = din("c_eye_d", [128, 128], BF)
    c_eye2_i = din("c_eye2_d", [128, 128], BF)
    c_slm_i = din("c_slm_d", [128, 128], BF)
    c_um_i = din("c_um_d", [128, 128], BF)
    c_e256_i = din("c_e256_d", [128, 512], BF)
    c_sel_i = din("c_sel_d", [128, 16])
    c_misc_i = din("c_misc_d", [128, 32])
    out_d = nc.dram_tensor("out", [D, TC], F32, kind="ExternalOutput").ap()

    sp_q = nc.dram_tensor("sp_q", [H, 128, 2 * TC], BF).ap()
    sp_kfm = nc.dram_tensor("sp_kfm", [H, 128, 2 * TC], BF).ap()
    sp_ktm = nc.dram_tensor("sp_ktm", [H, 128, NCH * 256], BF).ap()
    sp_ti = nc.dram_tensor("sp_ti", [H, 128, NCH * 128], BF).ap()
    sp_uv = nc.dram_tensor("sp_uv", [H, 128, NCH * 256], BF).ap()
    sp_kq = nc.dram_tensor("sp_kq", [H, 128, NCH * 128], BF).ap()
    pg_in = nc.dram_tensor("pg_in", [H, 128, 1024], BF).ap()
    pg_out = nc.dram_tensor("pg_out", [NCORE, H, 128, 1024], BF, addr_space="Shared").ap()
    ws_in = nc.dram_tensor("ws_in", [1, 16], F32).ap()
    ws_out = nc.dram_tensor("ws_out", [1, 16], F32, addr_space="Shared").ap()
    tgu_in = nc.dram_tensor("tgu_in", [11, NT, 128, 128], BF).ap()
    tgu_out = nc.dram_tensor("tgu_out", [NCORE, 11, NT, 128, 128], BF, addr_space="Shared").ap()
    twd_in = nc.dram_tensor("twd_in", [2, 44, 128, 128], BF).ap()
    twd_out = nc.dram_tensor("twd_out", [NCORE, 2, 44, 128, 128], BF, addr_space="Shared").ap()
    y1_d = nc.dram_tensor("y1_d", [NB, 128, NT * 512], BF).ap()

    kb = KB(strict=strict)
    with ExitStack() as ctx:
        E = ctx.enter_context
        bigA = E(nc.sbuf_tensor("bigA", [128, NT * XW], BF))
        bigB = E(nc.sbuf_tensor("bigB", [128, 32768], BF))
        bigC = E(nc.sbuf_tensor("bigC", [128, 22528], BF))
        scrF = E(nc.sbuf_tensor("scrF", [128, 1472], F32))
        c_eye = E(nc.sbuf_tensor("c_eye", [128, 128], BF))
        c_eye2 = E(nc.sbuf_tensor("c_eye2", [128, 128], BF))
        c_slm = E(nc.sbuf_tensor("c_slm", [128, 128], BF))
        c_um = E(nc.sbuf_tensor("c_um", [128, 128], BF))
        c_e256 = E(nc.sbuf_tensor("c_e256", [128, 512], BF))
        c_sel = E(nc.sbuf_tensor("c_sel", [128, 16], F32))
        c_misc = E(nc.sbuf_tensor("c_misc", [128, 32], F32))
        c_conv = E(nc.sbuf_tensor("c_conv", [128, 192], F32))
        c_nw2 = E(nc.sbuf_tensor("c_nw2", [128, 2], F32))
        c_nww = E(nc.sbuf_tensor("c_nww", [128, 16], F32))
        ones_c = E(nc.sbuf_tensor("ones_c", [128, 1], BF))
        ones_r = E(nc.sbuf_tensor("ones_r", [1, 128], BF))
        ones_rf = E(nc.sbuf_tensor("ones_rf", [1, 128], F32))
        beta_t = E(nc.sbuf_tensor("beta_t", [128, 128], F32))
        rows = E(nc.sbuf_tensor("rows", [1, 3136], F32))
        rows_b = E(nc.sbuf_tensor("rows_b", [1, 5120], BF))
        mxc = E(nc.sbuf_tensor("mxc", [128, 16], BF))
        colsF = E(nc.sbuf_tensor("colsF", [128, 2], F32))
        cfm = E(nc.sbuf_tensor("cfm", [128, 16], F32))
        cfs = E(nc.sbuf_tensor("cfs", [128, 16], F32))
        S0 = E(nc.sbuf_tensor("S0", [128, 1024], BF))
        psA = E(nc.psum_tensor("psA", [128, 512], F32))
        psB = E(nc.psum_tensor("psB", [128, 512], F32))
        psC = E(nc.psum_tensor("psC", [128, 512], F32))
        psT1 = E(nc.psum_tensor("psT1", [128, 1024], BF))
        psT2 = E(nc.psum_tensor("psT2", [128, 1024], BF))
        psN1 = E(nc.psum_tensor("psN1", [128, 512], F32))
        psN2 = E(nc.psum_tensor("psN2", [128, 512], F32))
        psS = E(nc.psum_tensor("psS", [128, 512], F32))
        sLD = Sem(E(nc.semaphore("sLD")))
        sLG = Sem(E(nc.semaphore("sLG")))
        sPE = Sem(E(nc.semaphore("sPE")))
        sDV = Sem(E(nc.semaphore("sDV")))
        sAC = Sem(E(nc.semaphore("sAC")))
        sGP = Sem(E(nc.semaphore("sGP")))
        sCC = Sem(E(nc.semaphore("sCC")))
        block = E(nc.Block())

        kb.own = {"tensor": sPE, "vector": sDV, "scalar": sAC, "gpsimd": sLG,
                  "sync": sLD}
        kb.ownq = {"gpsimd": sGP}
        bA, bB_, bC_ = Bank(psA), Bank(psB), Bank(psC)
        bT1, bT2 = Bank(psT1), Bank(psT2)
        bN1, bN2, bS = Bank(psN1), Bank(psN2), Bank(psS)

        def ckpt(name):
            if stop_after == name:
                raise _Stop()

        def mm(bank, out_ap, lhsT, rhs, start, stop, w=(), inc=()):
            ww = list(w) + (bank.take() if start else [])
            ii = list(inc) + ([(sPE, 1)] if stop else [])
            kb.op("tensor", "matmul", out_ap, lhsT, rhs, w=ww, inc=ii,
                  start=start, stop=stop)
            if stop:
                bank.wr = (sPE, sPE.n)

        def transpose(bank, out_ap, in_ap, ident, w=()):
            ww = list(w) + bank.take()
            kb.op("tensor", "transpose", out_ap, in_ap, ident, w=ww, inc=[(sPE, 1)])
            bank.wr = (sPE, sPE.n)

        def rd(bank, eng, sem, meth, *a, w=(), inc=(), **kw):
            ww = list(w) + ([bank.wr] if bank.wr else [])
            kb.op(eng, meth, *a, w=ww, inc=list(inc) + [(sem, 1)], **kw)
            bank.rd.append((sem, sem.n))

        def vop(meth, *a, w=(), inc=(), **kw):
            kb.op("vector", meth, *a, w=w, inc=list(inc) + [(sDV, 1)], **kw)

        def aop(meth, *a, w=(), inc=(), **kw):
            kb.op("scalar", meth, *a, w=w, inc=list(inc) + [(sAC, 1)], **kw)

        def dma(eng, sem, dst, src, w=()):
            kb.op(eng, "dma_start", w=w, inc=[(sem, 16)], out=dst, in_=src)

        try:

            dma("sync", sLD, c_eye[:, :], c_eye_i)
            dma("sync", sLD, c_eye2[:, :], c_eye2_i)
            dma("sync", sLD, c_slm[:, :], c_slm_i)
            dma("sync", sLD, c_um[:, :], c_um_i)
            dma("sync", sLD, c_e256[:, :], c_e256_i)
            dma("sync", sLD, c_sel[:, :], c_sel_i)
            dma("sync", sLD, c_misc[:, :], c_misc_i)
            dma("sync", sLD, c_nw2[:, :], nw2_i)
            dma("sync", sLD, c_nww[:, :], nww_i)
            for t3 in range(3):
                dma("sync", sLD,
                    c_conv[:, 64 * t3:64 * t3 + 64].rearrange("p (f j) -> p f j", f=16),
                    convw[t3].rearrange("(f p) j -> p f j", p=128))
            CONSTW = (sLD, sLD.n)

            xv = bigA[:, :].rearrange("p (d c) -> p d c", d=NT)
            dma("gpsimd", sLG, xv[:, :, 0:HALO], xhalo.rearrange("(d p) c -> p d c", p=128))
            for dq in range(4):
                dma("gpsimd", sLG, xv[:, 4 * dq:4 * dq + 4, HALO:XW],
                    xT.rearrange("(d p) t -> p d t", p=128)[:, 4 * dq:4 * dq + 4, :])
            dma("gpsimd", sLG, bigB[:, 32640:32768].rearrange("p (d o) -> p d o", d=NT),
                wb.rearrange("(d p) o -> p d o", p=128))
            XLOAD = (sLG, sLG.n)
            kb.op("gpsimd", "memset", ones_c[:, :], 1.0, inc=[(sGP, 1)])
            kb.op("gpsimd", "memset", ones_r[:, :], 1.0, inc=[(sGP, 1)])
            kb.op("gpsimd", "memset", ones_rf[:, :], 1.0, inc=[(sGP, 1)])
            kb.op("gpsimd", "memset", rows[0:1, 0:16], 0.0, inc=[(sGP, 1)])
            ONES = (sGP, sGP.n)

            for tb in range(NB):
                for d in range(NT):
                    mm(bS, psS[0:8, 0:512], bigB[:, 32640 + 8 * d:32640 + 8 * d + 8],
                       bigA[:, XW * d + HALO + 512 * tb: XW * d + HALO + 512 * tb + 512],
                       start=(d == 0), stop=(d == NT - 1),
                       w=([XLOAD, CONSTW] if d == 0 else ()))
                rd(bS, "scalar", sAC, "activation", bigB[0:8, 512 * tb:512 * tb + 512],
                   psS[0:8, 0:512], AF.Sigmoid)
            BETAF = (sAC, sAC.n)
            for i in range(NCH):
                bnk = bT1 if i % 2 == 0 else bT2
                transpose(bnk, bnk.ap[0:128, 0:8], bigB[0:8, 128 * i:128 * i + 128],
                          c_eye[0:8, 0:8], w=[BETAF])
                rd(bnk, "vector", sDV, "tensor_copy", beta_t[:, 8 * i:8 * i + 8],
                   bnk.ap[0:128, 0:8])
            BETAT = (sDV, sDV.n)
            kb.op("vector", "memset", rows[:, :], 0.0, inc=[(sDV, 1)], w=[ONES])
            kb.op("vector", "memset", rows_b[:, :], 0.0, inc=[(sDV, 1)])
            ckpt('beta')

            accq = bigC
            for k in range(NT):
                dma("sync", sLD, scrF[:, 0:1408],
                    wguT_sh.rearrange("(a p) o -> a p o", p=128)[k], w=[(sDV, sDV.n)])
                vop("reduce_sum", scrF[:, 1408 + k:1409 + k], scrF[:, 0:1408], AX.X,
                    apply_absolute_value=True, w=[(sLD, sLD.n)])
            for k in range(44):
                dma("sync", sLD, scrF[:, 0:256],
                    wdT_sh.rearrange("(a p) o -> a p o", p=128)[k],
                    w=[(sDV, sDV.n)])
                vop("reduce_sum", scrF[:, 1424 + k:1425 + k], scrF[:, 0:256], AX.X,
                    apply_absolute_value=True, w=[(sLD, sLD.n)])
            vop("reduce_sum", colsF[:, 0:1], scrF[:, 1408:1424], AX.X)
            vop("reduce_sum", colsF[:, 1:2], scrF[:, 1424:1468], AX.X)
            kb.op("vector", "memset", scrF[:, 1468:1469], 1.0, inc=[(sDV, 1)])
            mm(bS, psS[0:1, 0:2], scrF[:, 1468:1469], colsF[:, 0:2], start=True, stop=True,
               w=[(sDV, sDV.n)])
            rd(bS, "vector", sDV, "tensor_copy", rows[0:1, 0:2], psS[0:1, 0:2],
               w=[ONES])
            kb.op("sync", "dma_start", out=ws_in, in_=rows[0:1, 0:16],
                  w=[(sDV, sDV.n), (sGP, sGP.n)], inc=[(sLD, 16)])
            kb.op("gpsimd", "collective_compute", "AllReduce", ALU.add,
                  [list(range(NCORE))], [ws_in], [ws_out],
                  w=[(sLD, sLD.n)], inc=[(sCC, 1)])
            dma("sync", sLD, rows[0:1, 16:32], ws_out, w=[(sCC, sCC.n)])
            NGU, NWD = float(D * 11264), float(FF * D)
            vop("tensor_scalar", rows[0:1, 1536:2048], rows[0:1, 0:512],
                1.0 / NGU, 1e-5, op0=ALU.mult, op1=ALU.max,
                w=[(sLD, sLD.n), (sDV, sDV.n)])
            vop("tensor_scalar", rows[0:1, 2048:2560], rows[0:1, 0:512],
                1.0 / NWD, 1e-5, op0=ALU.mult, op1=ALU.max)
            vop("reciprocal", rows[0:1, 2560:3072], rows[0:1, 1536:2048])
            MRDYt = (sDV, sDV.n)
            mm(bS, psS[0:128, 0:2], ones_rf[:, :], rows[0:1, 1552:1554],
               start=True, stop=True, w=[MRDYt, ONES])
            mm(bS, psS[0:128, 2:4], ones_rf[:, :], rows[0:1, 2064:2066],
               start=True, stop=True)
            mm(bS, psS[0:128, 4:6], ones_rf[:, :], rows[0:1, 2576:2578],
               start=True, stop=True)
            PE3 = (sPE, sPE.n)
            vop("reciprocal", rows[0:1, 2560:3072], rows[0:1, 2048:2560], w=[PE3])
            mm(bS, psS[0:128, 6:8], ones_rf[:, :], rows[0:1, 2576:2578],
               start=True, stop=True, w=[(sDV, sDV.n)])
            rd(bS, "vector", sDV, "tensor_copy", cfm[:, 0:2], psS[0:128, 0:2])
            rd(bS, "vector", sDV, "tensor_copy", cfm[:, 2:4], psS[0:128, 2:4])
            rd(bS, "vector", sDV, "tensor_copy", cfs[:, 0:2], psS[0:128, 4:6])
            rd(bS, "vector", sDV, "tensor_copy", cfs[:, 2:4], psS[0:128, 6:8])
            TSCALE = (sDV, sDV.n)
            ckpt('tern1')

            def CM(i):
                return c_misc[0:1, i:i + 1]

            def stats_sum(bank, tiles):
                for i, (ap_, w_) in enumerate(tiles):
                    mm(bank, bank.ap[0:1, 0:512], ones_c[:, :], ap_,
                       start=(i == 0), stop=(i == len(tiles) - 1), w=list(w_))

            def bcast(bank, row_bf, w=()):
                mm(bank, bank.ap[0:128, 0:512], ones_r[:, :], row_bf,
                   start=True, stop=True, w=w)

            O_QB, O_KB, O_KT, O_VT = 0, 4096, 8192, 12288
            O_SQ, O_QP = 16384, 20480
            O_TI, O_UV, O_KQ = 22656, 24704, 28800
            O_M, O_X, O_XT, O_R, O_RT, O_DG = 30848, 30976, 31104, 31232, 31360, 31488
            C_TERN, C_PG = 0, 2816

            BETAPE = (sPE, sPE.n)
            bufs_free = [None]
            wslot_user = [None, None]

            def proj_conv(widx, conv_off, dst_off, half, extraw=()):
                slot = widx % 2
                for j in range(4):
                    vop("tensor_scalar", bigB[:, O_DG + 128 * j:O_DG + 128 * j + 128],
                        c_eye[:, :], c_conv[:, conv_off + j:conv_off + j + 1], None,
                        op0=ALU.mult, w=[CONSTW, (sPE, sPE.n)] + list(extraw))
                DIAG = (sDV, sDV.n)
                for bi, (c0, cw) in enumerate(PBLK):
                    bank = bA if bi % 2 == 0 else bB_
                    for d in range(NT):
                        mm(bank, bank.ap[0:128, 0:cw],
                           bigC[:, 11008 + 2048 * slot + 128 * d:
                                11008 + 2048 * slot + 128 * d + 128],
                           bigA[:, XW * d + c0:XW * d + c0 + cw],
                           start=(d == 0), stop=(d == NT - 1),
                           w=([(sLG, sLG.n), XLOAD] if (bi == 0 and d == 0) else ()))
                    rd(bank, "scalar", sAC, "copy", bigB[:, O_QP + c0:O_QP + c0 + cw],
                       bank.ap[0:128, 0:cw], w=list(extraw))
                QPRE = (sAC, sAC.n)
                for tb in range(NB):
                    for j in range(4):
                        s0 = HALO - 3 + 512 * tb + j
                        mm(bC_, psC[0:128, 0:512],
                           bigB[:, O_DG + 128 * j:O_DG + 128 * j + 128],
                           bigB[:, O_QP + s0:O_QP + s0 + 512],
                           start=(j == 0), stop=(j == 3),
                           w=([QPRE, DIAG] if j == 0 else ()))
                    rd(bC_, "scalar", sAC, "activation",
                       bigB[:, dst_off + 2048 * half + 512 * tb:
                            dst_off + 2048 * half + 512 * tb + 512],
                       psC[0:128, 0:512], AF.Silu, w=list(extraw))
                return (sAC, sAC.n)

            def l2norm(dst_off, sc, bi_):
                vop("tensor_tensor", bigB[:, O_SQ:O_SQ + 4096],
                    bigB[:, dst_off:dst_off + 4096],
                    bigB[:, dst_off:dst_off + 4096], ALU.mult, w=[(sAC, sAC.n)])
                SQ = (sDV, sDV.n)
                for tb in range(NB):
                    stats_sum(bS, [
                        (bigB[:, O_SQ + 512 * tb:O_SQ + 512 * tb + 512], [SQ, ONES]),
                        (bigB[:, O_SQ + 2048 + 512 * tb:O_SQ + 2048 + 512 * tb + 512], []),
                    ])
                    rd(bS, "scalar", sAC, "activation", rows[0:1, 64:576],
                       psS[0:1, 0:512], AF.Sqrt, scale=sc, bias=bi_)
                    vop("reciprocal", rows[0:1, 64:576], rows[0:1, 64:576], w=[(sAC, sAC.n)])
                    aop("copy", rows_b[0:1, 4096:4608], rows[0:1, 64:576], w=[(sDV, sDV.n)])
                    bcast(bN2, rows_b[0:1, 4096:4608], w=[(sAC, sAC.n)])
                    for hf in range(2):
                        sl = bigB[:, dst_off + 2048 * hf + 512 * tb:
                                  dst_off + 2048 * hf + 512 * tb + 512]
                        rd(bN2, "vector", sDV, "tensor_tensor", sl, sl,
                           psN2[0:128, 0:512], ALU.mult)

            TERN_DONE = None
            CC_W = None
            for h in range(H):
                extraw = ([BETAPE] if h == 0 else []) + [bufs_free[0]]
                wsrc = [(wq, 0), (wq, 1), (wk, 0), (wk, 1), (wv, 0), (wv, 1)]

                def wload(widx):
                    wt, hf = wsrc[widx]
                    slot = widx % 2
                    dma("gpsimd", sLG, bigC[:, 11008 + 2048 * slot:11008 + 2048 * slot + 2048]
                        .rearrange("p (d o) -> p d o", d=NT),
                        wt[h][:, 128 * hf:128 * hf + 128]
                        .rearrange("(d p) o -> p d o", p=128),
                        w=[wslot_user[slot], (bufs_free[0] if h > 0 else None)])

                wload(0)
                wload(1)
                dsts = [O_QB, O_QB, O_KB, O_KB, O_KT, O_KT]
                coffs = [8 * h, 8 * h + 4, 64 + 8 * h, 64 + 8 * h + 4,
                         128 + 8 * h, 128 + 8 * h + 4]
                for widx in range(6):
                    proj_conv(widx, coffs[widx], dsts[widx], widx % 2, extraw=extraw)
                    wslot_user[widx % 2] = (sPE, sPE.n)
                    if widx + 2 < 6:
                        wload(widx + 2)
                VDONE = (sAC, sAC.n)
                ckpt('proj0')
                l2norm(O_QB, CM(3), CM(0))
                l2norm(O_KB, CM(6), CM(1))
                KNORM = (sDV, sDV.n)
                ckpt('l2')

                def do_transposes(src_off, dst_off, wfirst):
                    for g in range(8):
                        bnk = bT1 if g % 2 == 0 else bT2
                        for s in range(4):
                            idx = 4 * g + s
                            ci, m = idx // 2, idx % 2
                            transpose(bnk, bnk.ap[0:128, 128 * s:128 * s + 128],
                                      bigB[:, src_off + 2048 * m + 128 * ci:
                                           src_off + 2048 * m + 128 * ci + 128],
                                      c_eye[:, :],
                                      w=([wfirst] if (g == 0 and s == 0) else ()))
                        for s in range(4):
                            idx = 4 * g + s
                            ci, m = idx // 2, idx % 2
                            rd(bnk, "vector", sDV, "tensor_copy",
                               bigB[:, dst_off + 256 * ci + 128 * m:
                                    dst_off + 256 * ci + 128 * m + 128],
                               bnk.ap[0:128, 128 * s:128 * s + 128])
                do_transposes(O_KT, O_VT, VDONE)
                do_transposes(O_KB, O_KT, KNORM)
                TRD = (sDV, sDV.n)
                ckpt('tr')

                pgb = C_PG + 1024 * h
                for pt in range(2):
                    vop("tensor_copy", bigC[:, pgb + 512 * pt:pgb + 512 * pt + 256],
                        c_e256[:, 256 * pt:256 * pt + 256],
                        w=[CONSTW] + ([TERN_DONE] if h == 1 else []))
                    kb.op("vector", "memset",
                          bigC[:, pgb + 512 * pt + 256:pgb + 512 * pt + 512], 0.0,
                          inc=[(sDV, 1)])
                SCR = [(O_M, O_X, O_XT, O_R, O_RT), (32000, 32128, 32256, 32384, 32512)]
                BKS = [(bN1, psN1, bN2, psN2, bT1, psT1),
                       (bS, psS, bA, psA, bT2, psT2)]

                bmk = [None, None]

                def b_stageA(ci):
                    par = ci % 2
                    oM, oX, oXT, oR, oRT = SCR[par]
                    b1, p1, b2, p2, bt, pt_ = BKS[par]
                    kc = 128 * ci
                    bcol = beta_t[:, 8 * ci + h:8 * ci + h + 1]
                    mm(b1, p1[0:128, 0:128], bigB[:, O_KB + kc:O_KB + kc + 128],
                       bigB[:, O_KB + kc:O_KB + kc + 128], start=True, stop=False,
                       w=[KNORM, BETAT])
                    mm(b1, p1[0:128, 0:128],
                       bigB[:, O_KB + 2048 + kc:O_KB + 2048 + kc + 128],
                       bigB[:, O_KB + 2048 + kc:O_KB + 2048 + kc + 128],
                       start=False, stop=True)
                    rd(b1, "vector", sDV, "scalar_tensor_tensor",
                       bigB[:, oM:oM + 128], p1[0:128, 0:128], bcol, c_slm[:, :],
                       op0=ALU.mult, op1=ALU.mult)
                    vop("tensor_tensor", bigB[:, oM:oM + 128], bigB[:, oM:oM + 128],
                        c_eye[:, :], ALU.add)
                    MRDY = (sDV, sDV.n)
                    transpose(bt, pt_[0:128, 0:128], bigB[:, oM:oM + 128],
                              c_eye[:, :], w=[MRDY])
                    rd(bt, "vector", sDV, "tensor_tensor", bigB[:, oX:oX + 128],
                       c_eye2[:, :], pt_[0:128, 0:128], ALU.subtract)
                    vop("tensor_tensor", bigB[:, oXT:oXT + 128], c_eye2[:, :],
                        bigB[:, oM:oM + 128], ALU.subtract)
                    bmk[par] = [(sDV, sDV.n)]

                def b_step(ci, it):
                    par = ci % 2
                    oM, oX, oXT, oR, oRT = SCR[par]
                    b1, p1, b2, p2, bt, pt_ = BKS[par]
                    kc = 128 * ci
                    bcol = beta_t[:, 8 * ci + h:8 * ci + h + 1]
                    last = (it == 5)
                    XRDY = bmk[par] if isinstance(bmk[par], list) else [bmk[par]]
                    mm(b1, p1[0:128, 0:128], bigB[:, oM:oM + 128],
                       bigB[:, oX:oX + 128], start=True, stop=True, w=XRDY)
                    if not last:
                        mm(b2, p2[0:128, 0:128], bigB[:, oX:oX + 128],
                           bigB[:, oM:oM + 128], start=True, stop=True, w=XRDY)
                    rd(b1, "vector", sDV, "tensor_tensor", bigB[:, oR:oR + 128],
                       c_eye2[:, :], p1[0:128, 0:128], ALU.subtract)
                    if not last:
                        rd(b2, "vector", sDV, "tensor_tensor",
                           bigB[:, oRT:oRT + 128], c_eye2[:, :],
                           p2[0:128, 0:128], ALU.subtract)
                    RRDY = (sDV, sDV.n)
                    mm(b1, p1[0:128, 128:256], bigB[:, oXT:oXT + 128],
                       bigB[:, oR:oR + 128], start=True, stop=True, w=[RRDY])
                    if not last:
                        mm(b2, p2[0:128, 128:256], bigB[:, oX:oX + 128],
                           bigB[:, oRT:oRT + 128], start=True, stop=True, w=[RRDY])
                        rd(b1, "scalar", sAC, "copy", bigB[:, oX:oX + 128],
                           p1[0:128, 128:256], w=[(sPE, sPE.n)])
                        rd(b2, "vector", sDV, "tensor_copy", bigB[:, oXT:oXT + 128],
                           p2[0:128, 128:256])
                        bmk[par] = [(sDV, sDV.n), (sAC, sAC.n)]
                    else:
                        rd(b1, "vector", sDV, "tensor_scalar",
                           bigB[:, O_TI + kc:O_TI + kc + 128], p1[0:128, 128:256],
                           bcol, None, op0=ALU.mult)
                        bmk[par] = [(sDV, sDV.n)]

                def b_stageC(ci):
                    par = ci % 2
                    b1, p1, b2, p2, bt, pt_ = BKS[par]
                    uvb, uvp = (bA, psA) if par == 0 else (bN1, psN1)
                    kqb, kqp = (bN2, psN2) if par == 0 else (bS, psS)
                    kc = 128 * ci
                    TIR = (sDV, sDV.n)
                    mm(uvb, uvp[0:128, 0:256], bigB[:, O_TI + kc:O_TI + kc + 128],
                       bigB[:, O_VT + 256 * ci:O_VT + 256 * ci + 256],
                       start=True, stop=True, w=[TIR, TRD])
                    rd(uvb, "scalar", sAC, "copy",
                       bigB[:, O_UV + 256 * ci:O_UV + 256 * ci + 256],
                       uvp[0:128, 0:256])
                    mm(kqb, kqp[0:128, 0:128], bigB[:, O_KB + kc:O_KB + kc + 128],
                       bigB[:, O_QB + kc:O_QB + kc + 128], start=True, stop=False)
                    mm(kqb, kqp[0:128, 0:128],
                       bigB[:, O_KB + 2048 + kc:O_KB + 2048 + kc + 128],
                       bigB[:, O_QB + 2048 + kc:O_QB + 2048 + kc + 128],
                       start=False, stop=True)
                    rd(kqb, "vector", sDV, "tensor_tensor",
                       bigB[:, O_KQ + kc:O_KQ + kc + 128], kqp[0:128, 0:128],
                       c_um[:, :], ALU.mult)

                def b_stagePG(ci):
                    kc = 128 * ci
                    PGR = (sDV, sDV.n)
                    for m in range(2):
                        mm(bB_, psB[0:128, 0:512],
                           bigB[:, O_KB + 2048 * m + kc:O_KB + 2048 * m + kc + 128],
                           bigC[:, pgb + 512 * m:pgb + 512 * m + 512],
                           start=(m == 0), stop=(m == 1), w=([PGR] if m == 0 else ()))
                    rd(bB_, "vector", sDV, "tensor_copy", bigB[:, O_SQ:O_SQ + 256],
                       psB[0:128, 0:256])
                    rd(bB_, "vector", sDV, "tensor_tensor",
                       bigB[:, O_SQ + 256:O_SQ + 512], psB[0:128, 256:512],
                       bigB[:, O_VT + 256 * ci:O_VT + 256 * ci + 256], ALU.subtract)
                    RSB = [(sDV, sDV.n)]
                    mm(bB_, psB[0:128, 0:512], bigB[:, O_TI + kc:O_TI + kc + 128],
                       bigB[:, O_SQ:O_SQ + 512], start=True, stop=True, w=RSB)
                    rd(bB_, "scalar", sAC, "copy", bigB[:, O_SQ:O_SQ + 512],
                       psB[0:128, 0:512])
                    YSB = (sAC, sAC.n)
                    for pt in range(2):
                        mm(bC_, psC[0:128, 0:512],
                           bigB[:, O_KT + 256 * ci + 128 * pt:
                                O_KT + 256 * ci + 128 * pt + 128],
                           bigB[:, O_SQ:O_SQ + 512], start=True, stop=True, w=[YSB])
                        rd(bC_, "vector", sDV, "tensor_tensor",
                           bigC[:, pgb + 512 * pt:pgb + 512 * pt + 512],
                           bigC[:, pgb + 512 * pt:pgb + 512 * pt + 512],
                           psC[0:128, 0:512], ALU.subtract)

                for cp in range(NCH // 2):
                    c0, c1 = 2 * cp, 2 * cp + 1
                    b_stageA(c0)
                    b_stageA(c1)
                    for it in range(6):
                        b_step(c0, it)
                        b_step(c1, it)
                    b_stageC(c0)
                    b_stageC(c1)
                    b_stagePG(c0)
                    b_stagePG(c1)
                ckpt('chunks')
                SPW = [(sDV, sDV.n), (sAC, sAC.n)]
                dma("sync", sLD, sp_q[h], bigB[:, O_QB:O_QB + 4096], w=SPW)
                dma("sync", sLD, sp_kfm[h], bigB[:, O_KB:O_KB + 4096], w=SPW)
                dma("sync", sLD, sp_ktm[h], bigB[:, O_KT:O_KT + 4096], w=SPW)
                dma("sync", sLD, sp_ti[h], bigB[:, O_TI:O_TI + 2048], w=SPW)
                dma("sync", sLD, sp_uv[h], bigB[:, O_UV:O_UV + 4096], w=SPW)
                dma("sync", sLD, sp_kq[h], bigB[:, O_KQ:O_KQ + 2048], w=SPW)
                bufs_free[0] = (sLD, sLD.n)
                ckpt('h%d' % h)

                if h == 0:
                    for k in range(NT):
                        dma("sync", sLD, scrF[:, 0:1408],
                            wguT_sh.rearrange("(a p) o -> a p o", p=128)[k],
                            w=[(sDV, sDV.n)])
                        vop("tensor_scalar", scrF[:, 0:1408], scrF[:, 0:1408],
                            cfs[:, 0:1], 1.25, op0=ALU.mult, op1=ALU.min,
                            w=[(sLD, sLD.n), TSCALE])
                        vop("tensor_scalar", scrF[:, 0:1408], scrF[:, 0:1408],
                            -1.25, MAGIC, op0=ALU.max, op1=ALU.add)
                        vop("tensor_scalar", bigC[:, C_TERN:C_TERN + 1408],
                            scrF[:, 0:1408], -MAGIC, None, op0=ALU.add)
                        kb.op("sync", "dma_start",
                              out=tgu_in[:, k].rearrange("l p c -> p l c"),
                              in_=bigC[:, C_TERN:C_TERN + 1408]
                              .rearrange("p (l c) -> p l c", l=11),
                              w=[(sDV, sDV.n)], inc=[(sLD, 16)])
                    for li in range(2):
                        for k in range(11):
                            dma("sync", sLD,
                                scrF[:, 0:512].rearrange("p (a c) -> p a c", a=4),
                                wdT_sh.rearrange("(a p) o -> p a o", p=128)
                                [:, 4 * k:4 * k + 4, 128 * li:128 * li + 128],
                                w=[(sDV, sDV.n)])
                            vop("tensor_scalar", scrF[:, 0:512], scrF[:, 0:512],
                                cfs[:, 3:4], 1.25, op0=ALU.mult, op1=ALU.min,
                                w=[(sLD, sLD.n), TSCALE])
                            vop("tensor_scalar", scrF[:, 0:512], scrF[:, 0:512],
                                -1.25, MAGIC, op0=ALU.max, op1=ALU.add)
                            vop("tensor_scalar", bigC[:, C_TERN:C_TERN + 512],
                                scrF[:, 0:512], -MAGIC, None, op0=ALU.add)
                            kb.op("sync", "dma_start",
                                  out=twd_in[li, 4 * k:4 * k + 4]
                                  .rearrange("a p c -> p a c"),
                                  in_=bigC[:, C_TERN:C_TERN + 512]
                                  .rearrange("p (a c) -> p a c", a=4),
                                  w=[(sDV, sDV.n)], inc=[(sLD, 16)])
                    if dump:
                        dbg_cf = nc.dram_tensor("dbg_cf", [128, 8], F32,
                                                kind="ExternalOutput").ap()
                        dbg_scr = nc.dram_tensor("dbg_scr", [128, 512], F32,
                                                 kind="ExternalOutput").ap()
                        dbg_bc = nc.dram_tensor("dbg_bc", [128, 512], BF,
                                                kind="ExternalOutput").ap()
                        dma("sync", sLD, dbg_cf[:, 0:4], cfs[:, 0:4],
                            w=[(sDV, sDV.n)])
                        dma("sync", sLD, dbg_cf[:, 4:8], cfm[:, 0:4],
                            w=[(sDV, sDV.n)])
                        dma("sync", sLD, dbg_scr, scrF[:, 0:512], w=[(sDV, sDV.n)])
                        dma("sync", sLD, dbg_bc, bigC[:, C_TERN:C_TERN + 512],
                            w=[(sDV, sDV.n)])
                    TERN_DONE = (sLD, sLD.n)
                if h == 1:
                    kb.op("gpsimd", "collective_compute", "AllGather", ALU.bypass,
                          [list(range(NCORE))], [tgu_in], [tgu_out],
                          w=[TERN_DONE], inc=[(sCC, 1)])
                    kb.op("gpsimd", "collective_compute", "AllGather", ALU.bypass,
                          [list(range(NCORE))], [twd_in], [twd_out],
                          w=[TERN_DONE], inc=[(sCC, 1)])
                    CC_W = (sCC, sCC.n)
                ckpt2 = 'h%d' % h
                if stop_after == ckpt2:
                    raise _Stop()

            for h in range(H):
                pgb = C_PG + 1024 * h
                sl2 = 1024 * (h % 2)
                W0 = [bufs_free[0], (sDV, sDV.n), (sLD, sLD.n)]
                for a in range(2):
                    bnk = bT1 if a == 0 else bT2
                    for b in range(2):
                        transpose(bnk, bnk.ap[0:128, 128 * b:128 * b + 128],
                                  bigC[:, pgb + 512 * b + 128 * a:
                                       pgb + 512 * b + 128 * a + 128],
                                  c_eye[:, :], w=W0)
                    for b in range(2):
                        rd(bnk, "vector", sDV, "tensor_copy",
                           bigB[:, sl2 + 512 * a + 128 * b:sl2 + 512 * a + 128 * b + 128],
                           bnk.ap[0:128, 128 * b:128 * b + 128])
                    vop("tensor_copy", bigB[:, sl2 + 512 * a + 256:sl2 + 512 * a + 512],
                        bigC[:, pgb + 512 * a + 256:pgb + 512 * a + 512])
                dma("sync", sLD, pg_in[h], bigB[:, sl2:sl2 + 1024], w=[(sDV, sDV.n)])
            kb.op("gpsimd", "collective_compute", "AllGather", ALU.bypass,
                  [list(range(NCORE))], [pg_in], [pg_out],
                  w=[(sLD, sLD.n)], inc=[(sCC, 1)])
            PG_AG = (sCC, sCC.n)
            ckpt('p2')

            D_KF, D_KT, D_QF, D_TI, D_UV, D_KQ = 0, 4096, 8192, 12288, 14336, 18432
            D_RS, D_US, D_OSQ, D_AG = 20480, 20736, 20992, 25088
            D_MSK = 21504
            def d_reload(h):
                RELW = [(sPE, sPE.n), (sDV, sDV.n), (sAC, sAC.n), (sLD, sLD.n)]
                dma("sync", sLD, bigB[:, D_KF:D_KF + 4096], sp_kfm[h], w=RELW)
                dma("sync", sLD, bigB[:, D_KT:D_KT + 4096], sp_ktm[h], w=RELW)
                dma("sync", sLD, bigB[:, D_QF:D_QF + 4096], sp_q[h], w=RELW)
                dma("sync", sLD, bigB[:, D_TI:D_TI + 2048], sp_ti[h], w=RELW)
                dma("sync", sLD, bigB[:, D_UV:D_UV + 4096], sp_uv[h], w=RELW)
                dma("sync", sLD, bigB[:, D_KQ:D_KQ + 2048], sp_kq[h], w=RELW)
                for j in range(7):
                    dma("sync", sLD, bigB[:, D_AG + 1024 * j:D_AG + 1024 * j + 1024],
                        pg_out[j, h], w=RELW + [PG_AG])
                return (sLD, sLD.n)
            MSKC = 16384

            def d_compose(h, RELDh):
                hoff = 512 * (h % 2)
                kb.op("vector", "memset", S0[:, hoff:hoff + 512], 0.0,
                      inc=[(sDV, 1)], w=[(sPE, sPE.n)])
                for j in range(7):
                    base = D_AG + 1024 * j
                    for pt in range(2):
                        vop("tensor_scalar", bigC[:, MSKC + 512 * pt:MSKC + 512 * pt + 256],
                            bigB[:, base + 512 * pt:base + 512 * pt + 256],
                            c_sel[:, j:j + 1], None, op0=ALU.mult,
                            w=[RELDh, (sPE, sPE.n)] if (j == 0 and pt == 0) else ())
                        vop("scalar_tensor_tensor",
                            bigC[:, MSKC + 512 * pt:MSKC + 512 * pt + 256],
                            c_e256[:, 256 * pt:256 * pt + 256], c_sel[:, 8 + j:9 + j],
                            bigC[:, MSKC + 512 * pt:MSKC + 512 * pt + 256],
                            op0=ALU.mult, op1=ALU.add)
                        vop("tensor_scalar",
                            bigC[:, MSKC + 512 * pt + 256:MSKC + 512 * pt + 512],
                            bigB[:, base + 512 * pt + 256:base + 512 * pt + 512],
                            c_sel[:, j:j + 1], None, op0=ALU.mult)
                    MRD = (sDV, sDV.n)
                    for a in range(2):
                        for k2 in range(2):
                            mm(bA, psA[0:128, 256 * a:256 * a + 256],
                               bigC[:, MSKC + 512 * k2 + 128 * a:
                                    MSKC + 512 * k2 + 128 * a + 128],
                               S0[:, hoff + 256 * k2:hoff + 256 * k2 + 256],
                               start=(k2 == 0), stop=(k2 == 1),
                               w=([MRD] if k2 == 0 else ()))
                    for a in range(2):
                        rd(bA, "vector", sDV, "tensor_tensor",
                           S0[:, hoff + 256 * a:hoff + 256 * a + 256],
                           psA[0:128, 256 * a:256 * a + 256],
                           bigC[:, MSKC + 512 * a + 256:MSKC + 512 * a + 512],
                           ALU.add)
                return (sDV, sDV.n)
            RELD = d_reload(0)
            CMARK = d_compose(0, RELD)
            for h in range(H):
                hoff = 512 * (h % 2)

                smark = CMARK
                for ci in range(NCH):
                    kc = 128 * ci
                    for m in range(2):
                        mm(bC_, psC[0:128, 0:256],
                           bigB[:, D_KF + 2048 * m + kc:D_KF + 2048 * m + kc + 128],
                           S0[:, hoff + 256 * m:hoff + 256 * m + 256],
                           start=(m == 0), stop=(m == 1),
                           w=([RELD, CMARK, smark] if m == 0 else ()))
                    rd(bC_, "scalar", sAC, "copy", bigB[:, D_RS:D_RS + 256],
                       psC[0:128, 0:256])
                    mm(bC_, psC[0:128, 256:512], bigB[:, D_TI + kc:D_TI + kc + 128],
                       bigB[:, D_RS:D_RS + 256], start=True, stop=True,
                       w=[(sAC, sAC.n)])
                    rd(bC_, "vector", sDV, "tensor_tensor", bigB[:, D_US:D_US + 256],
                       bigB[:, D_UV + 256 * ci:D_UV + 256 * ci + 256],
                       psC[0:128, 256:512], ALU.subtract)
                    URD = (sDV, sDV.n)
                    for a in range(2):
                        for k2 in range(2):
                            mm(bA, psA[0:128, 128 * a:128 * a + 128],
                               S0[:, hoff + 256 * k2 + 128 * a:
                                  hoff + 256 * k2 + 128 * a + 128],
                               bigB[:, D_QF + 2048 * k2 + kc:D_QF + 2048 * k2 + kc + 128],
                               start=(k2 == 0), stop=False)
                        mm(bA, psA[0:128, 128 * a:128 * a + 128],
                           bigB[:, D_US + 128 * a:D_US + 128 * a + 128],
                           bigB[:, D_KQ + kc:D_KQ + kc + 128], start=False, stop=True,
                           w=[URD])
                    for m in range(2):
                        mm(bB_, psB[0:128, 256 * m:256 * m + 256],
                           bigB[:, D_KT + 256 * ci + 128 * m:
                                D_KT + 256 * ci + 128 * m + 128],
                           bigB[:, D_US:D_US + 256], start=(m == 0), stop=(m == 1),
                           w=([URD] if m == 0 else ()))
                    rd(bB_, "vector", sDV, "tensor_tensor",
                       S0[:, hoff:hoff + 512], S0[:, hoff:hoff + 512],
                       psB[0:128, 0:512], ALU.add)
                    smark = (sDV, sDV.n)
                    for a in range(2):
                        rd(bA, "scalar", sAC, "copy",
                           bigA[:, XW * (2 * h + a) + kc:XW * (2 * h + a) + kc + 128],
                           psA[0:128, 128 * a:128 * a + 128])
                ODONE = (sAC, sAC.n)
                if h + 1 < H:
                    RELD = d_reload(h + 1)
                    CMARK = d_compose(h + 1, RELD)
                vop("tensor_tensor", bigB[:, D_OSQ:D_OSQ + 2048],
                    bigA[:, XW * 2 * h:XW * 2 * h + 2048],
                    bigA[:, XW * 2 * h:XW * 2 * h + 2048], ALU.mult, w=[ODONE])
                vop("tensor_tensor", bigB[:, D_OSQ + 2048:D_OSQ + 4096],
                    bigA[:, XW * (2 * h + 1):XW * (2 * h + 1) + 2048],
                    bigA[:, XW * (2 * h + 1):XW * (2 * h + 1) + 2048], ALU.mult)
                OSQ = (sDV, sDV.n)
                for tb in range(NB):
                    stats_sum(bS, [
                        (bigB[:, D_OSQ + 512 * tb:D_OSQ + 512 * tb + 512], [OSQ]),
                        (bigB[:, D_OSQ + 2048 + 512 * tb:D_OSQ + 2048 + 512 * tb + 512], []),
                    ])
                    rd(bS, "scalar", sAC, "activation", rows[0:1, 64:576],
                       psS[0:1, 0:512], AF.Sqrt, scale=CM(4), bias=CM(2))
                    vop("reciprocal", rows[0:1, 64:576], rows[0:1, 64:576],
                        w=[(sAC, sAC.n)])
                    aop("copy", rows_b[0:1, 4096:4608], rows[0:1, 64:576],
                        w=[(sDV, sDV.n)])
                    bcast(bN2, rows_b[0:1, 4096:4608], w=[(sAC, sAC.n)])
                    for a in range(2):
                        sl = bigA[:, XW * (2 * h + a) + 512 * tb:
                                  XW * (2 * h + a) + 512 * tb + 512]
                        rd(bN2, "vector", sDV, "tensor_tensor", sl, sl,
                           psN2[0:128, 0:512], ALU.mult)
                        vop("tensor_scalar", sl, sl, c_nw2[:, a:a + 1], None,
                            op0=ALU.mult)
            OHAT = (sDV, sDV.n)
            ckpt('p4')

            wo_user = [None, None]
            x_user = [None, None]
            for ot in range(NT):
                slot = ot % 2
                dma("gpsimd", sLG, bigC[:, 2048 * slot:2048 * slot + 2048]
                    .rearrange("p (d o) -> p d o", d=NT),
                    wo[ot].rearrange("(d p) o -> p d o", p=128),
                    w=[wo_user[slot]] +
                    ([(sDV, sDV.n), (sPE, sPE.n), (sLD, sLD.n)] if ot < 2 else []))
                WOL = (sLG, sLG.n)
                for tb in range(NB):
                    xslot = (ot * NB + tb) % 2
                    dma("sync", sLD, scrF[:, 512 * xslot:512 * xslot + 512],
                        xT.rearrange("(d p) t -> p d t", p=128)
                        [:, ot, 512 * tb:512 * tb + 512],
                        w=[x_user[xslot]] +
                        ([(sDV, sDV.n)] if (ot == 0 and tb < 2) else []))
                    XR = (sLD, sLD.n)
                    bank = bA if (ot * NB + tb) % 2 == 0 else bB_
                    for d in range(NT):
                        mm(bank, bank.ap[0:128, 0:512],
                           bigC[:, 2048 * slot + 128 * d:2048 * slot + 128 * d + 128],
                           bigA[:, XW * d + 512 * tb:XW * d + 512 * tb + 512],
                           start=(d == 0), stop=(d == NT - 1),
                           w=([WOL, OHAT] if d == 0 else ()))
                    rd(bank, "vector", sDV, "tensor_tensor",
                       bigB[:, 2048 * ot + 512 * tb:2048 * ot + 512 * tb + 512],
                       bank.ap[0:128, 0:512], scrF[:, 512 * xslot:512 * xslot + 512],
                       ALU.add, w=[XR])
                    x_user[xslot] = (sDV, sDV.n)
                wo_user[slot] = (sPE, sPE.n)
            RDONE = (sDV, sDV.n)
            for tb in range(NB):
                stats_sum(bS, [(bigB[:, 2048 * d + 512 * tb:2048 * d + 512 * tb + 512],
                                [RDONE] if d == 0 else []) for d in range(NT)])
                rd(bS, "vector", sDV, "tensor_scalar", rows[0:1, 64:576],
                   psS[0:1, 0:512], 1.0 / float(D), None, op0=ALU.mult,
                   w=[(sAC, sAC.n)])
                marks = []
                for d in range(NT):
                    sqo = 4096 if d % 2 == 0 else 5120
                    aop("activation", bigC[:, sqo:sqo + 512],
                        bigB[:, 2048 * d + 512 * tb:2048 * d + 512 * tb + 512],
                        AF.Square, w=[RDONE] + ([marks[d - 2]] if d >= 2
                                                else [(sDV, sDV.n)]))
                    mm(bN1, psN1[0:1, 0:512], ones_c[:, :], bigC[:, sqo:sqo + 512],
                       start=(d == 0), stop=(d == NT - 1), w=[(sAC, sAC.n)],
                       inc=([] if d == NT - 1 else [(sPE, 1)]))
                    marks.append((sPE, sPE.n))
                rd(bN1, "vector", sDV, "tensor_scalar", rows[0:1, 576:1088],
                   psN1[0:1, 0:512], 1.0 / float(D), None, op0=ALU.mult)
                vop("tensor_tensor", rows[0:1, 1088:1600], rows[0:1, 64:576],
                    rows[0:1, 64:576], ALU.mult)
                vop("tensor_tensor", rows[0:1, 576:1088], rows[0:1, 576:1088],
                    rows[0:1, 1088:1600], ALU.subtract)
                aop("activation", rows[0:1, 576:1088], rows[0:1, 576:1088], AF.Sqrt,
                    bias=CM(2), scale=CM(6), w=[(sDV, sDV.n)])
                vop("reciprocal", rows[0:1, 576:1088], rows[0:1, 576:1088],
                    w=[(sAC, sAC.n)])
                aop("copy", rows_b[0:1, 4096:4608], rows[0:1, 64:576], w=[(sDV, sDV.n)])
                aop("copy", rows_b[0:1, 4608:5120], rows[0:1, 576:1088])
                LNR = (sAC, sAC.n)
                bcast(bN1, rows_b[0:1, 4096:4608], w=[LNR])
                rd(bN1, "vector", sDV, "tensor_copy", bigC[:, 4096:4608],
                   psN1[0:128, 0:512])
                bcast(bN2, rows_b[0:1, 4608:5120], w=[LNR])
                rd(bN2, "vector", sDV, "tensor_copy", bigC[:, 4608:5120],
                   psN2[0:128, 0:512])
                MB = (sDV, sDV.n)
                for d in range(NT):
                    sl = bigB[:, 2048 * d + 512 * tb:2048 * d + 512 * tb + 512]
                    vop("tensor_tensor", sl, sl, bigC[:, 4096:4608], ALU.subtract,
                        w=[MB])
                    vop("tensor_tensor", sl, sl, bigC[:, 4608:5120], ALU.mult)
                dma("sync", sLD, y1_d[tb].rearrange("p (d t) -> p d t", d=NT),
                    bigB[:, :].rearrange("p (d t) -> p d t", d=NT)
                    [:, :, 512 * tb:512 * tb + 512], w=[(sDV, sDV.n)])
            Y1S = (sLD, sLD.n)
            ckpt('p5')

            for tb in range(NB):
                marks = []
                for d in range(NT):
                    sqo = 512 if d % 2 == 0 else 5248
                    aop("activation", bigC[:, sqo:sqo + 512],
                        bigB[:, 2048 * d + 512 * tb:2048 * d + 512 * tb + 512],
                        AF.Square, w=([marks[d - 2]] if d >= 2
                                      else [(sPE, sPE.n), (sDV, sDV.n)]))
                    mm(bN1, psN1[0:1, 0:512], ones_c[:, :], bigC[:, sqo:sqo + 512],
                       start=(d == 0), stop=(d == NT - 1), w=[(sAC, sAC.n)],
                       inc=([] if d == NT - 1 else [(sPE, 1)]))
                    marks.append((sPE, sPE.n))
                rd(bN1, "scalar", sAC, "activation", rows[0:1, 64:576],
                   psN1[0:1, 0:512], AF.Sqrt, scale=CM(5), bias=CM(2))
                vop("reciprocal", rows[0:1, 64:576], rows[0:1, 64:576], w=[(sAC, sAC.n)])
                aop("copy", rows_b[0:1, 4096:4608], rows[0:1, 64:576], w=[(sDV, sDV.n)])
                bcast(bN2, rows_b[0:1, 4096:4608], w=[(sAC, sAC.n)])
                rd(bN2, "vector", sDV, "tensor_copy", bigC[:, 0:512], psN2[0:128, 0:512])
                IVB = (sDV, sDV.n)
                for d in range(NT):
                    vop("tensor_tensor", bigC[:, 512:1024],
                        bigB[:, 2048 * d + 512 * tb:2048 * d + 512 * tb + 512],
                        bigC[:, 0:512], ALU.mult, w=[IVB])
                    aop("activation", bigA[:, XW * d + 512 * tb:XW * d + 512 * tb + 512],
                        bigC[:, 512:1024], AF.Identity, scale=c_nww[:, d:d + 1],
                        w=[(sDV, sDV.n)])
                    op_init = ALU.max if d == 0 else None
                    xsl = bigA[:, XW * d + 512 * tb:XW * d + 512 * tb + 512]
                    if d == 0:
                        vop("tensor_tensor", bigC[:, 1024 + 512 * tb:1536 + 512 * tb],
                            xsl, xsl, ALU.max, w=[(sAC, sAC.n)])
                        vop("tensor_tensor", bigC[:, 3072 + 512 * tb:3584 + 512 * tb],
                            xsl, xsl, ALU.min)
                    else:
                        vop("tensor_tensor", bigC[:, 1024 + 512 * tb:1536 + 512 * tb],
                            bigC[:, 1024 + 512 * tb:1536 + 512 * tb], xsl, ALU.max,
                            w=[(sAC, sAC.n)])
                        vop("tensor_tensor", bigC[:, 3072 + 512 * tb:3584 + 512 * tb],
                            bigC[:, 3072 + 512 * tb:3584 + 512 * tb], xsl, ALU.min)
            for tb in range(NB):
                vop("tensor_scalar", bigC[:, 3072 + 512 * tb:3584 + 512 * tb],
                    bigC[:, 3072 + 512 * tb:3584 + 512 * tb], -1.0, None, op0=ALU.mult)
                vop("tensor_tensor", bigC[:, 1024 + 512 * tb:1536 + 512 * tb],
                    bigC[:, 1024 + 512 * tb:1536 + 512 * tb],
                    bigC[:, 3072 + 512 * tb:3584 + 512 * tb], ALU.max)
            XNF = (sDV, sDV.n)
            for i in range(NT):
                bnk = bT1 if i % 2 == 0 else bT2
                transpose(bnk, bnk.ap[0:128, 128 * (i % 4 // 2):128 * (i % 4 // 2) + 128],
                          bigC[:, 1024 + 128 * i:1024 + 128 * i + 128], c_eye[:, :],
                          w=[XNF])
                rd(bnk, "vector", sDV, "reduce_max", mxc[:, i:i + 1],
                   bnk.ap[0:128, 128 * (i % 4 // 2):128 * (i % 4 // 2) + 128], AX.X)
            MXC = (sDV, sDV.n)
            transpose(bT1, psT1[0:16, 0:128], mxc[:, 0:16], c_eye[:, :], w=[MXC])
            rd(bT1, "vector", sDV, "tensor_copy", bigC[0:16, 5120:5248],
               psT1[0:16, 0:128])
            kb.op("sync", "dma_start", out=rows_b[0:1, 4096:6144]
                  if False else rows_b[0:1, 0:2048],
                  in_=bigC[0:16, 5120:5248], w=[(sDV, sDV.n)], inc=[(sLD, 16)])
            MAXR = (sLD, sLD.n)
            aop("copy", rows[0:1, 1088:3136], rows_b[0:1, 0:2048], w=[MAXR])
            vop("tensor_scalar", rows[0:1, 1088:3136], rows[0:1, 1088:3136], 1e-5,
                None, op0=ALU.max, w=[(sAC, sAC.n)])
            vop("reciprocal", rows[0:1, 64:2112] if False else rows[0:1, 1088:3136],
                rows[0:1, 1088:3136])
            vop("tensor_scalar", rows[0:1, 1088:3136], rows[0:1, 1088:3136], 127.0,
                None, op0=ALU.mult)
            aop("copy", rows_b[0:1, 0:2048], rows[0:1, 1088:3136], w=[(sDV, sDV.n)])
            S1R = (sAC, sAC.n)
            vop("reciprocal", rows[0:1, 1088:3136], rows[0:1, 1088:3136], w=[S1R])
            vop("tensor_scalar", rows[0:1, 1088:3136], rows[0:1, 1088:3136],
                cfm[0:1, 0:1], None, op0=ALU.mult)
            vop("tensor_tensor", rows_b[0:1, 2048:4096] if False else rows[0:1, 64:1088],
                rows[0:1, 1088:2112], rows[0:1, 1088:2112], ALU.mult)
            vop("tensor_tensor", rows[0:1, 64:1088], rows[0:1, 64:1088],
                rows[0:1, 1088:2112], ALU.mult)
            aop("copy", rows_b[0:1, 2048:3072], rows[0:1, 64:1088], w=[(sDV, sDV.n)])
            vop("tensor_tensor", rows[0:1, 64:1088], rows[0:1, 2112:3136],
                rows[0:1, 2112:3136], ALU.mult, w=[(sAC, sAC.n)])
            vop("tensor_tensor", rows[0:1, 64:1088], rows[0:1, 64:1088],
                rows[0:1, 2112:3136], ALU.mult)
            aop("copy", rows_b[0:1, 3072:4096], rows[0:1, 64:1088], w=[(sDV, sDV.n)])
            C13 = (sAC, sAC.n)
            for tb in range(NB):
                bcast(bN1, rows_b[0:1, 512 * tb:512 * tb + 512], w=[S1R])
                rd(bN1, "vector", sDV, "tensor_copy", bigC[:, 0:512], psN1[0:128, 0:512])
                SB1 = (sDV, sDV.n)
                for d in range(NT):
                    sl = bigA[:, XW * d + 512 * tb:XW * d + 512 * tb + 512]
                    vop("tensor_tensor", bigC[:, 512:1024], sl, bigC[:, 0:512],
                        ALU.mult, w=[SB1])
                    vop("tensor_scalar", bigC[:, 512:1024], bigC[:, 512:1024],
                        MAGIC, -MAGIC, op0=ALU.add, op1=ALU.add)
                    vop("tensor_copy", sl, bigC[:, 512:1024])
            XINT = (sDV, sDV.n)

            F_WP0, F_WP1 = 8192, 12288
            F_G, F_U, F_G2, F_HF, F_F2B, F_G2B, F_HN = 16384, 16896, 17408, 17920, 18432, 18944, 19456
            F_WD0, F_WD1 = 0, 5632
            F_R2S, F_MB, F_IB, F_SQ2 = 11264, 11776, 12288, 12800
            F_Y1 = 20480
            wp_user = [None, None]
            wd_user = [None, None]
            y1r_user = [None]
            of_user = [None, None]
            for tb in range(NB):
                for i in range(44):
                    slot = i % 2
                    wpo = F_WP0 if slot == 0 else F_WP1
                    for gu in range(2):
                        oti = i + 44 * gu
                        sh, loc = oti // 11, oti % 11
                        dma("sync", sLD, bigB[:, wpo + 2048 * gu:wpo + 2048 * gu + 2048]
                            .rearrange("p (d o) -> p d o", d=NT),
                            tgu_out[sh, loc].rearrange("d p o -> p d o"),
                            w=[wp_user[slot], Y1S] +
                            ([CC_W] if (tb == 0 and i < 2) else []))
                    WPL = (sLD, sLD.n)
                    bkg = [bA, bN1][i % 2]
                    bku = [bB_, bN2][i % 2]
                    for d in range(NT):
                        mm(bkg, bkg.ap[0:128, 0:512],
                           bigB[:, wpo + 128 * d:wpo + 128 * d + 128],
                           bigA[:, XW * d + 512 * tb:XW * d + 512 * tb + 512],
                           start=(d == 0), stop=(d == NT - 1),
                           w=([WPL, XINT] if d == 0 else ()))
                    for d in range(NT):
                        mm(bku, bku.ap[0:128, 0:512],
                           bigB[:, wpo + 2048 + 128 * d:wpo + 2048 + 128 * d + 128],
                           bigA[:, XW * d + 512 * tb:XW * d + 512 * tb + 512],
                           start=(d == 0), stop=(d == NT - 1))
                    rd(bkg, "scalar", sAC, "activation", bigB[:, F_G:F_G + 512],
                       bkg.ap[0:128, 0:512], AF.Relu, w=[(sDV, sDV.n)])
                    rd(bku, "scalar", sAC, "copy", bigB[:, F_U:F_U + 512],
                       bku.ap[0:128, 0:512])
                    GUE = (sAC, sAC.n)
                    vop("tensor_tensor", bigB[:, F_G2:F_G2 + 512],
                        bigB[:, F_G:F_G + 512], bigB[:, F_G:F_G + 512], ALU.mult,
                        w=[GUE])
                    vop("tensor_tensor", bigC[:, 512 * i:512 * i + 512],
                        bigB[:, F_G2:F_G2 + 512], bigB[:, F_U:F_U + 512], ALU.mult)
                    hsl = bigC[:, 512 * i:512 * i + 512]
                    if i == 0:
                        vop("tensor_tensor", bigB[:, F_HF:F_HF + 512], hsl, hsl, ALU.max)
                        vop("tensor_tensor", bigB[:, F_HN:F_HN + 512], hsl, hsl, ALU.min)
                    else:
                        vop("tensor_tensor", bigB[:, F_HF:F_HF + 512],
                            bigB[:, F_HF:F_HF + 512], hsl, ALU.max)
                        vop("tensor_tensor", bigB[:, F_HN:F_HN + 512],
                            bigB[:, F_HN:F_HN + 512], hsl, ALU.min)
                    wp_user[slot] = (sDV, sDV.n)
                vop("tensor_scalar", bigB[:, F_HN:F_HN + 512], bigB[:, F_HN:F_HN + 512],
                    -1.0, None, op0=ALU.mult)
                vop("tensor_tensor", bigB[:, F_HF:F_HF + 512], bigB[:, F_HF:F_HF + 512],
                    bigB[:, F_HN:F_HN + 512], ALU.max)
                HFD = (sDV, sDV.n)
                for s in range(4):
                    bnk = bT1 if s % 2 == 0 else bT2
                    transpose(bnk, bnk.ap[0:128, 0:128],
                              bigB[:, F_HF + 128 * s:F_HF + 128 * s + 128], c_eye[:, :],
                              w=[HFD])
                    rd(bnk, "vector", sDV, "reduce_max", mxc[:, s:s + 1],
                       bnk.ap[0:128, 0:128], AX.X)
                transpose(bT1, psT1[0:4, 0:128], mxc[:, 0:4], c_eye[:, :],
                          w=[(sDV, sDV.n)])
                rd(bT1, "vector", sDV, "tensor_copy", bigB[0:4, F_R2S:F_R2S + 128],
                   psT1[0:4, 0:128])
                kb.op("sync", "dma_start", out=rows_b[0:1, 4096:4608],
                      in_=bigB[0:4, F_R2S:F_R2S + 128], w=[(sDV, sDV.n)], inc=[(sLD, 16)])
                aop("copy", rows[0:1, 64:576], rows_b[0:1, 4096:4608],
                    w=[(sLD, sLD.n), (sDV, sDV.n)])
                vop("tensor_tensor", rows[0:1, 64:576],
                    rows_b[0:1, 2048 + 512 * tb:2560 + 512 * tb], rows[0:1, 64:576],
                    ALU.mult, w=[(sAC, sAC.n), C13])
                vop("tensor_scalar", rows[0:1, 64:576], rows[0:1, 64:576], 1e-5, None,
                    op0=ALU.max)
                vop("reciprocal", rows[0:1, 576:1088], rows[0:1, 64:576])
                vop("tensor_tensor", rows[0:1, 576:1088], rows[0:1, 576:1088],
                    rows_b[0:1, 2048 + 512 * tb:2560 + 512 * tb], ALU.mult)
                vop("tensor_scalar", rows[0:1, 576:1088], rows[0:1, 576:1088], 127.0,
                    None, op0=ALU.mult)
                vop("tensor_scalar", rows[0:1, 64:576], rows[0:1, 64:576],
                    cfm[0:1, 3:4], 1.0 / 127.0, op0=ALU.mult, op1=ALU.mult)
                aop("copy", rows_b[0:1, 4096:4608], rows[0:1, 576:1088],
                    w=[(sDV, sDV.n)])
                aop("copy", rows_b[0:1, 4608:5120], rows[0:1, 64:576])
                F2R = (sAC, sAC.n)
                bcast(bN1, rows_b[0:1, 4096:4608], w=[F2R])
                rd(bN1, "vector", sDV, "tensor_copy", bigB[:, F_F2B:F_F2B + 512],
                   psN1[0:128, 0:512])
                F2B = (sDV, sDV.n)
                x2marks = []
                for i in range(44):
                    sl = bigC[:, 512 * i:512 * i + 512]
                    vop("tensor_tensor", bigB[:, F_G2:F_G2 + 512], sl,
                        bigB[:, F_F2B:F_F2B + 512], ALU.mult, w=[F2B])
                    vop("tensor_scalar", sl, bigB[:, F_G2:F_G2 + 512],
                        MAGIC, -MAGIC, op0=ALU.add, op1=ALU.add)
                    x2marks.append((sDV, sDV.n))
                X2Q = (sDV, sDV.n)
                bcast(bN2, rows_b[0:1, 4608:5120], w=[F2R])
                rd(bN2, "vector", sDV, "tensor_copy", bigB[:, F_G2B:F_G2B + 512],
                   psN2[0:128, 0:512])
                G2B = (sDV, sDV.n)
                dma("sync", sLD, bigB[:, F_Y1:F_Y1 + NT * 512], y1_d[tb],
                    w=[y1r_user[0], Y1S])
                Y1R = (sLD, sLD.n)
                for o2 in range(NT):
                    slot = o2 % 2
                    wdo = F_WD0 if slot == 0 else F_WD1
                    sh, loc = o2 // 2, o2 % 2
                    dma("sync", sLD, bigB[:, wdo:wdo + 5632]
                        .rearrange("p (f o) -> p f o", f=44),
                        twd_out[sh, loc].rearrange("f p o -> p f o"),
                        w=[wd_user[slot], XINT] +
                        ([CC_W] if (tb == 0 and o2 < 2) else []))
                    WDL = (sLD, sLD.n)
                    bank = [bA, bB_, bN1, bN2][o2 % 4]
                    for f in range(44):
                        mm(bank, bank.ap[0:128, 0:512],
                           bigB[:, wdo + 128 * f:wdo + 128 * f + 128],
                           bigC[:, 512 * f:512 * f + 512],
                           start=(f == 0), stop=(f == 43),
                           w=([WDL, x2marks[f]] if o2 < 2 else
                              ([WDL] if f == 0 else ())))
                    rd(bank, "vector", sDV, "tensor_tensor",
                       bigB[:, F_R2S:F_R2S + 512], bank.ap[0:128, 0:512],
                       bigB[:, F_G2B:F_G2B + 512], ALU.mult, w=[G2B, Y1R])
                    vop("tensor_tensor", bigB[:, F_Y1 + 512 * o2:F_Y1 + 512 * o2 + 512],
                        bigB[:, F_Y1 + 512 * o2:F_Y1 + 512 * o2 + 512],
                        bigB[:, F_R2S:F_R2S + 512], ALU.add)
                    wd_user[slot] = (sPE, sPE.n)
                R2D = (sDV, sDV.n)
                stats_sum(bS, [(bigB[:, F_Y1 + 512 * d:F_Y1 + 512 * d + 512],
                                [R2D] if d == 0 else []) for d in range(NT)])
                rd(bS, "vector", sDV, "tensor_scalar", rows[0:1, 64:576],
                   psS[0:1, 0:512], 1.0 / float(D), None, op0=ALU.mult,
                   w=[(sAC, sAC.n)])
                marks = []
                for d in range(NT):
                    sqo = F_SQ2 + 512 * (d % 2)
                    aop("activation", bigB[:, sqo:sqo + 512],
                        bigB[:, F_Y1 + 512 * d:F_Y1 + 512 * d + 512], AF.Square,
                        w=[R2D] + ([marks[d - 2]] if d >= 2 else [(sDV, sDV.n)]))
                    mm(bN1, psN1[0:1, 0:512], ones_c[:, :], bigB[:, sqo:sqo + 512],
                       start=(d == 0), stop=(d == NT - 1), w=[(sAC, sAC.n)],
                       inc=([] if d == NT - 1 else [(sPE, 1)]))
                    marks.append((sPE, sPE.n))
                rd(bN1, "vector", sDV, "tensor_scalar", rows[0:1, 576:1088],
                   psN1[0:1, 0:512], 1.0 / float(D), None, op0=ALU.mult)
                vop("tensor_tensor", rows[0:1, 1088:1600], rows[0:1, 64:576],
                    rows[0:1, 64:576], ALU.mult)
                vop("tensor_tensor", rows[0:1, 576:1088], rows[0:1, 576:1088],
                    rows[0:1, 1088:1600], ALU.subtract)
                aop("activation", rows[0:1, 576:1088], rows[0:1, 576:1088], AF.Sqrt,
                    bias=CM(2), scale=CM(6), w=[(sDV, sDV.n)])
                vop("reciprocal", rows[0:1, 576:1088], rows[0:1, 576:1088],
                    w=[(sAC, sAC.n)])
                aop("copy", rows_b[0:1, 4096:4608], rows[0:1, 64:576], w=[(sDV, sDV.n)])
                aop("copy", rows_b[0:1, 4608:5120], rows[0:1, 576:1088])
                LN2R = (sAC, sAC.n)
                bcast(bN1, rows_b[0:1, 4096:4608], w=[LN2R])
                rd(bN1, "vector", sDV, "tensor_copy", bigB[:, F_MB:F_MB + 512],
                   psN1[0:128, 0:512])
                bcast(bN2, rows_b[0:1, 4608:5120], w=[LN2R])
                rd(bN2, "vector", sDV, "tensor_copy", bigB[:, F_IB:F_IB + 512],
                   psN2[0:128, 0:512])
                MB2 = (sDV, sDV.n)
                for d in range(NT):
                    osl = d % 2
                    ooff = 512 * osl
                    vop("tensor_tensor", scrF[:, ooff:ooff + 512],
                        bigB[:, F_Y1 + 512 * d:F_Y1 + 512 * d + 512],
                        bigB[:, F_MB:F_MB + 512], ALU.subtract,
                        w=[MB2, of_user[osl]])
                    vop("tensor_tensor", scrF[:, ooff:ooff + 512],
                        scrF[:, ooff:ooff + 512], bigB[:, F_IB:F_IB + 512], ALU.mult)
                    kb.op("sync", "dma_start",
                          out=out_d.rearrange("(d p) t -> p d t", p=128)
                          [:, d, 512 * tb:512 * tb + 512],
                          in_=scrF[:, ooff:ooff + 512],
                          w=[(sDV, sDV.n)], inc=[(sLD, 16)])
                    of_user[osl] = (sLD, sLD.n)
                y1r_user[0] = (sDV, sDV.n)

        except _Stop:
            pass
        if dump:
            dbg_y1 = nc.dram_tensor("dbg_y1", [NB, 128, NT * 512], BF,
                                    kind="ExternalOutput").ap()
            dbg_tgu = nc.dram_tensor("dbg_tgu", [11, NT, 128, 128], BF,
                                     kind="ExternalOutput").ap()
            dbg_twd = nc.dram_tensor("dbg_twd", [2, 44, 128, 128], BF,
                                     kind="ExternalOutput").ap()
            dbg_pg = nc.dram_tensor("dbg_pg", [NCORE, 128, 1024], BF,
                                    kind="ExternalOutput").ap()
            dbg_uv = nc.dram_tensor("dbg_uv", [128, NCH * 256], BF,
                                    kind="ExternalOutput").ap()
            dbg_ti = nc.dram_tensor("dbg_ti", [128, NCH * 128], BF,
                                    kind="ExternalOutput").ap()
            dbg_kf = nc.dram_tensor("dbg_kf", [128, 2 * TC], BF,
                                    kind="ExternalOutput").ap()
            WALL = [(sLD, sLD.n), (sLG, sLG.n), (sDV, sDV.n), (sAC, sAC.n),
                    (sPE, sPE.n), (sCC, sCC.n)]
            dma("sync", sLD, dbg_y1, y1_d, w=WALL)
            dma("sync", sLD, dbg_tgu, tgu_out[0], w=WALL)
            dma("sync", sLD, dbg_twd, twd_out[0], w=WALL)
            dma("sync", sLD, dbg_pg, pg_out[:, 0], w=WALL)
            dma("sync", sLD, dbg_uv, sp_uv[0], w=WALL)
            dma("sync", sLD, dbg_ti, sp_ti[0], w=WALL)
            dma("sync", sLD, dbg_kf, sp_kfm[0], w=WALL)
            dbg_ws = nc.dram_tensor("dbg_ws", [1, 16], F32, kind="ExternalOutput").ap()
            dbg_tgin = nc.dram_tensor("dbg_tgin", [11, NT, 128, 128], BF,
                                      kind="ExternalOutput").ap()
            dbg_twin = nc.dram_tensor("dbg_twin", [2, 44, 128, 128], BF,
                                      kind="ExternalOutput").ap()
            dma("sync", sLD, dbg_ws, ws_out, w=WALL)
            dma("sync", sLD, dbg_tgin, tgu_in, w=WALL)
            dma("sync", sLD, dbg_twin, twd_in, w=WALL)
            dbg_rw = nc.dram_tensor("dbg_rw", [1, 56], F32, kind="ExternalOutput").ap()
            dbg_rb = nc.dram_tensor("dbg_rb", [1, 4096], BF, kind="ExternalOutput").ap()
            dma("sync", sLD, dbg_rw, rows[0:1, 0:56], w=WALL)
            dbg_rw2 = nc.dram_tensor("dbg_rw2", [1, 32], F32, kind="ExternalOutput").ap()
            dma("sync", sLD, dbg_rw2, rows[0:1, 2552:2584], w=WALL)
            dma("sync", sLD, dbg_rb, rows_b[0:1, 0:4096], w=WALL)
            dbg_xi = nc.dram_tensor("dbg_xi", [128, 4112], BF, kind="ExternalOutput").ap()
            dbg_x2 = nc.dram_tensor("dbg_x2", [128, 2048], BF, kind="ExternalOutput").ap()
            dma("sync", sLD, dbg_xi, bigA[:, 0:4112], w=WALL)
            dma("sync", sLD, dbg_x2, bigC[:, 0:2048], w=WALL)
        kb.wait("sync", sLD, sLG, sDV, sAC, sPE, sGP, sCC)

        @block.tensor
        def _(e):
            for fn in kb.q["tensor"]:
                fn(e)

        @block.vector
        def _(e):
            for fn in kb.q["vector"]:
                fn(e)

        @block.scalar
        def _(e):
            for fn in kb.q["scalar"]:
                fn(e)

        @block.gpsimd
        def _(e):
            for fn in kb.q["gpsimd"]:
                fn(e)

        @block.sync
        def _(e):
            for fn in kb.q["sync"]:
                fn(e)

    return nc


_NC_CACHE = [None]
_LAST_RES = [None]


def kernel(**inputs):
    x = np.asarray(inputs["L0_attn_hidden_states"], np.float32)[0]
    Wq = np.asarray(inputs["L0_attn_Wq"], np.float32)
    Wk = np.asarray(inputs["L0_attn_Wk"], np.float32)
    Wv = np.asarray(inputs["L0_attn_Wv"], np.float32)
    Wb = np.asarray(inputs["L0_attn_Wb"], np.float32)
    cq = np.asarray(inputs["L0_attn_conv_q"], np.float32)
    ck = np.asarray(inputs["L0_attn_conv_k"], np.float32)
    cv = np.asarray(inputs["L0_attn_conv_v"], np.float32)
    nw = np.asarray(inputs["L0_attn_norm_w"], np.float32)
    Wo = np.asarray(inputs["L0_attn_Wo"], np.float32)
    Wgu = np.asarray(inputs["L0_ffn_w_gate_up"], np.float32)
    Wd = np.asarray(inputs["L0_ffn_w_down"], np.float32)
    nww = np.asarray(inputs["L0_ffn_norm_weight"], np.float32)

    wq3 = np.ascontiguousarray(Wq.reshape(D, H, DK).transpose(1, 0, 2))
    wk3 = np.ascontiguousarray(Wk.reshape(D, H, DK).transpose(1, 0, 2))
    wv3 = np.ascontiguousarray(Wv.reshape(D, H, DK).transpose(1, 0, 2))
    wo16 = np.ascontiguousarray(Wo.reshape(D, NT, 128).transpose(1, 0, 2))
    convw = np.ascontiguousarray(np.stack([cq, ck, cv], 0))
    wguT = np.ascontiguousarray(Wgu.T)
    wdT = np.ascontiguousarray(Wd.T)
    nw2 = np.ascontiguousarray(nw.reshape(2, 128).T)
    nww16 = np.ascontiguousarray(nww.reshape(NT, 128).T)

    eye = np.eye(128, dtype=np.float32)
    c_eye = eye.astype(BF16)
    c_eye2 = (2.0 * eye).astype(BF16)
    c_slm = np.tril(np.ones((128, 128), np.float32), -1).astype(BF16)
    c_um = np.triu(np.ones((128, 128), np.float32)).astype(BF16)
    e256 = np.zeros((128, 512), np.float32)
    e256[:, 0:128] = eye
    e256[:, 256 + 128:512] = eye
    c_e256 = e256.astype(BF16)

    c_misc_h = np.zeros((128, 32), np.float32)
    c_misc_h[:, 8] = 1.0 / float(D * 11264)
    c_misc_h[:, 9] = 1.0 / float(FF * D)
    c_misc_h[:, 10:16] = 1.0
    c_misc_h[:, 16] = 1.0 / float(D * 11264)
    c_misc_h[:, 17] = 1.0 / float(FF * D)
    c_misc_h[:, 18:32] = 1.0
    c_misc_h[:, 0] = 256e-6
    c_misc_h[:, 1] = 1e-6
    c_misc_h[:, 2] = 1e-5
    c_misc_h[:, 3] = 256.0
    c_misc_h[:, 4] = 1.0 / 256.0
    c_misc_h[:, 5] = 1.0 / 2048.0
    c_misc_h[:, 6] = 1.0
    in_maps = []
    for c in range(NCORE):
        lo = c * TC
        sel = np.zeros((128, 16), np.float32)
        sel[:, :c] = 1.0
        sel[:, 8:16] = 1.0 - sel[:, 0:8]
        xh = (np.zeros((D, HALO), np.float32) if c == 0
              else np.ascontiguousarray(x[lo - HALO:lo].T))
        in_maps.append({
            "xT": np.ascontiguousarray(x[lo:lo + TC].T),
            "xhalo": xh,
            "wq": wq3, "wk": wk3, "wv": wv3, "wb": Wb,
            "convw": convw, "wo": wo16,
            "wguT_sh": np.ascontiguousarray(wguT[:, GU_SH * c:GU_SH * (c + 1)]),
            "wdT_sh": np.ascontiguousarray(wdT[:, WD_SH * c:WD_SH * (c + 1)]),
            "nw2": nw2, "nww16": nww16,
            "c_eye_d": c_eye, "c_eye2_d": c_eye2, "c_slm_d": c_slm, "c_um_d": c_um,
            "c_e256_d": c_e256, "c_sel_d": sel, "c_misc_d": c_misc_h,
        })
    if _NC_CACHE[0] is None:
        _NC_CACHE[0] = build()
    nc = _NC_CACHE[0]
    res = run_bass_kernel_spmd(nc, in_maps, core_ids=list(range(NCORE)))
    _LAST_RES[0] = res
    out = np.concatenate(
        [np.asarray(res.results[c]["out"], np.float32).T for c in range(NCORE)], 0)
    return out[None]




# version 12
# speedup vs baseline: 1.0598x; 1.0009x over previous
import numpy as np
import ml_dtypes
import concourse.bass as bass
import concourse.mybir as mybir
from concourse.bass_utils import run_bass_kernel_spmd
from contextlib import ExitStack

BF16 = ml_dtypes.bfloat16
F32 = mybir.dt.float32
BF = mybir.dt.bfloat16
AF = mybir.ActivationFunctionType
ALU = mybir.AluOpType
AX = mybir.AxisListType
MAGIC = float(np.float32(3 * 2 ** 22))

NCORE = 8
T, D, H, DK = 16384, 2048, 8, 256
TC = T // NCORE
HALO = 8
XW = TC + HALO
PBLK = [(0, 512), (512, 512), (1024, 512), (1536, 512), (2048, 8)]
NCH, CH = 16, 128
FF = 5632
NT = 16
NB = 4
GU_SH = 1408
WD_SH = 256


class Sem:
    def __init__(self, h):
        self.h = h
        self.n = 0


class KB:
    ENGS = ("tensor", "vector", "scalar", "gpsimd", "sync")

    def __init__(self, strict=False):
        self.q = {k: [] for k in self.ENGS}
        self.lastwait = {k: {} for k in self.ENGS}
        self.strict = strict
        self.own = {}

    def op(self, eng, meth, *a, w=(), inc=(), **kw):
        if self.strict and eng in self.own:
            w = list(w) + [self.own[eng]] + ([self.ownq[eng]] if eng in
                                             getattr(self, "ownq", {}) else [])
        wl = []
        for x in w:
            if x is None:
                continue
            s, v = x if isinstance(x, tuple) else (x, x.n)
            if v <= 0:
                continue
            lw = self.lastwait[eng]
            if lw.get(id(s), -1) >= v:
                continue
            lw[id(s)] = v
            wl.append((s.h, v))
        il = []
        for x in inc:
            s, k = x if isinstance(x, tuple) else (x, 1)
            s.n += k
            il.append((s.h, k))

        def fn(e, meth=meth, a=a, kw=kw, wl=wl, il=il):
            for h, v in wl:
                e.wait_ge(h, v)
            inst = getattr(e, meth)(*a, **kw)
            for h, k in il:
                inst.then_inc(h, k)
        self.q[eng].append(fn)

    def wait(self, eng, *sems):
        wl = []
        for x in sems:
            s, v = x if isinstance(x, tuple) else (x, x.n)
            if v <= 0:
                continue
            lw = self.lastwait[eng]
            if lw.get(id(s), -1) >= v:
                continue
            lw[id(s)] = v
            wl.append((s.h, v))

        def fn(e, wl=wl):
            for h, v in wl:
                e.wait_ge(h, v)
        self.q[eng].append(fn)


class Bank:
    def __init__(self, ap):
        self.ap = ap
        self.rd = []
        self.wr = None

    def take(self):
        r, self.rd = self.rd, []
        return r


class _Stop(Exception):
    pass


def build(strict=False, stop_after=None, dump=False):
    nc = bass.Bass("TRN2", target_bir_lowering=False,
                   detect_race_conditions=strict)

    def din(name, shape, dt=F32):
        return nc.dram_tensor(name, shape, dt, kind="ExternalInput").ap()

    xT = din("xT", [D, TC])
    xhalo = din("xhalo", [D, HALO])
    wq = din("wq", [H, D, DK])
    wk = din("wk", [H, D, DK])
    wv = din("wv", [H, D, DK])
    wb = din("wb", [D, H])
    convw = din("convw", [3, D, 4])
    wo = din("wo", [NT, D, 128])
    wguT_sh = din("wguT_sh", [D, GU_SH])
    wdT_sh = din("wdT_sh", [FF, WD_SH])
    nw2_i = din("nw2", [128, 2])
    nww_i = din("nww16", [128, NT])
    c_eye_i = din("c_eye_d", [128, 128], BF)
    c_eye2_i = din("c_eye2_d", [128, 128], BF)
    c_slm_i = din("c_slm_d", [128, 128], BF)
    c_um_i = din("c_um_d", [128, 128], BF)
    c_e256_i = din("c_e256_d", [128, 512], BF)
    c_sel_i = din("c_sel_d", [128, 16])
    c_misc_i = din("c_misc_d", [128, 32])
    out_d = nc.dram_tensor("out", [D, TC], F32, kind="ExternalOutput").ap()

    sp_q = nc.dram_tensor("sp_q", [H, 128, 2 * TC], BF).ap()
    sp_kfm = nc.dram_tensor("sp_kfm", [H, 128, 2 * TC], BF).ap()
    sp_ktm = nc.dram_tensor("sp_ktm", [H, 128, NCH * 256], BF).ap()
    sp_ti = nc.dram_tensor("sp_ti", [H, 128, NCH * 128], BF).ap()
    sp_uv = nc.dram_tensor("sp_uv", [H, 128, NCH * 256], BF).ap()
    sp_kq = nc.dram_tensor("sp_kq", [H, 128, NCH * 128], BF).ap()
    pg_in = nc.dram_tensor("pg_in", [H, 128, 1024], BF).ap()
    pg_out = nc.dram_tensor("pg_out", [NCORE, H, 128, 1024], BF, addr_space="Shared").ap()
    ws_in = nc.dram_tensor("ws_in", [1, 16], F32).ap()
    ws_out = nc.dram_tensor("ws_out", [1, 16], F32, addr_space="Shared").ap()
    tgu_in = nc.dram_tensor("tgu_in", [11, NT, 128, 128], BF).ap()
    tgu_out = nc.dram_tensor("tgu_out", [NCORE, 11, NT, 128, 128], BF, addr_space="Shared").ap()
    twd_in = nc.dram_tensor("twd_in", [2, 44, 128, 128], BF).ap()
    twd_out = nc.dram_tensor("twd_out", [NCORE, 2, 44, 128, 128], BF, addr_space="Shared").ap()
    y1_d = nc.dram_tensor("y1_d", [NB, 128, NT * 512], BF).ap()

    kb = KB(strict=strict)
    with ExitStack() as ctx:
        E = ctx.enter_context
        bigA = E(nc.sbuf_tensor("bigA", [128, NT * XW], BF))
        bigB = E(nc.sbuf_tensor("bigB", [128, 32768], BF))
        bigC = E(nc.sbuf_tensor("bigC", [128, 22528], BF))
        scrF = E(nc.sbuf_tensor("scrF", [128, 1472], F32))
        c_eye = E(nc.sbuf_tensor("c_eye", [128, 128], BF))
        c_eye2 = E(nc.sbuf_tensor("c_eye2", [128, 128], BF))
        c_slm = E(nc.sbuf_tensor("c_slm", [128, 128], BF))
        c_um = E(nc.sbuf_tensor("c_um", [128, 128], BF))
        c_e256 = E(nc.sbuf_tensor("c_e256", [128, 512], BF))
        c_sel = E(nc.sbuf_tensor("c_sel", [128, 16], F32))
        c_misc = E(nc.sbuf_tensor("c_misc", [128, 32], F32))
        c_conv = E(nc.sbuf_tensor("c_conv", [128, 192], F32))
        c_nw2 = E(nc.sbuf_tensor("c_nw2", [128, 2], F32))
        c_nww = E(nc.sbuf_tensor("c_nww", [128, 16], F32))
        ones_c = E(nc.sbuf_tensor("ones_c", [128, 1], BF))
        ones_r = E(nc.sbuf_tensor("ones_r", [1, 128], BF))
        ones_rf = E(nc.sbuf_tensor("ones_rf", [1, 128], F32))
        beta_t = E(nc.sbuf_tensor("beta_t", [128, 128], F32))
        rows = E(nc.sbuf_tensor("rows", [1, 3136], F32))
        rows_b = E(nc.sbuf_tensor("rows_b", [1, 5120], BF))
        mxc = E(nc.sbuf_tensor("mxc", [128, 16], BF))
        colsF = E(nc.sbuf_tensor("colsF", [128, 2], F32))
        cfm = E(nc.sbuf_tensor("cfm", [128, 16], F32))
        cfs = E(nc.sbuf_tensor("cfs", [128, 16], F32))
        S0 = E(nc.sbuf_tensor("S0", [128, 1024], BF))
        psA = E(nc.psum_tensor("psA", [128, 512], F32))
        psB = E(nc.psum_tensor("psB", [128, 512], F32))
        psC = E(nc.psum_tensor("psC", [128, 512], F32))
        psT1 = E(nc.psum_tensor("psT1", [128, 1024], BF))
        psT2 = E(nc.psum_tensor("psT2", [128, 1024], BF))
        psN1 = E(nc.psum_tensor("psN1", [128, 512], F32))
        psN2 = E(nc.psum_tensor("psN2", [128, 512], F32))
        psS = E(nc.psum_tensor("psS", [128, 512], F32))
        sLD = Sem(E(nc.semaphore("sLD")))
        sLG = Sem(E(nc.semaphore("sLG")))
        sPE = Sem(E(nc.semaphore("sPE")))
        sDV = Sem(E(nc.semaphore("sDV")))
        sAC = Sem(E(nc.semaphore("sAC")))
        sGP = Sem(E(nc.semaphore("sGP")))
        sCC = Sem(E(nc.semaphore("sCC")))
        block = E(nc.Block())

        kb.own = {"tensor": sPE, "vector": sDV, "scalar": sAC, "gpsimd": sLG,
                  "sync": sLD}
        kb.ownq = {"gpsimd": sGP}
        bA, bB_, bC_ = Bank(psA), Bank(psB), Bank(psC)
        bT1, bT2 = Bank(psT1), Bank(psT2)
        bN1, bN2, bS = Bank(psN1), Bank(psN2), Bank(psS)

        def ckpt(name):
            if stop_after == name:
                raise _Stop()

        def mm(bank, out_ap, lhsT, rhs, start, stop, w=(), inc=()):
            ww = list(w) + (bank.take() if start else [])
            ii = list(inc) + ([(sPE, 1)] if stop else [])
            kb.op("tensor", "matmul", out_ap, lhsT, rhs, w=ww, inc=ii,
                  start=start, stop=stop)
            if stop:
                bank.wr = (sPE, sPE.n)

        def transpose(bank, out_ap, in_ap, ident, w=()):
            ww = list(w) + bank.take()
            kb.op("tensor", "transpose", out_ap, in_ap, ident, w=ww, inc=[(sPE, 1)])
            bank.wr = (sPE, sPE.n)

        def rd(bank, eng, sem, meth, *a, w=(), inc=(), **kw):
            ww = list(w) + ([bank.wr] if bank.wr else [])
            kb.op(eng, meth, *a, w=ww, inc=list(inc) + [(sem, 1)], **kw)
            bank.rd.append((sem, sem.n))

        def vop(meth, *a, w=(), inc=(), **kw):
            kb.op("vector", meth, *a, w=w, inc=list(inc) + [(sDV, 1)], **kw)

        def aop(meth, *a, w=(), inc=(), **kw):
            kb.op("scalar", meth, *a, w=w, inc=list(inc) + [(sAC, 1)], **kw)

        def dma(eng, sem, dst, src, w=()):
            kb.op(eng, "dma_start", w=w, inc=[(sem, 16)], out=dst, in_=src)

        try:

            dma("sync", sLD, c_eye[:, :], c_eye_i)
            dma("sync", sLD, c_eye2[:, :], c_eye2_i)
            dma("sync", sLD, c_slm[:, :], c_slm_i)
            dma("sync", sLD, c_um[:, :], c_um_i)
            dma("sync", sLD, c_e256[:, :], c_e256_i)
            dma("sync", sLD, c_sel[:, :], c_sel_i)
            dma("sync", sLD, c_misc[:, :], c_misc_i)
            dma("sync", sLD, c_nw2[:, :], nw2_i)
            dma("sync", sLD, c_nww[:, :], nww_i)
            for t3 in range(3):
                dma("sync", sLD,
                    c_conv[:, 64 * t3:64 * t3 + 64].rearrange("p (f j) -> p f j", f=16),
                    convw[t3].rearrange("(f p) j -> p f j", p=128))
            CONSTW = (sLD, sLD.n)

            xv = bigA[:, :].rearrange("p (d c) -> p d c", d=NT)
            for w0 in range(2):
                dma("gpsimd", sLG,
                    bigC[:, 11008 + 2048 * w0:11008 + 2048 * w0 + 2048]
                    .rearrange("p (d o) -> p d o", d=NT),
                    wq[0][:, 128 * w0:128 * w0 + 128]
                    .rearrange("(d p) o -> p d o", p=128))
            W0PRE = (sLG, sLG.n)
            dma("gpsimd", sLG, xv[:, :, 0:HALO], xhalo.rearrange("(d p) c -> p d c", p=128))
            for dq in range(4):
                dma("gpsimd", sLG, xv[:, 4 * dq:4 * dq + 4, HALO:XW],
                    xT.rearrange("(d p) t -> p d t", p=128)[:, 4 * dq:4 * dq + 4, :])
            dma("gpsimd", sLG, bigB[:, 32640:32768].rearrange("p (d o) -> p d o", d=NT),
                wb.rearrange("(d p) o -> p d o", p=128))
            XLOAD = (sLG, sLG.n)
            kb.op("gpsimd", "memset", ones_c[:, :], 1.0, inc=[(sGP, 1)])
            kb.op("gpsimd", "memset", ones_r[:, :], 1.0, inc=[(sGP, 1)])
            kb.op("gpsimd", "memset", ones_rf[:, :], 1.0, inc=[(sGP, 1)])
            kb.op("gpsimd", "memset", rows[0:1, 0:16], 0.0, inc=[(sGP, 1)])
            ONES = (sGP, sGP.n)

            for tb in range(NB):
                for d in range(NT):
                    mm(bS, psS[0:8, 0:512], bigB[:, 32640 + 8 * d:32640 + 8 * d + 8],
                       bigA[:, XW * d + HALO + 512 * tb: XW * d + HALO + 512 * tb + 512],
                       start=(d == 0), stop=(d == NT - 1),
                       w=([XLOAD, CONSTW] if d == 0 else ()))
                rd(bS, "scalar", sAC, "activation", bigB[0:8, 512 * tb:512 * tb + 512],
                   psS[0:8, 0:512], AF.Sigmoid)
            BETAF = (sAC, sAC.n)
            for i in range(NCH):
                bnk = bT1 if i % 2 == 0 else bT2
                transpose(bnk, bnk.ap[0:128, 0:8], bigB[0:8, 128 * i:128 * i + 128],
                          c_eye[0:8, 0:8], w=[BETAF])
                rd(bnk, "vector", sDV, "tensor_copy", beta_t[:, 8 * i:8 * i + 8],
                   bnk.ap[0:128, 0:8])
            BETAT = (sDV, sDV.n)
            kb.op("vector", "memset", rows[:, :], 0.0, inc=[(sDV, 1)], w=[ONES])
            kb.op("vector", "memset", rows_b[:, :], 0.0, inc=[(sDV, 1)])
            ckpt('beta')

            accq = bigC
            for k in range(NT):
                dma("sync", sLD, scrF[:, 0:1408],
                    wguT_sh.rearrange("(a p) o -> a p o", p=128)[k], w=[(sDV, sDV.n)])
                vop("reduce_sum", scrF[:, 1408 + k:1409 + k], scrF[:, 0:1408], AX.X,
                    apply_absolute_value=True, w=[(sLD, sLD.n)])
            for k in range(44):
                dma("sync", sLD, scrF[:, 0:256],
                    wdT_sh.rearrange("(a p) o -> a p o", p=128)[k],
                    w=[(sDV, sDV.n)])
                vop("reduce_sum", scrF[:, 1424 + k:1425 + k], scrF[:, 0:256], AX.X,
                    apply_absolute_value=True, w=[(sLD, sLD.n)])
            vop("reduce_sum", colsF[:, 0:1], scrF[:, 1408:1424], AX.X)
            vop("reduce_sum", colsF[:, 1:2], scrF[:, 1424:1468], AX.X)
            kb.op("vector", "memset", scrF[:, 1468:1469], 1.0, inc=[(sDV, 1)])
            mm(bS, psS[0:1, 0:2], scrF[:, 1468:1469], colsF[:, 0:2], start=True, stop=True,
               w=[(sDV, sDV.n)])
            rd(bS, "vector", sDV, "tensor_copy", rows[0:1, 0:2], psS[0:1, 0:2],
               w=[ONES])
            kb.op("sync", "dma_start", out=ws_in, in_=rows[0:1, 0:16],
                  w=[(sDV, sDV.n), (sGP, sGP.n)], inc=[(sLD, 16)])
            kb.op("gpsimd", "collective_compute", "AllReduce", ALU.add,
                  [list(range(NCORE))], [ws_in], [ws_out],
                  w=[(sLD, sLD.n)], inc=[(sCC, 1)])
            dma("sync", sLD, rows[0:1, 16:32], ws_out, w=[(sCC, sCC.n)])
            NGU, NWD = float(D * 11264), float(FF * D)
            vop("tensor_scalar", rows[0:1, 1536:2048], rows[0:1, 0:512],
                1.0 / NGU, 1e-5, op0=ALU.mult, op1=ALU.max,
                w=[(sLD, sLD.n), (sDV, sDV.n)])
            vop("tensor_scalar", rows[0:1, 2048:2560], rows[0:1, 0:512],
                1.0 / NWD, 1e-5, op0=ALU.mult, op1=ALU.max)
            vop("reciprocal", rows[0:1, 2560:3072], rows[0:1, 1536:2048])
            MRDYt = (sDV, sDV.n)
            mm(bS, psS[0:128, 0:2], ones_rf[:, :], rows[0:1, 1552:1554],
               start=True, stop=True, w=[MRDYt, ONES])
            mm(bS, psS[0:128, 2:4], ones_rf[:, :], rows[0:1, 2064:2066],
               start=True, stop=True)
            mm(bS, psS[0:128, 4:6], ones_rf[:, :], rows[0:1, 2576:2578],
               start=True, stop=True)
            PE3 = (sPE, sPE.n)
            vop("reciprocal", rows[0:1, 2560:3072], rows[0:1, 2048:2560], w=[PE3])
            mm(bS, psS[0:128, 6:8], ones_rf[:, :], rows[0:1, 2576:2578],
               start=True, stop=True, w=[(sDV, sDV.n)])
            rd(bS, "vector", sDV, "tensor_copy", cfm[:, 0:2], psS[0:128, 0:2])
            rd(bS, "vector", sDV, "tensor_copy", cfm[:, 2:4], psS[0:128, 2:4])
            rd(bS, "vector", sDV, "tensor_copy", cfs[:, 0:2], psS[0:128, 4:6])
            rd(bS, "vector", sDV, "tensor_copy", cfs[:, 2:4], psS[0:128, 6:8])
            TSCALE = (sDV, sDV.n)
            ckpt('tern1')

            def CM(i):
                return c_misc[0:1, i:i + 1]

            def stats_sum(bank, tiles):
                for i, (ap_, w_) in enumerate(tiles):
                    mm(bank, bank.ap[0:1, 0:512], ones_c[:, :], ap_,
                       start=(i == 0), stop=(i == len(tiles) - 1), w=list(w_))

            def bcast(bank, row_bf, w=()):
                mm(bank, bank.ap[0:128, 0:512], ones_r[:, :], row_bf,
                   start=True, stop=True, w=w)

            O_QB, O_KB, O_KT, O_VT = 0, 4096, 8192, 12288
            O_SQ, O_QP = 16384, 20480
            O_TI, O_UV, O_KQ = 22656, 24704, 28800
            O_M, O_X, O_XT, O_R, O_RT, O_DG = 30848, 30976, 31104, 31232, 31360, 31488
            C_TERN, C_PG = 0, 2816

            BETAPE = (sPE, sPE.n)
            bufs_free = [None]
            wslot_user = [None, None]

            def proj_conv(widx, conv_off, dst_off, half, extraw=()):
                slot = widx % 2
                for j in range(4):
                    vop("tensor_scalar", bigB[:, O_DG + 128 * j:O_DG + 128 * j + 128],
                        c_eye[:, :], c_conv[:, conv_off + j:conv_off + j + 1], None,
                        op0=ALU.mult, w=[CONSTW, (sPE, sPE.n)] + list(extraw))
                DIAG = (sDV, sDV.n)
                for bi, (c0, cw) in enumerate(PBLK):
                    bank = bA if bi % 2 == 0 else bB_
                    for d in range(NT):
                        mm(bank, bank.ap[0:128, 0:cw],
                           bigC[:, 11008 + 2048 * slot + 128 * d:
                                11008 + 2048 * slot + 128 * d + 128],
                           bigA[:, XW * d + c0:XW * d + c0 + cw],
                           start=(d == 0), stop=(d == NT - 1),
                           w=([(sLG, sLG.n), XLOAD] if (bi == 0 and d == 0) else ()))
                    rd(bank, "scalar", sAC, "copy", bigB[:, O_QP + c0:O_QP + c0 + cw],
                       bank.ap[0:128, 0:cw], w=list(extraw))
                QPRE = (sAC, sAC.n)
                for tb in range(NB):
                    for j in range(4):
                        s0 = HALO - 3 + 512 * tb + j
                        mm(bC_, psC[0:128, 0:512],
                           bigB[:, O_DG + 128 * j:O_DG + 128 * j + 128],
                           bigB[:, O_QP + s0:O_QP + s0 + 512],
                           start=(j == 0), stop=(j == 3),
                           w=([QPRE, DIAG] if j == 0 else ()))
                    rd(bC_, "scalar", sAC, "activation",
                       bigB[:, dst_off + 2048 * half + 512 * tb:
                            dst_off + 2048 * half + 512 * tb + 512],
                       psC[0:128, 0:512], AF.Silu, w=list(extraw))
                return (sAC, sAC.n)

            def l2norm(dst_off, sc, bi_):
                vop("tensor_tensor", bigB[:, O_SQ:O_SQ + 4096],
                    bigB[:, dst_off:dst_off + 4096],
                    bigB[:, dst_off:dst_off + 4096], ALU.mult, w=[(sAC, sAC.n)])
                SQ = (sDV, sDV.n)
                for tb in range(NB):
                    stats_sum(bS, [
                        (bigB[:, O_SQ + 512 * tb:O_SQ + 512 * tb + 512], [SQ, ONES]),
                        (bigB[:, O_SQ + 2048 + 512 * tb:O_SQ + 2048 + 512 * tb + 512], []),
                    ])
                    rd(bS, "scalar", sAC, "activation", rows[0:1, 64:576],
                       psS[0:1, 0:512], AF.Sqrt, scale=sc, bias=bi_)
                    vop("reciprocal", rows[0:1, 64:576], rows[0:1, 64:576], w=[(sAC, sAC.n)])
                    aop("copy", rows_b[0:1, 4096:4608], rows[0:1, 64:576], w=[(sDV, sDV.n)])
                    bcast(bN2, rows_b[0:1, 4096:4608], w=[(sAC, sAC.n)])
                    for hf in range(2):
                        sl = bigB[:, dst_off + 2048 * hf + 512 * tb:
                                  dst_off + 2048 * hf + 512 * tb + 512]
                        rd(bN2, "vector", sDV, "tensor_tensor", sl, sl,
                           psN2[0:128, 0:512], ALU.mult)

            TERN_DONE = None
            CC_W = None
            for h in range(H):
                extraw = ([BETAPE] if h == 0 else []) + [bufs_free[0]]
                wsrc = [(wq, 0), (wq, 1), (wk, 0), (wk, 1), (wv, 0), (wv, 1)]

                def wload(widx):
                    wt, hf = wsrc[widx]
                    slot = widx % 2
                    dma("gpsimd", sLG, bigC[:, 11008 + 2048 * slot:11008 + 2048 * slot + 2048]
                        .rearrange("p (d o) -> p d o", d=NT),
                        wt[h][:, 128 * hf:128 * hf + 128]
                        .rearrange("(d p) o -> p d o", p=128),
                        w=[wslot_user[slot], (bufs_free[0] if h > 0 else None)])

                if h == 0:
                    wslot_user[0] = wslot_user[1] = None
                else:
                    wload(0)
                    wload(1)
                dsts = [O_QB, O_QB, O_KB, O_KB, O_KT, O_KT]
                coffs = [8 * h, 8 * h + 4, 64 + 8 * h, 64 + 8 * h + 4,
                         128 + 8 * h, 128 + 8 * h + 4]
                for widx in range(6):
                    proj_conv(widx, coffs[widx], dsts[widx], widx % 2, extraw=extraw)
                    wslot_user[widx % 2] = (sPE, sPE.n)
                    if widx + 2 < 6:
                        wload(widx + 2)
                VDONE = (sAC, sAC.n)
                ckpt('proj0')
                l2norm(O_QB, CM(3), CM(0))
                l2norm(O_KB, CM(6), CM(1))
                KNORM = (sDV, sDV.n)
                ckpt('l2')

                def do_transposes(src_off, dst_off, wfirst):
                    for g in range(8):
                        bnk = bT1 if g % 2 == 0 else bT2
                        for s in range(4):
                            idx = 4 * g + s
                            ci, m = idx // 2, idx % 2
                            transpose(bnk, bnk.ap[0:128, 128 * s:128 * s + 128],
                                      bigB[:, src_off + 2048 * m + 128 * ci:
                                           src_off + 2048 * m + 128 * ci + 128],
                                      c_eye[:, :],
                                      w=([wfirst] if (g == 0 and s == 0) else ()))
                        for s in range(4):
                            idx = 4 * g + s
                            ci, m = idx // 2, idx % 2
                            rd(bnk, "vector", sDV, "tensor_copy",
                               bigB[:, dst_off + 256 * ci + 128 * m:
                                    dst_off + 256 * ci + 128 * m + 128],
                               bnk.ap[0:128, 128 * s:128 * s + 128])
                do_transposes(O_KT, O_VT, VDONE)
                do_transposes(O_KB, O_KT, KNORM)
                TRD = (sDV, sDV.n)
                ckpt('tr')

                pgb = C_PG + 1024 * h
                for pt in range(2):
                    vop("tensor_copy", bigC[:, pgb + 512 * pt:pgb + 512 * pt + 256],
                        c_e256[:, 256 * pt:256 * pt + 256],
                        w=[CONSTW] + ([TERN_DONE] if h == 1 else []))
                    kb.op("vector", "memset",
                          bigC[:, pgb + 512 * pt + 256:pgb + 512 * pt + 512], 0.0,
                          inc=[(sDV, 1)])
                SCR = [(O_M, O_X, O_XT, O_R, O_RT), (32000, 32128, 32256, 32384, 32512)]
                BKS = [(bN1, psN1, bN2, psN2, bT1, psT1),
                       (bS, psS, bA, psA, bT2, psT2)]

                bmk = [None, None]

                def b_stageA(ci):
                    par = ci % 2
                    oM, oX, oXT, oR, oRT = SCR[par]
                    b1, p1, b2, p2, bt, pt_ = BKS[par]
                    kc = 128 * ci
                    bcol = beta_t[:, 8 * ci + h:8 * ci + h + 1]
                    mm(b1, p1[0:128, 0:128], bigB[:, O_KB + kc:O_KB + kc + 128],
                       bigB[:, O_KB + kc:O_KB + kc + 128], start=True, stop=False,
                       w=[KNORM, BETAT])
                    mm(b1, p1[0:128, 0:128],
                       bigB[:, O_KB + 2048 + kc:O_KB + 2048 + kc + 128],
                       bigB[:, O_KB + 2048 + kc:O_KB + 2048 + kc + 128],
                       start=False, stop=True)
                    rd(b1, "vector", sDV, "scalar_tensor_tensor",
                       bigB[:, oM:oM + 128], p1[0:128, 0:128], bcol, c_slm[:, :],
                       op0=ALU.mult, op1=ALU.mult)
                    vop("tensor_tensor", bigB[:, oM:oM + 128], bigB[:, oM:oM + 128],
                        c_eye[:, :], ALU.add)
                    MRDY = (sDV, sDV.n)
                    transpose(bt, pt_[0:128, 0:128], bigB[:, oM:oM + 128],
                              c_eye[:, :], w=[MRDY])
                    rd(bt, "vector", sDV, "tensor_tensor", bigB[:, oX:oX + 128],
                       c_eye2[:, :], pt_[0:128, 0:128], ALU.subtract)
                    vop("tensor_tensor", bigB[:, oXT:oXT + 128], c_eye2[:, :],
                        bigB[:, oM:oM + 128], ALU.subtract)
                    bmk[par] = [(sDV, sDV.n)]

                def b_step(ci, it):
                    par = ci % 2
                    oM, oX, oXT, oR, oRT = SCR[par]
                    b1, p1, b2, p2, bt, pt_ = BKS[par]
                    kc = 128 * ci
                    bcol = beta_t[:, 8 * ci + h:8 * ci + h + 1]
                    last = (it == 5)
                    XRDY = bmk[par] if isinstance(bmk[par], list) else [bmk[par]]
                    mm(b1, p1[0:128, 0:128], bigB[:, oM:oM + 128],
                       bigB[:, oX:oX + 128], start=True, stop=True, w=XRDY)
                    if not last:
                        mm(b2, p2[0:128, 0:128], bigB[:, oX:oX + 128],
                           bigB[:, oM:oM + 128], start=True, stop=True, w=XRDY)
                    rd(b1, "vector", sDV, "tensor_tensor", bigB[:, oR:oR + 128],
                       c_eye2[:, :], p1[0:128, 0:128], ALU.subtract)
                    if not last:
                        rd(b2, "vector", sDV, "tensor_tensor",
                           bigB[:, oRT:oRT + 128], c_eye2[:, :],
                           p2[0:128, 0:128], ALU.subtract)
                    RRDY = (sDV, sDV.n)
                    mm(b1, p1[0:128, 128:256], bigB[:, oXT:oXT + 128],
                       bigB[:, oR:oR + 128], start=True, stop=True, w=[RRDY])
                    if not last:
                        mm(b2, p2[0:128, 128:256], bigB[:, oX:oX + 128],
                           bigB[:, oRT:oRT + 128], start=True, stop=True, w=[RRDY])
                        rd(b1, "scalar", sAC, "copy", bigB[:, oX:oX + 128],
                           p1[0:128, 128:256], w=[(sPE, sPE.n)])
                        rd(b2, "vector", sDV, "tensor_copy", bigB[:, oXT:oXT + 128],
                           p2[0:128, 128:256])
                        bmk[par] = [(sDV, sDV.n), (sAC, sAC.n)]
                    else:
                        rd(b1, "vector", sDV, "tensor_scalar",
                           bigB[:, O_TI + kc:O_TI + kc + 128], p1[0:128, 128:256],
                           bcol, None, op0=ALU.mult)
                        bmk[par] = [(sDV, sDV.n)]

                def b_stageC(ci):
                    par = ci % 2
                    b1, p1, b2, p2, bt, pt_ = BKS[par]
                    uvb, uvp = (bA, psA) if par == 0 else (bN1, psN1)
                    kqb, kqp = (bN2, psN2) if par == 0 else (bS, psS)
                    kc = 128 * ci
                    TIR = (sDV, sDV.n)
                    mm(uvb, uvp[0:128, 0:256], bigB[:, O_TI + kc:O_TI + kc + 128],
                       bigB[:, O_VT + 256 * ci:O_VT + 256 * ci + 256],
                       start=True, stop=True, w=[TIR, TRD])
                    rd(uvb, "scalar", sAC, "copy",
                       bigB[:, O_UV + 256 * ci:O_UV + 256 * ci + 256],
                       uvp[0:128, 0:256])
                    mm(kqb, kqp[0:128, 0:128], bigB[:, O_KB + kc:O_KB + kc + 128],
                       bigB[:, O_QB + kc:O_QB + kc + 128], start=True, stop=False)
                    mm(kqb, kqp[0:128, 0:128],
                       bigB[:, O_KB + 2048 + kc:O_KB + 2048 + kc + 128],
                       bigB[:, O_QB + 2048 + kc:O_QB + 2048 + kc + 128],
                       start=False, stop=True)
                    rd(kqb, "vector", sDV, "tensor_tensor",
                       bigB[:, O_KQ + kc:O_KQ + kc + 128], kqp[0:128, 0:128],
                       c_um[:, :], ALU.mult)

                def b_stagePG(ci):
                    kc = 128 * ci
                    PGR = (sDV, sDV.n)
                    for m in range(2):
                        mm(bB_, psB[0:128, 0:512],
                           bigB[:, O_KB + 2048 * m + kc:O_KB + 2048 * m + kc + 128],
                           bigC[:, pgb + 512 * m:pgb + 512 * m + 512],
                           start=(m == 0), stop=(m == 1), w=([PGR] if m == 0 else ()))
                    rd(bB_, "vector", sDV, "tensor_copy", bigB[:, O_SQ:O_SQ + 256],
                       psB[0:128, 0:256])
                    rd(bB_, "vector", sDV, "tensor_tensor",
                       bigB[:, O_SQ + 256:O_SQ + 512], psB[0:128, 256:512],
                       bigB[:, O_VT + 256 * ci:O_VT + 256 * ci + 256], ALU.subtract)
                    RSB = [(sDV, sDV.n)]
                    mm(bB_, psB[0:128, 0:512], bigB[:, O_TI + kc:O_TI + kc + 128],
                       bigB[:, O_SQ:O_SQ + 512], start=True, stop=True, w=RSB)
                    rd(bB_, "scalar", sAC, "copy", bigB[:, O_SQ:O_SQ + 512],
                       psB[0:128, 0:512])
                    YSB = (sAC, sAC.n)
                    for pt in range(2):
                        mm(bC_, psC[0:128, 0:512],
                           bigB[:, O_KT + 256 * ci + 128 * pt:
                                O_KT + 256 * ci + 128 * pt + 128],
                           bigB[:, O_SQ:O_SQ + 512], start=True, stop=True, w=[YSB])
                        rd(bC_, "vector", sDV, "tensor_tensor",
                           bigC[:, pgb + 512 * pt:pgb + 512 * pt + 512],
                           bigC[:, pgb + 512 * pt:pgb + 512 * pt + 512],
                           psC[0:128, 0:512], ALU.subtract)

                for cp in range(NCH // 2):
                    c0, c1 = 2 * cp, 2 * cp + 1
                    b_stageA(c0)
                    b_stageA(c1)
                    for it in range(6):
                        b_step(c0, it)
                        b_step(c1, it)
                    b_stageC(c0)
                    b_stageC(c1)
                    b_stagePG(c0)
                    b_stagePG(c1)
                ckpt('chunks')
                SPW = [(sDV, sDV.n), (sAC, sAC.n)]
                dma("sync", sLD, sp_q[h], bigB[:, O_QB:O_QB + 4096], w=SPW)
                dma("sync", sLD, sp_kfm[h], bigB[:, O_KB:O_KB + 4096], w=SPW)
                dma("sync", sLD, sp_ktm[h], bigB[:, O_KT:O_KT + 4096], w=SPW)
                dma("sync", sLD, sp_ti[h], bigB[:, O_TI:O_TI + 2048], w=SPW)
                dma("sync", sLD, sp_uv[h], bigB[:, O_UV:O_UV + 4096], w=SPW)
                dma("sync", sLD, sp_kq[h], bigB[:, O_KQ:O_KQ + 2048], w=SPW)
                bufs_free[0] = (sLD, sLD.n)
                ckpt('h%d' % h)

                if h == 0:
                    for k in range(NT):
                        dma("sync", sLD, scrF[:, 0:1408],
                            wguT_sh.rearrange("(a p) o -> a p o", p=128)[k],
                            w=[(sDV, sDV.n)])
                        vop("tensor_scalar", scrF[:, 0:1408], scrF[:, 0:1408],
                            cfs[:, 0:1], 1.25, op0=ALU.mult, op1=ALU.min,
                            w=[(sLD, sLD.n), TSCALE])
                        vop("tensor_scalar", scrF[:, 0:1408], scrF[:, 0:1408],
                            -1.25, MAGIC, op0=ALU.max, op1=ALU.add)
                        vop("tensor_scalar", bigC[:, C_TERN:C_TERN + 1408],
                            scrF[:, 0:1408], -MAGIC, None, op0=ALU.add)
                        kb.op("sync", "dma_start",
                              out=tgu_in[:, k].rearrange("l p c -> p l c"),
                              in_=bigC[:, C_TERN:C_TERN + 1408]
                              .rearrange("p (l c) -> p l c", l=11),
                              w=[(sDV, sDV.n)], inc=[(sLD, 16)])
                    for li in range(2):
                        for k in range(11):
                            dma("sync", sLD,
                                scrF[:, 0:512].rearrange("p (a c) -> p a c", a=4),
                                wdT_sh.rearrange("(a p) o -> p a o", p=128)
                                [:, 4 * k:4 * k + 4, 128 * li:128 * li + 128],
                                w=[(sDV, sDV.n)])
                            vop("tensor_scalar", scrF[:, 0:512], scrF[:, 0:512],
                                cfs[:, 3:4], 1.25, op0=ALU.mult, op1=ALU.min,
                                w=[(sLD, sLD.n), TSCALE])
                            vop("tensor_scalar", scrF[:, 0:512], scrF[:, 0:512],
                                -1.25, MAGIC, op0=ALU.max, op1=ALU.add)
                            vop("tensor_scalar", bigC[:, C_TERN:C_TERN + 512],
                                scrF[:, 0:512], -MAGIC, None, op0=ALU.add)
                            kb.op("sync", "dma_start",
                                  out=twd_in[li, 4 * k:4 * k + 4]
                                  .rearrange("a p c -> p a c"),
                                  in_=bigC[:, C_TERN:C_TERN + 512]
                                  .rearrange("p (a c) -> p a c", a=4),
                                  w=[(sDV, sDV.n)], inc=[(sLD, 16)])
                    if dump:
                        dbg_cf = nc.dram_tensor("dbg_cf", [128, 8], F32,
                                                kind="ExternalOutput").ap()
                        dbg_scr = nc.dram_tensor("dbg_scr", [128, 512], F32,
                                                 kind="ExternalOutput").ap()
                        dbg_bc = nc.dram_tensor("dbg_bc", [128, 512], BF,
                                                kind="ExternalOutput").ap()
                        dma("sync", sLD, dbg_cf[:, 0:4], cfs[:, 0:4],
                            w=[(sDV, sDV.n)])
                        dma("sync", sLD, dbg_cf[:, 4:8], cfm[:, 0:4],
                            w=[(sDV, sDV.n)])
                        dma("sync", sLD, dbg_scr, scrF[:, 0:512], w=[(sDV, sDV.n)])
                        dma("sync", sLD, dbg_bc, bigC[:, C_TERN:C_TERN + 512],
                            w=[(sDV, sDV.n)])
                    TERN_DONE = (sLD, sLD.n)
                if h == 1:
                    kb.op("gpsimd", "collective_compute", "AllGather", ALU.bypass,
                          [list(range(NCORE))], [tgu_in], [tgu_out],
                          w=[TERN_DONE], inc=[(sCC, 1)])
                    kb.op("gpsimd", "collective_compute", "AllGather", ALU.bypass,
                          [list(range(NCORE))], [twd_in], [twd_out],
                          w=[TERN_DONE], inc=[(sCC, 1)])
                    CC_W = (sCC, sCC.n)
                ckpt2 = 'h%d' % h
                if stop_after == ckpt2:
                    raise _Stop()

            for h in range(H):
                pgb = C_PG + 1024 * h
                sl2 = 1024 * (h % 2)
                W0 = [bufs_free[0], (sDV, sDV.n), (sLD, sLD.n)]
                for a in range(2):
                    bnk = bT1 if a == 0 else bT2
                    for b in range(2):
                        transpose(bnk, bnk.ap[0:128, 128 * b:128 * b + 128],
                                  bigC[:, pgb + 512 * b + 128 * a:
                                       pgb + 512 * b + 128 * a + 128],
                                  c_eye[:, :], w=W0)
                    for b in range(2):
                        rd(bnk, "vector", sDV, "tensor_copy",
                           bigB[:, sl2 + 512 * a + 128 * b:sl2 + 512 * a + 128 * b + 128],
                           bnk.ap[0:128, 128 * b:128 * b + 128])
                    vop("tensor_copy", bigB[:, sl2 + 512 * a + 256:sl2 + 512 * a + 512],
                        bigC[:, pgb + 512 * a + 256:pgb + 512 * a + 512])
                dma("sync", sLD, pg_in[h], bigB[:, sl2:sl2 + 1024], w=[(sDV, sDV.n)])
            kb.op("gpsimd", "collective_compute", "AllGather", ALU.bypass,
                  [list(range(NCORE))], [pg_in], [pg_out],
                  w=[(sLD, sLD.n)], inc=[(sCC, 1)])
            PG_AG = (sCC, sCC.n)
            ckpt('p2')

            D_KF, D_KT, D_QF, D_TI, D_UV, D_KQ = 0, 4096, 8192, 12288, 14336, 18432
            D_RS, D_US, D_OSQ, D_AG = 20480, 20736, 20992, 25088
            D_MSK = 21504
            def d_reload(h):
                RELW = [(sPE, sPE.n), (sDV, sDV.n), (sAC, sAC.n), (sLD, sLD.n)]
                dma("sync", sLD, bigB[:, D_KF:D_KF + 4096], sp_kfm[h], w=RELW)
                dma("sync", sLD, bigB[:, D_KT:D_KT + 4096], sp_ktm[h], w=RELW)
                dma("sync", sLD, bigB[:, D_QF:D_QF + 4096], sp_q[h], w=RELW)
                dma("sync", sLD, bigB[:, D_TI:D_TI + 2048], sp_ti[h], w=RELW)
                dma("sync", sLD, bigB[:, D_UV:D_UV + 4096], sp_uv[h], w=RELW)
                dma("sync", sLD, bigB[:, D_KQ:D_KQ + 2048], sp_kq[h], w=RELW)
                for j in range(7):
                    dma("sync", sLD, bigB[:, D_AG + 1024 * j:D_AG + 1024 * j + 1024],
                        pg_out[j, h], w=RELW + [PG_AG])
                return (sLD, sLD.n)
            MSKC = 16384

            def d_compose(h, RELDh):
                hoff = 512 * (h % 2)
                kb.op("vector", "memset", S0[:, hoff:hoff + 512], 0.0,
                      inc=[(sDV, 1)], w=[(sPE, sPE.n)])
                for j in range(7):
                    base = D_AG + 1024 * j
                    for pt in range(2):
                        vop("tensor_scalar", bigC[:, MSKC + 512 * pt:MSKC + 512 * pt + 256],
                            bigB[:, base + 512 * pt:base + 512 * pt + 256],
                            c_sel[:, j:j + 1], None, op0=ALU.mult,
                            w=[RELDh, (sPE, sPE.n)] if (j == 0 and pt == 0) else ())
                        vop("scalar_tensor_tensor",
                            bigC[:, MSKC + 512 * pt:MSKC + 512 * pt + 256],
                            c_e256[:, 256 * pt:256 * pt + 256], c_sel[:, 8 + j:9 + j],
                            bigC[:, MSKC + 512 * pt:MSKC + 512 * pt + 256],
                            op0=ALU.mult, op1=ALU.add)
                        vop("tensor_scalar",
                            bigC[:, MSKC + 512 * pt + 256:MSKC + 512 * pt + 512],
                            bigB[:, base + 512 * pt + 256:base + 512 * pt + 512],
                            c_sel[:, j:j + 1], None, op0=ALU.mult)
                    MRD = (sDV, sDV.n)
                    for a in range(2):
                        for k2 in range(2):
                            mm(bA, psA[0:128, 256 * a:256 * a + 256],
                               bigC[:, MSKC + 512 * k2 + 128 * a:
                                    MSKC + 512 * k2 + 128 * a + 128],
                               S0[:, hoff + 256 * k2:hoff + 256 * k2 + 256],
                               start=(k2 == 0), stop=(k2 == 1),
                               w=([MRD] if k2 == 0 else ()))
                    for a in range(2):
                        rd(bA, "vector", sDV, "tensor_tensor",
                           S0[:, hoff + 256 * a:hoff + 256 * a + 256],
                           psA[0:128, 256 * a:256 * a + 256],
                           bigC[:, MSKC + 512 * a + 256:MSKC + 512 * a + 512],
                           ALU.add)
                return (sDV, sDV.n)
            RELD = d_reload(0)
            CMARK = d_compose(0, RELD)
            for h in range(H):
                hoff = 512 * (h % 2)

                smark = CMARK
                for ci in range(NCH):
                    kc = 128 * ci
                    for m in range(2):
                        mm(bC_, psC[0:128, 0:256],
                           bigB[:, D_KF + 2048 * m + kc:D_KF + 2048 * m + kc + 128],
                           S0[:, hoff + 256 * m:hoff + 256 * m + 256],
                           start=(m == 0), stop=(m == 1),
                           w=([RELD, CMARK, smark] if m == 0 else ()))
                    rd(bC_, "scalar", sAC, "copy", bigB[:, D_RS:D_RS + 256],
                       psC[0:128, 0:256])
                    mm(bC_, psC[0:128, 256:512], bigB[:, D_TI + kc:D_TI + kc + 128],
                       bigB[:, D_RS:D_RS + 256], start=True, stop=True,
                       w=[(sAC, sAC.n)])
                    rd(bC_, "vector", sDV, "tensor_tensor", bigB[:, D_US:D_US + 256],
                       bigB[:, D_UV + 256 * ci:D_UV + 256 * ci + 256],
                       psC[0:128, 256:512], ALU.subtract)
                    URD = (sDV, sDV.n)
                    for a in range(2):
                        for k2 in range(2):
                            mm(bA, psA[0:128, 128 * a:128 * a + 128],
                               S0[:, hoff + 256 * k2 + 128 * a:
                                  hoff + 256 * k2 + 128 * a + 128],
                               bigB[:, D_QF + 2048 * k2 + kc:D_QF + 2048 * k2 + kc + 128],
                               start=(k2 == 0), stop=False)
                        mm(bA, psA[0:128, 128 * a:128 * a + 128],
                           bigB[:, D_US + 128 * a:D_US + 128 * a + 128],
                           bigB[:, D_KQ + kc:D_KQ + kc + 128], start=False, stop=True,
                           w=[URD])
                    for m in range(2):
                        mm(bB_, psB[0:128, 256 * m:256 * m + 256],
                           bigB[:, D_KT + 256 * ci + 128 * m:
                                D_KT + 256 * ci + 128 * m + 128],
                           bigB[:, D_US:D_US + 256], start=(m == 0), stop=(m == 1),
                           w=([URD] if m == 0 else ()))
                    rd(bB_, "vector", sDV, "tensor_tensor",
                       S0[:, hoff:hoff + 512], S0[:, hoff:hoff + 512],
                       psB[0:128, 0:512], ALU.add)
                    smark = (sDV, sDV.n)
                    for a in range(2):
                        rd(bA, "scalar", sAC, "copy",
                           bigA[:, XW * (2 * h + a) + kc:XW * (2 * h + a) + kc + 128],
                           psA[0:128, 128 * a:128 * a + 128])
                ODONE = (sAC, sAC.n)
                if h + 1 < H:
                    RELD = d_reload(h + 1)
                    CMARK = d_compose(h + 1, RELD)
                vop("tensor_tensor", bigB[:, D_OSQ:D_OSQ + 2048],
                    bigA[:, XW * 2 * h:XW * 2 * h + 2048],
                    bigA[:, XW * 2 * h:XW * 2 * h + 2048], ALU.mult, w=[ODONE])
                vop("tensor_tensor", bigB[:, D_OSQ + 2048:D_OSQ + 4096],
                    bigA[:, XW * (2 * h + 1):XW * (2 * h + 1) + 2048],
                    bigA[:, XW * (2 * h + 1):XW * (2 * h + 1) + 2048], ALU.mult)
                OSQ = (sDV, sDV.n)
                for tb in range(NB):
                    stats_sum(bS, [
                        (bigB[:, D_OSQ + 512 * tb:D_OSQ + 512 * tb + 512], [OSQ]),
                        (bigB[:, D_OSQ + 2048 + 512 * tb:D_OSQ + 2048 + 512 * tb + 512], []),
                    ])
                    rd(bS, "scalar", sAC, "activation", rows[0:1, 64:576],
                       psS[0:1, 0:512], AF.Sqrt, scale=CM(4), bias=CM(2))
                    vop("reciprocal", rows[0:1, 64:576], rows[0:1, 64:576],
                        w=[(sAC, sAC.n)])
                    aop("copy", rows_b[0:1, 4096:4608], rows[0:1, 64:576],
                        w=[(sDV, sDV.n)])
                    bcast(bN2, rows_b[0:1, 4096:4608], w=[(sAC, sAC.n)])
                    for a in range(2):
                        sl = bigA[:, XW * (2 * h + a) + 512 * tb:
                                  XW * (2 * h + a) + 512 * tb + 512]
                        rd(bN2, "vector", sDV, "tensor_tensor", sl, sl,
                           psN2[0:128, 0:512], ALU.mult)
                        vop("tensor_scalar", sl, sl, c_nw2[:, a:a + 1], None,
                            op0=ALU.mult)
            OHAT = (sDV, sDV.n)
            ckpt('p4')

            wo_user = [None, None]
            x_user = [None, None]
            for ot in range(NT):
                slot = ot % 2
                dma("gpsimd", sLG, bigC[:, 2048 * slot:2048 * slot + 2048]
                    .rearrange("p (d o) -> p d o", d=NT),
                    wo[ot].rearrange("(d p) o -> p d o", p=128),
                    w=[wo_user[slot]] +
                    ([(sDV, sDV.n), (sPE, sPE.n), (sLD, sLD.n)] if ot < 2 else []))
                WOL = (sLG, sLG.n)
                for tb in range(NB):
                    xslot = (ot * NB + tb) % 2
                    dma("sync", sLD, scrF[:, 512 * xslot:512 * xslot + 512],
                        xT.rearrange("(d p) t -> p d t", p=128)
                        [:, ot, 512 * tb:512 * tb + 512],
                        w=[x_user[xslot]] +
                        ([(sDV, sDV.n)] if (ot == 0 and tb < 2) else []))
                    XR = (sLD, sLD.n)
                    bank = bA if (ot * NB + tb) % 2 == 0 else bB_
                    for d in range(NT):
                        mm(bank, bank.ap[0:128, 0:512],
                           bigC[:, 2048 * slot + 128 * d:2048 * slot + 128 * d + 128],
                           bigA[:, XW * d + 512 * tb:XW * d + 512 * tb + 512],
                           start=(d == 0), stop=(d == NT - 1),
                           w=([WOL, OHAT] if d == 0 else ()))
                    rd(bank, "vector", sDV, "tensor_tensor",
                       bigB[:, 2048 * ot + 512 * tb:2048 * ot + 512 * tb + 512],
                       bank.ap[0:128, 0:512], scrF[:, 512 * xslot:512 * xslot + 512],
                       ALU.add, w=[XR])
                    x_user[xslot] = (sDV, sDV.n)
                wo_user[slot] = (sPE, sPE.n)
            RDONE = (sDV, sDV.n)
            for tb in range(NB):
                stats_sum(bS, [(bigB[:, 2048 * d + 512 * tb:2048 * d + 512 * tb + 512],
                                [RDONE] if d == 0 else []) for d in range(NT)])
                rd(bS, "vector", sDV, "tensor_scalar", rows[0:1, 64:576],
                   psS[0:1, 0:512], 1.0 / float(D), None, op0=ALU.mult,
                   w=[(sAC, sAC.n)])
                marks = []
                for d in range(NT):
                    sqo = 4096 if d % 2 == 0 else 5120
                    aop("activation", bigC[:, sqo:sqo + 512],
                        bigB[:, 2048 * d + 512 * tb:2048 * d + 512 * tb + 512],
                        AF.Square, w=[RDONE] + ([marks[d - 2]] if d >= 2
                                                else [(sDV, sDV.n)]))
                    mm(bN1, psN1[0:1, 0:512], ones_c[:, :], bigC[:, sqo:sqo + 512],
                       start=(d == 0), stop=(d == NT - 1), w=[(sAC, sAC.n)],
                       inc=([] if d == NT - 1 else [(sPE, 1)]))
                    marks.append((sPE, sPE.n))
                rd(bN1, "vector", sDV, "tensor_scalar", rows[0:1, 576:1088],
                   psN1[0:1, 0:512], 1.0 / float(D), None, op0=ALU.mult)
                vop("tensor_tensor", rows[0:1, 1088:1600], rows[0:1, 64:576],
                    rows[0:1, 64:576], ALU.mult)
                vop("tensor_tensor", rows[0:1, 576:1088], rows[0:1, 576:1088],
                    rows[0:1, 1088:1600], ALU.subtract)
                aop("activation", rows[0:1, 576:1088], rows[0:1, 576:1088], AF.Sqrt,
                    bias=CM(2), scale=CM(6), w=[(sDV, sDV.n)])
                vop("reciprocal", rows[0:1, 576:1088], rows[0:1, 576:1088],
                    w=[(sAC, sAC.n)])
                aop("copy", rows_b[0:1, 4096:4608], rows[0:1, 64:576], w=[(sDV, sDV.n)])
                aop("copy", rows_b[0:1, 4608:5120], rows[0:1, 576:1088])
                LNR = (sAC, sAC.n)
                bcast(bN1, rows_b[0:1, 4096:4608], w=[LNR])
                rd(bN1, "vector", sDV, "tensor_copy", bigC[:, 4096:4608],
                   psN1[0:128, 0:512])
                bcast(bN2, rows_b[0:1, 4608:5120], w=[LNR])
                rd(bN2, "vector", sDV, "tensor_copy", bigC[:, 4608:5120],
                   psN2[0:128, 0:512])
                MB = (sDV, sDV.n)
                for d in range(NT):
                    sl = bigB[:, 2048 * d + 512 * tb:2048 * d + 512 * tb + 512]
                    vop("tensor_tensor", sl, sl, bigC[:, 4096:4608], ALU.subtract,
                        w=[MB])
                    vop("tensor_tensor", sl, sl, bigC[:, 4608:5120], ALU.mult)
                dma("sync", sLD, y1_d[tb].rearrange("p (d t) -> p d t", d=NT),
                    bigB[:, :].rearrange("p (d t) -> p d t", d=NT)
                    [:, :, 512 * tb:512 * tb + 512], w=[(sDV, sDV.n)])
            Y1S = (sLD, sLD.n)
            ckpt('p5')

            for tb in range(NB):
                marks = []
                for d in range(NT):
                    sqo = 512 if d % 2 == 0 else 5248
                    aop("activation", bigC[:, sqo:sqo + 512],
                        bigB[:, 2048 * d + 512 * tb:2048 * d + 512 * tb + 512],
                        AF.Square, w=([marks[d - 2]] if d >= 2
                                      else [(sPE, sPE.n), (sDV, sDV.n)]))
                    mm(bN1, psN1[0:1, 0:512], ones_c[:, :], bigC[:, sqo:sqo + 512],
                       start=(d == 0), stop=(d == NT - 1), w=[(sAC, sAC.n)],
                       inc=([] if d == NT - 1 else [(sPE, 1)]))
                    marks.append((sPE, sPE.n))
                rd(bN1, "scalar", sAC, "activation", rows[0:1, 64:576],
                   psN1[0:1, 0:512], AF.Sqrt, scale=CM(5), bias=CM(2))
                vop("reciprocal", rows[0:1, 64:576], rows[0:1, 64:576], w=[(sAC, sAC.n)])
                aop("copy", rows_b[0:1, 4096:4608], rows[0:1, 64:576], w=[(sDV, sDV.n)])
                bcast(bN2, rows_b[0:1, 4096:4608], w=[(sAC, sAC.n)])
                rd(bN2, "vector", sDV, "tensor_copy", bigC[:, 0:512], psN2[0:128, 0:512])
                IVB = (sDV, sDV.n)
                for d in range(NT):
                    vop("tensor_tensor", bigC[:, 512:1024],
                        bigB[:, 2048 * d + 512 * tb:2048 * d + 512 * tb + 512],
                        bigC[:, 0:512], ALU.mult, w=[IVB])
                    aop("activation", bigA[:, XW * d + 512 * tb:XW * d + 512 * tb + 512],
                        bigC[:, 512:1024], AF.Identity, scale=c_nww[:, d:d + 1],
                        w=[(sDV, sDV.n)])
                    op_init = ALU.max if d == 0 else None
                    xsl = bigA[:, XW * d + 512 * tb:XW * d + 512 * tb + 512]
                    if d == 0:
                        vop("tensor_tensor", bigC[:, 1024 + 512 * tb:1536 + 512 * tb],
                            xsl, xsl, ALU.max, w=[(sAC, sAC.n)])
                        vop("tensor_tensor", bigC[:, 3072 + 512 * tb:3584 + 512 * tb],
                            xsl, xsl, ALU.min)
                    else:
                        vop("tensor_tensor", bigC[:, 1024 + 512 * tb:1536 + 512 * tb],
                            bigC[:, 1024 + 512 * tb:1536 + 512 * tb], xsl, ALU.max,
                            w=[(sAC, sAC.n)])
                        vop("tensor_tensor", bigC[:, 3072 + 512 * tb:3584 + 512 * tb],
                            bigC[:, 3072 + 512 * tb:3584 + 512 * tb], xsl, ALU.min)
            for tb in range(NB):
                vop("tensor_scalar", bigC[:, 3072 + 512 * tb:3584 + 512 * tb],
                    bigC[:, 3072 + 512 * tb:3584 + 512 * tb], -1.0, None, op0=ALU.mult)
                vop("tensor_tensor", bigC[:, 1024 + 512 * tb:1536 + 512 * tb],
                    bigC[:, 1024 + 512 * tb:1536 + 512 * tb],
                    bigC[:, 3072 + 512 * tb:3584 + 512 * tb], ALU.max)
            XNF = (sDV, sDV.n)
            for i in range(NT):
                bnk = bT1 if i % 2 == 0 else bT2
                transpose(bnk, bnk.ap[0:128, 128 * (i % 4 // 2):128 * (i % 4 // 2) + 128],
                          bigC[:, 1024 + 128 * i:1024 + 128 * i + 128], c_eye[:, :],
                          w=[XNF])
                rd(bnk, "vector", sDV, "reduce_max", mxc[:, i:i + 1],
                   bnk.ap[0:128, 128 * (i % 4 // 2):128 * (i % 4 // 2) + 128], AX.X)
            MXC = (sDV, sDV.n)
            transpose(bT1, psT1[0:16, 0:128], mxc[:, 0:16], c_eye[:, :], w=[MXC])
            rd(bT1, "vector", sDV, "tensor_copy", bigC[0:16, 5120:5248],
               psT1[0:16, 0:128])
            kb.op("sync", "dma_start", out=rows_b[0:1, 4096:6144]
                  if False else rows_b[0:1, 0:2048],
                  in_=bigC[0:16, 5120:5248], w=[(sDV, sDV.n)], inc=[(sLD, 16)])
            MAXR = (sLD, sLD.n)
            aop("copy", rows[0:1, 1088:3136], rows_b[0:1, 0:2048], w=[MAXR])
            vop("tensor_scalar", rows[0:1, 1088:3136], rows[0:1, 1088:3136], 1e-5,
                None, op0=ALU.max, w=[(sAC, sAC.n)])
            vop("reciprocal", rows[0:1, 64:2112] if False else rows[0:1, 1088:3136],
                rows[0:1, 1088:3136])
            vop("tensor_scalar", rows[0:1, 1088:3136], rows[0:1, 1088:3136], 127.0,
                None, op0=ALU.mult)
            aop("copy", rows_b[0:1, 0:2048], rows[0:1, 1088:3136], w=[(sDV, sDV.n)])
            S1R = (sAC, sAC.n)
            vop("reciprocal", rows[0:1, 1088:3136], rows[0:1, 1088:3136], w=[S1R])
            vop("tensor_scalar", rows[0:1, 1088:3136], rows[0:1, 1088:3136],
                cfm[0:1, 0:1], None, op0=ALU.mult)
            vop("tensor_tensor", rows_b[0:1, 2048:4096] if False else rows[0:1, 64:1088],
                rows[0:1, 1088:2112], rows[0:1, 1088:2112], ALU.mult)
            vop("tensor_tensor", rows[0:1, 64:1088], rows[0:1, 64:1088],
                rows[0:1, 1088:2112], ALU.mult)
            aop("copy", rows_b[0:1, 2048:3072], rows[0:1, 64:1088], w=[(sDV, sDV.n)])
            vop("tensor_tensor", rows[0:1, 64:1088], rows[0:1, 2112:3136],
                rows[0:1, 2112:3136], ALU.mult, w=[(sAC, sAC.n)])
            vop("tensor_tensor", rows[0:1, 64:1088], rows[0:1, 64:1088],
                rows[0:1, 2112:3136], ALU.mult)
            aop("copy", rows_b[0:1, 3072:4096], rows[0:1, 64:1088], w=[(sDV, sDV.n)])
            C13 = (sAC, sAC.n)
            for tb in range(NB):
                bcast(bN1, rows_b[0:1, 512 * tb:512 * tb + 512], w=[S1R])
                rd(bN1, "vector", sDV, "tensor_copy", bigC[:, 0:512], psN1[0:128, 0:512])
                SB1 = (sDV, sDV.n)
                for d in range(NT):
                    sl = bigA[:, XW * d + 512 * tb:XW * d + 512 * tb + 512]
                    vop("tensor_tensor", bigC[:, 512:1024], sl, bigC[:, 0:512],
                        ALU.mult, w=[SB1])
                    vop("tensor_scalar", bigC[:, 512:1024], bigC[:, 512:1024],
                        MAGIC, -MAGIC, op0=ALU.add, op1=ALU.add)
                    vop("tensor_copy", sl, bigC[:, 512:1024])
            XINT = (sDV, sDV.n)

            F_WP0, F_WP1 = 8192, 12288
            F_G, F_U, F_G2, F_HF, F_F2B, F_G2B, F_HN = 16384, 16896, 17408, 17920, 18432, 18944, 19456
            F_WD0, F_WD1 = 0, 5632
            F_R2S, F_MB, F_IB, F_SQ2 = 11264, 11776, 12288, 12800
            F_Y1 = 20480
            wp_user = [None, None]
            wd_user = [None, None]
            y1r_user = [None]
            of_user = [None, None]
            for tb in range(NB):
                for i in range(44):
                    slot = i % 2
                    wpo = F_WP0 if slot == 0 else F_WP1
                    for gu in range(2):
                        oti = i + 44 * gu
                        sh, loc = oti // 11, oti % 11
                        dma("sync", sLD, bigB[:, wpo + 2048 * gu:wpo + 2048 * gu + 2048]
                            .rearrange("p (d o) -> p d o", d=NT),
                            tgu_out[sh, loc].rearrange("d p o -> p d o"),
                            w=[wp_user[slot], Y1S] +
                            ([CC_W] if (tb == 0 and i < 2) else []))
                    WPL = (sLD, sLD.n)
                    bkg = [bA, bN1][i % 2]
                    bku = [bB_, bN2][i % 2]
                    for d in range(NT):
                        mm(bkg, bkg.ap[0:128, 0:512],
                           bigB[:, wpo + 128 * d:wpo + 128 * d + 128],
                           bigA[:, XW * d + 512 * tb:XW * d + 512 * tb + 512],
                           start=(d == 0), stop=(d == NT - 1),
                           w=([WPL, XINT] if d == 0 else ()))
                    for d in range(NT):
                        mm(bku, bku.ap[0:128, 0:512],
                           bigB[:, wpo + 2048 + 128 * d:wpo + 2048 + 128 * d + 128],
                           bigA[:, XW * d + 512 * tb:XW * d + 512 * tb + 512],
                           start=(d == 0), stop=(d == NT - 1))
                    rd(bkg, "scalar", sAC, "activation", bigB[:, F_G:F_G + 512],
                       bkg.ap[0:128, 0:512], AF.Relu, w=[(sDV, sDV.n)])
                    rd(bku, "scalar", sAC, "copy", bigB[:, F_U:F_U + 512],
                       bku.ap[0:128, 0:512])
                    GUE = (sAC, sAC.n)
                    vop("tensor_tensor", bigB[:, F_G2:F_G2 + 512],
                        bigB[:, F_G:F_G + 512], bigB[:, F_G:F_G + 512], ALU.mult,
                        w=[GUE])
                    vop("tensor_tensor", bigC[:, 512 * i:512 * i + 512],
                        bigB[:, F_G2:F_G2 + 512], bigB[:, F_U:F_U + 512], ALU.mult)
                    hsl = bigC[:, 512 * i:512 * i + 512]
                    if i == 0:
                        vop("tensor_tensor", bigB[:, F_HF:F_HF + 512], hsl, hsl, ALU.max)
                        vop("tensor_tensor", bigB[:, F_HN:F_HN + 512], hsl, hsl, ALU.min)
                    else:
                        vop("tensor_tensor", bigB[:, F_HF:F_HF + 512],
                            bigB[:, F_HF:F_HF + 512], hsl, ALU.max)
                        vop("tensor_tensor", bigB[:, F_HN:F_HN + 512],
                            bigB[:, F_HN:F_HN + 512], hsl, ALU.min)
                    wp_user[slot] = (sDV, sDV.n)
                vop("tensor_scalar", bigB[:, F_HN:F_HN + 512], bigB[:, F_HN:F_HN + 512],
                    -1.0, None, op0=ALU.mult)
                vop("tensor_tensor", bigB[:, F_HF:F_HF + 512], bigB[:, F_HF:F_HF + 512],
                    bigB[:, F_HN:F_HN + 512], ALU.max)
                HFD = (sDV, sDV.n)
                for s in range(4):
                    bnk = bT1 if s % 2 == 0 else bT2
                    transpose(bnk, bnk.ap[0:128, 0:128],
                              bigB[:, F_HF + 128 * s:F_HF + 128 * s + 128], c_eye[:, :],
                              w=[HFD])
                    rd(bnk, "vector", sDV, "reduce_max", mxc[:, s:s + 1],
                       bnk.ap[0:128, 0:128], AX.X)
                transpose(bT1, psT1[0:4, 0:128], mxc[:, 0:4], c_eye[:, :],
                          w=[(sDV, sDV.n)])
                rd(bT1, "vector", sDV, "tensor_copy", bigB[0:4, F_R2S:F_R2S + 128],
                   psT1[0:4, 0:128])
                kb.op("sync", "dma_start", out=rows_b[0:1, 4096:4608],
                      in_=bigB[0:4, F_R2S:F_R2S + 128], w=[(sDV, sDV.n)], inc=[(sLD, 16)])
                aop("copy", rows[0:1, 64:576], rows_b[0:1, 4096:4608],
                    w=[(sLD, sLD.n), (sDV, sDV.n)])
                vop("tensor_tensor", rows[0:1, 64:576],
                    rows_b[0:1, 2048 + 512 * tb:2560 + 512 * tb], rows[0:1, 64:576],
                    ALU.mult, w=[(sAC, sAC.n), C13])
                vop("tensor_scalar", rows[0:1, 64:576], rows[0:1, 64:576], 1e-5, None,
                    op0=ALU.max)
                vop("reciprocal", rows[0:1, 576:1088], rows[0:1, 64:576])
                vop("tensor_tensor", rows[0:1, 576:1088], rows[0:1, 576:1088],
                    rows_b[0:1, 2048 + 512 * tb:2560 + 512 * tb], ALU.mult)
                vop("tensor_scalar", rows[0:1, 576:1088], rows[0:1, 576:1088], 127.0,
                    None, op0=ALU.mult)
                vop("tensor_scalar", rows[0:1, 64:576], rows[0:1, 64:576],
                    cfm[0:1, 3:4], 1.0 / 127.0, op0=ALU.mult, op1=ALU.mult)
                aop("copy", rows_b[0:1, 4096:4608], rows[0:1, 576:1088],
                    w=[(sDV, sDV.n)])
                aop("copy", rows_b[0:1, 4608:5120], rows[0:1, 64:576])
                F2R = (sAC, sAC.n)
                bcast(bN1, rows_b[0:1, 4096:4608], w=[F2R])
                rd(bN1, "vector", sDV, "tensor_copy", bigB[:, F_F2B:F_F2B + 512],
                   psN1[0:128, 0:512])
                F2B = (sDV, sDV.n)
                x2marks = []
                for i in range(44):
                    sl = bigC[:, 512 * i:512 * i + 512]
                    vop("tensor_tensor", bigB[:, F_G2:F_G2 + 512], sl,
                        bigB[:, F_F2B:F_F2B + 512], ALU.mult, w=[F2B])
                    vop("tensor_scalar", sl, bigB[:, F_G2:F_G2 + 512],
                        MAGIC, -MAGIC, op0=ALU.add, op1=ALU.add)
                    x2marks.append((sDV, sDV.n))
                X2Q = (sDV, sDV.n)
                bcast(bN2, rows_b[0:1, 4608:5120], w=[F2R])
                rd(bN2, "vector", sDV, "tensor_copy", bigB[:, F_G2B:F_G2B + 512],
                   psN2[0:128, 0:512])
                G2B = (sDV, sDV.n)
                dma("sync", sLD, bigB[:, F_Y1:F_Y1 + NT * 512], y1_d[tb],
                    w=[y1r_user[0], Y1S])
                Y1R = (sLD, sLD.n)
                for o2 in range(NT):
                    slot = o2 % 2
                    wdo = F_WD0 if slot == 0 else F_WD1
                    sh, loc = o2 // 2, o2 % 2
                    dma("sync", sLD, bigB[:, wdo:wdo + 5632]
                        .rearrange("p (f o) -> p f o", f=44),
                        twd_out[sh, loc].rearrange("f p o -> p f o"),
                        w=[wd_user[slot], XINT] +
                        ([CC_W] if (tb == 0 and o2 < 2) else []))
                    WDL = (sLD, sLD.n)
                    bank = [bA, bB_, bN1, bN2][o2 % 4]
                    for f in range(44):
                        mm(bank, bank.ap[0:128, 0:512],
                           bigB[:, wdo + 128 * f:wdo + 128 * f + 128],
                           bigC[:, 512 * f:512 * f + 512],
                           start=(f == 0), stop=(f == 43),
                           w=([WDL, x2marks[f]] if o2 < 2 else
                              ([WDL] if f == 0 else ())))
                    rd(bank, "vector", sDV, "tensor_tensor",
                       bigB[:, F_R2S:F_R2S + 512], bank.ap[0:128, 0:512],
                       bigB[:, F_G2B:F_G2B + 512], ALU.mult, w=[G2B, Y1R])
                    vop("tensor_tensor", bigB[:, F_Y1 + 512 * o2:F_Y1 + 512 * o2 + 512],
                        bigB[:, F_Y1 + 512 * o2:F_Y1 + 512 * o2 + 512],
                        bigB[:, F_R2S:F_R2S + 512], ALU.add)
                    wd_user[slot] = (sPE, sPE.n)
                R2D = (sDV, sDV.n)
                stats_sum(bS, [(bigB[:, F_Y1 + 512 * d:F_Y1 + 512 * d + 512],
                                [R2D] if d == 0 else []) for d in range(NT)])
                rd(bS, "vector", sDV, "tensor_scalar", rows[0:1, 64:576],
                   psS[0:1, 0:512], 1.0 / float(D), None, op0=ALU.mult,
                   w=[(sAC, sAC.n)])
                marks = []
                for d in range(NT):
                    sqo = F_SQ2 + 512 * (d % 2)
                    aop("activation", bigB[:, sqo:sqo + 512],
                        bigB[:, F_Y1 + 512 * d:F_Y1 + 512 * d + 512], AF.Square,
                        w=[R2D] + ([marks[d - 2]] if d >= 2 else [(sDV, sDV.n)]))
                    mm(bN1, psN1[0:1, 0:512], ones_c[:, :], bigB[:, sqo:sqo + 512],
                       start=(d == 0), stop=(d == NT - 1), w=[(sAC, sAC.n)],
                       inc=([] if d == NT - 1 else [(sPE, 1)]))
                    marks.append((sPE, sPE.n))
                rd(bN1, "vector", sDV, "tensor_scalar", rows[0:1, 576:1088],
                   psN1[0:1, 0:512], 1.0 / float(D), None, op0=ALU.mult)
                vop("tensor_tensor", rows[0:1, 1088:1600], rows[0:1, 64:576],
                    rows[0:1, 64:576], ALU.mult)
                vop("tensor_tensor", rows[0:1, 576:1088], rows[0:1, 576:1088],
                    rows[0:1, 1088:1600], ALU.subtract)
                aop("activation", rows[0:1, 576:1088], rows[0:1, 576:1088], AF.Sqrt,
                    bias=CM(2), scale=CM(6), w=[(sDV, sDV.n)])
                vop("reciprocal", rows[0:1, 576:1088], rows[0:1, 576:1088],
                    w=[(sAC, sAC.n)])
                aop("copy", rows_b[0:1, 4096:4608], rows[0:1, 64:576], w=[(sDV, sDV.n)])
                aop("copy", rows_b[0:1, 4608:5120], rows[0:1, 576:1088])
                LN2R = (sAC, sAC.n)
                bcast(bN1, rows_b[0:1, 4096:4608], w=[LN2R])
                rd(bN1, "vector", sDV, "tensor_copy", bigB[:, F_MB:F_MB + 512],
                   psN1[0:128, 0:512])
                bcast(bN2, rows_b[0:1, 4608:5120], w=[LN2R])
                rd(bN2, "vector", sDV, "tensor_copy", bigB[:, F_IB:F_IB + 512],
                   psN2[0:128, 0:512])
                MB2 = (sDV, sDV.n)
                for d in range(NT):
                    osl = d % 2
                    ooff = 512 * osl
                    vop("tensor_tensor", scrF[:, ooff:ooff + 512],
                        bigB[:, F_Y1 + 512 * d:F_Y1 + 512 * d + 512],
                        bigB[:, F_MB:F_MB + 512], ALU.subtract,
                        w=[MB2, of_user[osl]])
                    vop("tensor_tensor", scrF[:, ooff:ooff + 512],
                        scrF[:, ooff:ooff + 512], bigB[:, F_IB:F_IB + 512], ALU.mult)
                    kb.op("sync", "dma_start",
                          out=out_d.rearrange("(d p) t -> p d t", p=128)
                          [:, d, 512 * tb:512 * tb + 512],
                          in_=scrF[:, ooff:ooff + 512],
                          w=[(sDV, sDV.n)], inc=[(sLD, 16)])
                    of_user[osl] = (sLD, sLD.n)
                y1r_user[0] = (sDV, sDV.n)

        except _Stop:
            pass
        if dump:
            dbg_y1 = nc.dram_tensor("dbg_y1", [NB, 128, NT * 512], BF,
                                    kind="ExternalOutput").ap()
            dbg_tgu = nc.dram_tensor("dbg_tgu", [11, NT, 128, 128], BF,
                                     kind="ExternalOutput").ap()
            dbg_twd = nc.dram_tensor("dbg_twd", [2, 44, 128, 128], BF,
                                     kind="ExternalOutput").ap()
            dbg_pg = nc.dram_tensor("dbg_pg", [NCORE, 128, 1024], BF,
                                    kind="ExternalOutput").ap()
            dbg_uv = nc.dram_tensor("dbg_uv", [128, NCH * 256], BF,
                                    kind="ExternalOutput").ap()
            dbg_ti = nc.dram_tensor("dbg_ti", [128, NCH * 128], BF,
                                    kind="ExternalOutput").ap()
            dbg_kf = nc.dram_tensor("dbg_kf", [128, 2 * TC], BF,
                                    kind="ExternalOutput").ap()
            WALL = [(sLD, sLD.n), (sLG, sLG.n), (sDV, sDV.n), (sAC, sAC.n),
                    (sPE, sPE.n), (sCC, sCC.n)]
            dma("sync", sLD, dbg_y1, y1_d, w=WALL)
            dma("sync", sLD, dbg_tgu, tgu_out[0], w=WALL)
            dma("sync", sLD, dbg_twd, twd_out[0], w=WALL)
            dma("sync", sLD, dbg_pg, pg_out[:, 0], w=WALL)
            dma("sync", sLD, dbg_uv, sp_uv[0], w=WALL)
            dma("sync", sLD, dbg_ti, sp_ti[0], w=WALL)
            dma("sync", sLD, dbg_kf, sp_kfm[0], w=WALL)
            dbg_ws = nc.dram_tensor("dbg_ws", [1, 16], F32, kind="ExternalOutput").ap()
            dbg_tgin = nc.dram_tensor("dbg_tgin", [11, NT, 128, 128], BF,
                                      kind="ExternalOutput").ap()
            dbg_twin = nc.dram_tensor("dbg_twin", [2, 44, 128, 128], BF,
                                      kind="ExternalOutput").ap()
            dma("sync", sLD, dbg_ws, ws_out, w=WALL)
            dma("sync", sLD, dbg_tgin, tgu_in, w=WALL)
            dma("sync", sLD, dbg_twin, twd_in, w=WALL)
            dbg_rw = nc.dram_tensor("dbg_rw", [1, 56], F32, kind="ExternalOutput").ap()
            dbg_rb = nc.dram_tensor("dbg_rb", [1, 4096], BF, kind="ExternalOutput").ap()
            dma("sync", sLD, dbg_rw, rows[0:1, 0:56], w=WALL)
            dbg_rw2 = nc.dram_tensor("dbg_rw2", [1, 32], F32, kind="ExternalOutput").ap()
            dma("sync", sLD, dbg_rw2, rows[0:1, 2552:2584], w=WALL)
            dma("sync", sLD, dbg_rb, rows_b[0:1, 0:4096], w=WALL)
            dbg_xi = nc.dram_tensor("dbg_xi", [128, 4112], BF, kind="ExternalOutput").ap()
            dbg_x2 = nc.dram_tensor("dbg_x2", [128, 2048], BF, kind="ExternalOutput").ap()
            dma("sync", sLD, dbg_xi, bigA[:, 0:4112], w=WALL)
            dma("sync", sLD, dbg_x2, bigC[:, 0:2048], w=WALL)
        kb.wait("sync", sLD, sLG, sDV, sAC, sPE, sGP, sCC)

        @block.tensor
        def _(e):
            for fn in kb.q["tensor"]:
                fn(e)

        @block.vector
        def _(e):
            for fn in kb.q["vector"]:
                fn(e)

        @block.scalar
        def _(e):
            for fn in kb.q["scalar"]:
                fn(e)

        @block.gpsimd
        def _(e):
            for fn in kb.q["gpsimd"]:
                fn(e)

        @block.sync
        def _(e):
            for fn in kb.q["sync"]:
                fn(e)

    return nc


_NC_CACHE = [None]
_LAST_RES = [None]


def kernel(**inputs):
    x = np.asarray(inputs["L0_attn_hidden_states"], np.float32)[0]
    Wq = np.asarray(inputs["L0_attn_Wq"], np.float32)
    Wk = np.asarray(inputs["L0_attn_Wk"], np.float32)
    Wv = np.asarray(inputs["L0_attn_Wv"], np.float32)
    Wb = np.asarray(inputs["L0_attn_Wb"], np.float32)
    cq = np.asarray(inputs["L0_attn_conv_q"], np.float32)
    ck = np.asarray(inputs["L0_attn_conv_k"], np.float32)
    cv = np.asarray(inputs["L0_attn_conv_v"], np.float32)
    nw = np.asarray(inputs["L0_attn_norm_w"], np.float32)
    Wo = np.asarray(inputs["L0_attn_Wo"], np.float32)
    Wgu = np.asarray(inputs["L0_ffn_w_gate_up"], np.float32)
    Wd = np.asarray(inputs["L0_ffn_w_down"], np.float32)
    nww = np.asarray(inputs["L0_ffn_norm_weight"], np.float32)

    wq3 = np.ascontiguousarray(Wq.reshape(D, H, DK).transpose(1, 0, 2))
    wk3 = np.ascontiguousarray(Wk.reshape(D, H, DK).transpose(1, 0, 2))
    wv3 = np.ascontiguousarray(Wv.reshape(D, H, DK).transpose(1, 0, 2))
    wo16 = np.ascontiguousarray(Wo.reshape(D, NT, 128).transpose(1, 0, 2))
    convw = np.ascontiguousarray(np.stack([cq, ck, cv], 0))
    wguT = np.ascontiguousarray(Wgu.T)
    wdT = np.ascontiguousarray(Wd.T)
    nw2 = np.ascontiguousarray(nw.reshape(2, 128).T)
    nww16 = np.ascontiguousarray(nww.reshape(NT, 128).T)

    eye = np.eye(128, dtype=np.float32)
    c_eye = eye.astype(BF16)
    c_eye2 = (2.0 * eye).astype(BF16)
    c_slm = np.tril(np.ones((128, 128), np.float32), -1).astype(BF16)
    c_um = np.triu(np.ones((128, 128), np.float32)).astype(BF16)
    e256 = np.zeros((128, 512), np.float32)
    e256[:, 0:128] = eye
    e256[:, 256 + 128:512] = eye
    c_e256 = e256.astype(BF16)

    c_misc_h = np.zeros((128, 32), np.float32)
    c_misc_h[:, 8] = 1.0 / float(D * 11264)
    c_misc_h[:, 9] = 1.0 / float(FF * D)
    c_misc_h[:, 10:16] = 1.0
    c_misc_h[:, 16] = 1.0 / float(D * 11264)
    c_misc_h[:, 17] = 1.0 / float(FF * D)
    c_misc_h[:, 18:32] = 1.0
    c_misc_h[:, 0] = 256e-6
    c_misc_h[:, 1] = 1e-6
    c_misc_h[:, 2] = 1e-5
    c_misc_h[:, 3] = 256.0
    c_misc_h[:, 4] = 1.0 / 256.0
    c_misc_h[:, 5] = 1.0 / 2048.0
    c_misc_h[:, 6] = 1.0
    in_maps = []
    for c in range(NCORE):
        lo = c * TC
        sel = np.zeros((128, 16), np.float32)
        sel[:, :c] = 1.0
        sel[:, 8:16] = 1.0 - sel[:, 0:8]
        xh = (np.zeros((D, HALO), np.float32) if c == 0
              else np.ascontiguousarray(x[lo - HALO:lo].T))
        in_maps.append({
            "xT": np.ascontiguousarray(x[lo:lo + TC].T),
            "xhalo": xh,
            "wq": wq3, "wk": wk3, "wv": wv3, "wb": Wb,
            "convw": convw, "wo": wo16,
            "wguT_sh": np.ascontiguousarray(wguT[:, GU_SH * c:GU_SH * (c + 1)]),
            "wdT_sh": np.ascontiguousarray(wdT[:, WD_SH * c:WD_SH * (c + 1)]),
            "nw2": nw2, "nww16": nww16,
            "c_eye_d": c_eye, "c_eye2_d": c_eye2, "c_slm_d": c_slm, "c_um_d": c_um,
            "c_e256_d": c_e256, "c_sel_d": sel, "c_misc_d": c_misc_h,
        })
    if _NC_CACHE[0] is None:
        _NC_CACHE[0] = build()
    nc = _NC_CACHE[0]
    res = run_bass_kernel_spmd(nc, in_maps, core_ids=list(range(NCORE)))
    _LAST_RES[0] = res
    out = np.concatenate(
        [np.asarray(res.results[c]["out"], np.float32).T for c in range(NCORE)], 0)
    return out[None]


